# version 16
# baseline (speedup 1.0000x reference)
import math
import numpy as np
import ml_dtypes
import concourse.bass as bass
import concourse.mybir as mybir
from concourse.bass_utils import run_bass_kernel_spmd
from contextlib import ExitStack

F32 = mybir.dt.float32
BF16 = mybir.dt.bfloat16
AF = mybir.ActivationFunctionType
OP = mybir.AluOpType
AX = mybir.AxisListType

NCORES = 8
B = 16
T = 2048
D = 1024
H = 16
DH = 64
DFF = 2816
NSEQ = B // NCORES
NT = T // 128
NKC = D // 128
NDC = DFF // 128
HALF = DFF // 2
NHC = HALF // 128

MASK_NEG = -30000.0

_DEBUG_TAPS = []
_DEBUG = False


class Prog:
    ENGS = ("sync", "gpsimd", "tensor", "vector", "scalar")

    def __init__(self, nc, es):
        self.nc = nc
        self.es = es
        self.sems = {}
        self.cnt = {}
        for name in ("T", "V", "S"):
            self.sem(name)
        self.ops = {e: [] for e in self.ENGS}
        self.observed = {e: {} for e in self.ENGS}

    def sem(self, name):
        if name not in self.sems:
            self.sems[name] = self.es.enter_context(self.nc.semaphore(name))
            self.cnt[name] = 0
        return name

    def emit(self, eng, fn, waits=(), inc=None, inc_n=1):
        wl = []
        for s, v in waits:
            if self.observed[eng].get(s, 0) < v:
                wl.append((s, v))
                self.observed[eng][s] = v
        tok = None
        if inc is not None:
            self.cnt[inc] += inc_n
            tok = self.cnt[inc]
        self.ops[eng].append((wl, fn, inc, inc_n))
        return tok

    def drain(self, eng):
        self.ops[eng].append(([], lambda e: e.drain(), None, 1))

    def tok(self, sem):
        return (sem, self.cnt[sem])

    def barrier(self, tag=""):
        toks = [(s, v) for s, v in self.cnt.items() if v > 0]
        for e in self.ENGS:
            for s, v in toks:
                if self.observed[e].get(s, 0) < v:
                    self.ops[e].append(([(s, v)], None, None, 1))
                    self.observed[e][s] = v

    def build(self, block):
        sems = self.sems

        def mk(eng_name):
            def body(eng):
                for wl, fn, inc, inc_n in self.ops[eng_name]:
                    for s, v in wl:
                        eng.wait_ge(sems[s], v)
                    if fn is None:
                        continue
                    r = fn(eng)
                    if inc is not None:
                        r.then_inc(sems[inc], inc_n)
            return body

        block.sync(mk("sync"))
        block.gpsimd(mk("gpsimd"))
        block.tensor(mk("tensor"))
        block.vector(mk("vector"))
        block.scalar(mk("scalar"))


def _host_consts():
    import jax
    import jax.numpy as jnp

    def _rope_tensor():
        pos = jnp.arange(T, dtype=jnp.float32)
        angles = pos[:, None] / 10000.0 ** (jnp.arange(0, DH, 2, dtype=jnp.float32) / DH)
        return jnp.stack([jnp.cos(angles), jnp.sin(angles)], axis=-1).reshape(T, DH)

    with jax.default_device(jax.devices("cpu")[0]):
        A = np.asarray(jax.jit(_rope_tensor)(), dtype=np.float32)
    Bm = np.float32(1.0) - A
    Bs = Bm.copy()
    Bs[:, 0::2] *= -1.0
    At = np.zeros((128, NT * DH), np.float32)
    Bt = np.zeros((128, NT * DH), np.float32)
    for t in range(NT):
        At[:, t * DH:(t + 1) * DH] = A[t * 128:(t + 1) * 128]
        Bt[:, t * DH:(t + 1) * DH] = Bs[t * 128:(t + 1) * 128]
    ident = np.eye(128).astype(ml_dtypes.bfloat16)
    mask = np.where(np.arange(128)[None, :] >= np.arange(128)[:, None],
                    0.0, MASK_NEG).astype(np.float32)
    return {
        "ropeA": At,
        "ropeB": Bt,
        "ident": ident,
        "identf": np.eye(128).astype(np.float32),
        "mask": mask,
        "ones_row": np.ones((1, 128), ml_dtypes.bfloat16),
        "ones_pair": (lambda z: (z.__setitem__((0, slice(None)), 1.0),
                                 z.__setitem__((64, slice(None)), 1.0), z)[-1])(
            np.zeros((65, 128), ml_dtypes.bfloat16)),
        "ones_row_f32": np.ones((1, 128), np.float32),
        "ones_col": np.ones((128, 1), ml_dtypes.bfloat16),
    }


def _permute_qkv(Wqkv, bqkv):
    idx = np.zeros(3 * D, np.int64)
    for h in range(H):
        base = h * 3 * DH
        for c in range(DH):
            idx[h * DH + c] = base + c
            idx[D + h * DH + c] = base + DH + c
            idx[2 * D + h * DH + c] = base + 2 * DH + c
    return Wqkv[idx], bqkv[idx]


def emit_quant_head(p, tn, src, F, after=(), sg_wait=None, sq_wait=(),
                    vbuf=None):
    nb = F // 16
    g = lambda a: a.rearrange("p (g k) -> p g k", k=16)
    v_ = p.emit
    mb = tn["mblk"][:, 0:nb]
    rb = tn["rblk"][:, 0:nb]
    r6 = tn["r6sq"][:, 0:nb]
    s2 = tn["s2b"][:, 0:nb]
    sg = tn["sgq"][:, 0:F]
    vt = src if vbuf is None else vbuf[:, 0:F]
    aft = list(after)
    if sg_wait is None:
        sg_wait = [("V", p.cnt["V"])]
    tok_am = v_("vector", lambda e: e.tensor_reduce(mb, g(src), AX.X, OP.max,
                                                    apply_absolute_value=True),
                waits=aft, inc="V")
    p.drain("vector")
    v_("scalar", lambda e: e.sign(sg, src), waits=aft + list(sg_wait))
    p.drain("scalar")
    v_("vector", lambda e: e.reciprocal(rb, mb))
    p.drain("vector")
    v_("vector", lambda e: e.tensor_tensor(r6, rb, rb, OP.mult))
    v_("vector", lambda e: e.tensor_scalar(s2, mb, 1.0 / 12.0, 5.0e-13, OP.mult, OP.max))
    p.drain("vector")
    v_("vector", lambda e: e.tensor_scalar(rb, r6, 36.0, 9.0e24, OP.mult, OP.min))
    p.drain("vector")
    tok_sq = v_("scalar", lambda e: e.activation(vt, src, AF.Square),
                waits=[("V", tok_am)] + list(sq_wait), inc="S")
    p.drain("scalar")
    r6b = rb.rearrange("p g -> p g ()").broadcast_to([128, nb, 16])
    v_("vector", lambda e: e.tensor_tensor(g(vt), g(vt), r6b, OP.mult),
       waits=[("S", tok_sq)])
    p.drain("vector")
    return vt, tok_am


def emit_quant_tail(p, tn, vt, F, out_bf, out_wait=()):
    nb = F // 16
    g = lambda a: a.rearrange("p (g k) -> p g k", k=16)
    v_ = p.emit
    s2 = tn["s2b"][:, 0:nb]
    sg = tn["sgq"][:, 0:F]
    aA = tn["accqA"][:, 0:F]
    aB = tn["accqB"][:, 0:F]
    v_("vector", lambda e: e.tensor_scalar(aA, vt, 25.0, 2.0, OP.is_gt, OP.mult))
    p.drain("vector")
    v_("vector", lambda e: e.scalar_tensor_tensor(aB, vt, 12.25, aA, OP.is_gt, OP.add))
    p.drain("vector")
    v_("vector", lambda e: e.scalar_tensor_tensor(aA, vt, 6.25, aB, OP.is_gt, OP.add))
    p.drain("vector")
    v_("vector", lambda e: e.tensor_scalar(aB, aA, 2.0, None, OP.mult))
    p.drain("vector")
    v_("vector", lambda e: e.scalar_tensor_tensor(aA, vt, 3.0625, aB, OP.is_gt, OP.add))
    p.drain("vector")
    v_("vector", lambda e: e.scalar_tensor_tensor(aB, vt, 1.5625, aA, OP.is_gt, OP.add))
    p.drain("vector")
    v_("vector", lambda e: e.scalar_tensor_tensor(aA, vt, 0.5625, aB, OP.is_gt, OP.add))
    p.drain("vector")
    v_("vector", lambda e: e.scalar_tensor_tensor(aB, vt, 0.0625, aA, OP.is_gt, OP.add))
    p.drain("vector")
    s2b = s2.rearrange("p g -> p g ()").broadcast_to([128, nb, 16])
    v_("vector", lambda e: e.tensor_tensor(g(aA), g(aB), s2b, OP.mult))
    p.drain("vector")
    tok = v_("vector", lambda e: e.tensor_tensor(out_bf, aA, sg, OP.mult),
             waits=list(out_wait), inc="V")
    p.drain("vector")
    return tok


def emit_quant(p, tn, src, F, out_bf, after=(), out_wait=(), sg_wait=None):
    vt, _ = emit_quant_head(p, tn, src, F, after=after, sg_wait=sg_wait)
    return emit_quant_tail(p, tn, vt, F, out_bf, out_wait=out_wait)


def build_program(limit=None):
    nc = bass.Bass("TRN2", target_bir_lowering=False)
    es = ExitStack()

    xh = nc.dram_tensor("x", [NSEQ * T, D], F32, kind="ExternalInput").ap()
    wqkv = nc.dram_tensor("wqkv", [3 * D, D], F32, kind="ExternalInput").ap()
    bqkv = nc.dram_tensor("bqkv", [1, 2048], F32, kind="ExternalInput").ap()
    bvcolh = nc.dram_tensor("bvcolh", [128, 8], F32, kind="ExternalInput").ap()
    wproj = nc.dram_tensor("wproj", [D, D], F32, kind="ExternalInput").ap()
    bproj = nc.dram_tensor("bproj", [1, D], F32, kind="ExternalInput").ap()
    wgate = nc.dram_tensor("wgate", [DFF, D], F32, kind="ExternalInput").ap()
    wup = nc.dram_tensor("wup", [DFF, D], F32, kind="ExternalInput").ap()
    wdown = nc.dram_tensor("wdown", [D, DFF], F32, kind="ExternalInput").ap()
    c_ropeA = nc.dram_tensor("ropeA", [128, NT * DH], F32, kind="ExternalInput").ap()
    c_ropeB = nc.dram_tensor("ropeB", [128, NT * DH], F32, kind="ExternalInput").ap()
    c_ident = nc.dram_tensor("ident", [128, 128], BF16, kind="ExternalInput").ap()
    c_mask = nc.dram_tensor("mask", [128, 128], F32, kind="ExternalInput").ap()
    c_ones_row = nc.dram_tensor("ones_row", [1, 128], BF16, kind="ExternalInput").ap()
    c_ones_col = nc.dram_tensor("ones_col", [128, 1], BF16, kind="ExternalInput").ap()
    c_ones_pair = nc.dram_tensor("ones_pair", [65, 128], BF16, kind="ExternalInput").ap()
    out_h = nc.dram_tensor("out", [NSEQ * T, D], F32, kind="ExternalOutput").ap()
    sWg = nc.dram_tensor("sWg", [NKC, 128, DFF], BF16, kind="Internal").ap()
    sWu = nc.dram_tensor("sWu", [NKC, 128, DFF], BF16, kind="Internal").ap()
    sWd = nc.dram_tensor("sWd", [NDC, 128, D], BF16, kind="Internal").ap()
    mscr = nc.dram_tensor("mscr", [1, 2048], BF16, kind="Internal").ap()
    sWqT = nc.dram_tensor("sWqT", [NKC, 128, 3072], BF16, kind="Internal").ap()
    sWpT = nc.dram_tensor("sWpT", [NKC, 128, 1024], BF16, kind="Internal").ap()

    dbg = {}
    if _DEBUG:
        for name, shape, dt in _DEBUG_TAPS:
            dbg[name] = nc.dram_tensor(name, shape, dt, kind="ExternalOutput").ap()

    sb = lambda name, shape, dt: es.enter_context(nc.sbuf_tensor(name, shape, dt))[:]
    ropebuf = sb("ropebuf", [128, 256], F32)
    ident = sb("ident_s", [128, 128], BF16)
    maskt = sb("mask_s", [128, 128], F32)
    onesr = sb("onesr", [1, 128], BF16)
    onesc = sb("onesc", [128, 1], BF16)
    onespair = sb("onespair", [65, 128], BF16)
    brow = sb("brow", [1, 2048], BF16)
    bvcol = sb("bvcol", [128, 8], BF16)
    xres = sb("xres", [128, NT * D], BF16)
    qkt = sb("qkt", [128, 32 * 1024], BF16)
    ub = sb("ub", [128, 16 * 1024], BF16)
    wb = sb("wb", [128, 24576], BF16)
    vtile = sb("vtile", [128, HALF], F32)
    htile = sb("htile", [128, HALF], F32)
    kropet = sb("kropet", [128, D], BF16)
    qropet = sb("qropet", [128, D], BF16)
    accA = sb("accA", [128, D], BF16)
    accB = sb("accB", [128, D], BF16)
    stats = sb("stats", [128, 8], F32)
    mblk = sb("mblk", [128, HALF // 16], F32)
    rblk = sb("rblk", [128, HALF // 16], F32)
    r6sq = sb("r6sq", [128, HALF // 16], F32)
    s2b = sb("s2b", [128, HALF // 16], F32)
    qn2 = sb("qn2", [128, H], F32)
    kn2 = sb("kn2", [128, H], F32)
    nrm = sb("nrm", [128, H], F32)
    negBs = sb("negBs", [128, H], F32)
    lrec = sb("lrec", [128, H], F32)
    nrow = sb("nrow", [1, 16], BF16)
    xln = accB
    xfst = accA
    outt = vtile[:, 0:1024]
    PS = es.enter_context(nc.psum_tensor("PS", [128, 2048], F32))[:]
    PSD = es.enter_context(nc.psum_tensor("PSD", [128, 1088], F32))[:]
    PST = es.enter_context(nc.psum_tensor("PST", [128, 512], BF16))[:]

    p = Prog(nc, es)
    for sname in ("LC", "LGC", "LW0", "LW1", "LG0", "LG1", "LB0", "LB1",
                  "SB0", "SB1", "LQ",
                  "LX0", "LX1", "LX2", "LX3", "MB0", "MB1", "MG0", "MG1",
                  "RP0", "RP1", "LP", "LF", "ST"):
        p.sem(sname)


    WQKVT = wb[:, 0:24576]
    OFM = wb[:, 0:16384]
    PT0 = wb[:, 16384:17408]
    PT1 = wb[:, 17408:18432]
    opair = wb[:, 18432:20480]
    WDH = wb[:, 0:NHC * 1024]
    accqA = wb[:, 11264:12672]
    accqB = wb[:, 12672:14080]
    sgq = wb[:, 14080:15488]
    hstage = wb[:, 15488:16896]
    WGH = qkt[:, 0:NKC * HALF]
    WUH = qkt[:, NKC * HALF:2 * NKC * HALF]
    hqv = qkt[:, 22528:23936]
    WPROJT = qkt[:, 0:8192]
    bprow2 = qkt[0:1, 8192:9216]
    pbias = qkt[0:1, 9216:10240]
    cbias = qkt[0:1, 10240:11264]

    tn = dict(mblk=mblk, rblk=rblk, r6sq=r6sq, s2b=s2b,
              sgq=sgq, accqA=accqA, accqB=accqB)
    tn_f0 = dict(mblk=mblk, rblk=rblk, r6sq=r6sq, s2b=s2b,
                 accqA=qkt[:, 25344:26368], accqB=qkt[:, 26368:27392],
                 sgq=qkt[:, 27392:28416])

    def qk_slice(h, which, c0, c1):
        r0 = (h % 2) * 64
        base = (h // 2) * 4096 + which * 2048
        return qkt[r0:r0 + 64, base + c0:base + c1]

    p.emit("sync", lambda e: e.dma_start(out=ident, in_=c_ident), inc="LC", inc_n=16)
    p.emit("sync", lambda e: e.dma_start(out=maskt, in_=c_mask), inc="LC", inc_n=16)
    p.emit("sync", lambda e: e.dma_start(out=onesr, in_=c_ones_row), inc="LC", inc_n=16)
    p.emit("sync", lambda e: e.dma_start(out=onesc, in_=c_ones_col), inc="LC", inc_n=16)
    p.emit("sync", lambda e: e.dma_start(out=onespair, in_=c_ones_pair), inc="LC", inc_n=16)
    p.emit("gpsimd", lambda e: e.dma_start(out=brow, in_=bqkv), inc="LGC", inc_n=16)
    p.emit("gpsimd", lambda e: e.dma_start(out=bvcol, in_=bvcolh), inc="LGC", inc_n=16)
    consts_tok = [p.tok("LC"), p.tok("LGC")]

    def emit_transposes(src_bf, n, dst_fn, waits, pair_mode=False):
        ntr = n
        j = 0
        while j < ntr:
            jn = min(4, ntr - j)
            for i in range(jn):
                if pair_mode:
                    hh = j + i
                    r0 = (hh % 2) * 64
                    slot = i // 2
                    fn = (lambda e, hh=hh, slot=slot, r0=r0: e.transpose(
                        PST[r0:r0 + 64, slot * 128:(slot + 1) * 128],
                        src_bf[:, hh * 64:(hh + 1) * 64], ident,
                        tile_position=(0, r0)))
                else:
                    fn = (lambda e, jj=j + i, i=i: e.transpose(
                        PST[:, i * 128:(i + 1) * 128],
                        src_bf[:, jj * 128:(jj + 1) * 128], ident))
                p.emit("tensor", fn, waits=waits if (j == 0 and i == 0) else [],
                       inc="T" if i == jn - 1 else None)
            tok_t = p.cnt["T"]
            nu = (jn // 2 if pair_mode else jn)
            tok_e = p.emit("scalar", lambda e, j=j, nu=nu: e.copy(
                dst_fn(j, nu),
                PST[:, 0:nu * 128].rearrange("p (k c) -> p k c", c=128)),
                waits=[("T", tok_t)], inc="S")
            p.drain("scalar")
            p.emit("tensor", None, waits=[("S", tok_e)])
            j += jn
        return p.cnt["S"]


    def emit_xload(s):
        toks = {}
        for g4 in range(4):
            sem = "LX%d" % g4
            for t in range(g4 * 4, g4 * 4 + 4):
                p.emit("gpsimd", lambda e, t=t, s=s: e.dma_start(
                    out=xres[:, t * D:(t + 1) * D],
                    in_=xh[s * T + t * 128: s * T + (t + 1) * 128, :]),
                    inc=sem, inc_n=16)
            toks[g4] = (sem, p.cnt[sem])
        return toks


    def emit_wprep():
        jobs = []
        for (s_ap, nrb, K, dst, base) in ((wgate, NDC, D, sWg, 0),
                                          (wup, NDC, D, sWu, 0),
                                          (wdown, NKC, DFF, sWd, 0)):
            nh = K // HALF if K > HALF else 1
            for r in range(nrb):
                for half in range(nh):
                    jobs.append((s_ap, r, half, min(HALF, K), dst, base, True))
        bjobs = []
        for r in range(3 * D // 128):
            bjobs.append((wqkv, r, 0, D, sWqT, 0, False))
        for r in range(NKC):
            bjobs.append((wproj, r, 0, D, sWpT, 0, False))
        out = []
        bi = 0
        for i, j in enumerate(jobs):
            out.append(j)
            if i % 2 == 1 and bi < len(bjobs):
                out.append(bjobs[bi]); bi += 1
        out.extend(bjobs[bi:])
        jobs = out
        srcs = [htile, vtile]
        hqs = [qkt[:, 22528:23936], qkt[:, 23936:25344]]
        hss = [wb[:, 15488:16896], wb[:, 16896:18304]]
        tns = [dict(tn), dict(tn)]
        tns[1]["sgq"] = wb[:, 18304:19712]
        lws = ["LW0", "LW1"]
        lgs = ["LG0", "LG1"]
        rel_src = {0: ("V", 0), 1: ("V", 0)}
        rel_hq = {0: ("T", 0), 1: ("T", 0)}
        rel_hs = {0: ("LG0", 0), 1: ("LG1", 0)}
        lbs = ["LB0", "LB1"]
        sbs = ["SB0", "SB1"]
        hqBs = [qkt[:, 25344:26368], qkt[:, 26368:27392]]
        hsBs = [wb[:, 19712:20736], wb[:, 20736:21760]]
        rel_hqB = {0: ("T", 0), 1: ("T", 0)}
        rel_hsB = {0: ("SB0", 0), 1: ("SB1", 0)}
        bcnt = [0]
        for i, (s_ap, r, half, Fh, dst, base, do_q) in enumerate(jobs):
            par = i % 2
            bpar = bcnt[0] % 2
            srcb = srcs[par][:, 0:Fh]
            hq = (hqs[par] if do_q else hqBs[bpar])[:, 0:Fh]
            hs = hss[par] if do_q else hsBs[bpar]
            if do_q:
                tok_ld = p.emit("sync", lambda e, s_ap=s_ap, r=r, half=half,
                                Fh=Fh, srcb=srcb: e.dma_start(
                    out=srcb, in_=s_ap[r * 128:(r + 1) * 128,
                                       half * HALF:half * HALF + Fh]),
                    waits=[rel_src[par]], inc=lws[par], inc_n=16)
                tok_q = emit_quant(p, tns[par], srcb, Fh, hq,
                                   after=[(lws[par], tok_ld)],
                                   out_wait=[rel_hq[par]], sg_wait=[rel_src[par]])
                rel_src[par] = ("V", tok_q)
                trw = [("V", tok_q), rel_hs[par]]
            else:
                tok_ld = p.emit("gpsimd", lambda e, s_ap=s_ap, r=r, Fh=Fh,
                                hq=hq: e.dma_start(
                    out=hq, in_=s_ap[r * 128:(r + 1) * 128, 0:Fh]),
                    waits=[rel_hqB[bpar]], inc=lbs[bpar], inc_n=16)
                trw = [(lbs[bpar], tok_ld), rel_hsB[bpar]]
            nck = Fh // 128
            emit_transposes(hq, nck,
                            lambda j, u, hs=hs: hs[:, j * 128:(j + u) * 128
                                                   ].rearrange("p (k c) -> p k c", c=128),
                            waits=trw)
            if do_q:
                rel_hq[par] = ("T", p.cnt["T"])
            else:
                rel_hqB[bpar] = ("T", p.cnt["T"])
            tok_ev = p.cnt["S"]
            for j in range(nck):
                cglob = half * (HALF // 128) + j
                p.emit("gpsimd", lambda e, j=j, cglob=cglob, r=r, dst=dst,
                       base=base, hs=hs: e.dma_start(
                    out=dst[base + cglob, 0:128, r * 128:(r + 1) * 128],
                    in_=hs[:, j * 128:(j + 1) * 128]),
                    waits=[("S", tok_ev)] if j == 0 else [],
                    inc=(lgs[par] if do_q else sbs[bpar]), inc_n=16)
            if do_q:
                rel_hs[par] = (lgs[par], p.cnt[lgs[par]])
            else:
                rel_hsB[bpar] = (sbs[bpar], p.cnt[sbs[bpar]])
                bcnt[0] += 1

    for s, v in consts_tok:
        p.emit("vector", None, waits=[(s, v)])
        p.emit("scalar", None, waits=[(s, v)])
        p.emit("tensor", None, waits=[(s, v)])
    xload0 = emit_xload(0)
    emit_wprep()
    p.barrier("after-wprep")

    def emit_ln(xt_ap, out_bf_ap, after=(), stats_wait=None, dump=None):
        if dump is None:
            dump = vtile[:, 0:D]
        if stats_wait is None:
            stats_wait = [("V", p.cnt["V"])]
        gate = list(after) + list(stats_wait)
        p.emit("scalar", lambda e: e.activation(dump, xt_ap, AF.Copy,
                                                accum_out=stats[:, 0:1]),
               waits=gate)
        p.drain("scalar")
        tok_s = p.emit("scalar", lambda e: e.activation(dump, xt_ap, AF.Square,
                                                        accum_out=stats[:, 1:2]),
                       inc="S")
        p.drain("scalar")
        p.emit("vector", lambda e: e.tensor_scalar(stats[:, 2:3], stats[:, 0:1],
                                                   -1.0 / D, None, OP.mult),
               waits=[("S", tok_s)])
        p.drain("vector")
        p.emit("vector", lambda e: e.tensor_tensor(stats[:, 3:4], stats[:, 2:3],
                                                   stats[:, 2:3], OP.mult))
        p.drain("vector")
        p.emit("vector", lambda e: e.scalar_tensor_tensor(stats[:, 4:5], stats[:, 1:2],
                                                          1.0 / D, stats[:, 3:4],
                                                          OP.mult, OP.subtract))
        p.drain("vector")
        tok_v = p.emit("vector", lambda e: e.tensor_scalar(stats[:, 5:6], stats[:, 4:5],
                                                           1e-5, None, OP.add), inc="V")
        p.drain("vector")
        p.emit("scalar", lambda e: e.activation(stats[:, 6:7], stats[:, 5:6], AF.Ln),
               waits=[("V", tok_v)])
        p.drain("scalar")
        tok_r = p.emit("scalar", lambda e: e.activation(stats[:, 7:8], stats[:, 6:7],
                                                        AF.Exp, scale=-0.5), inc="S")
        p.drain("scalar")
        tok = p.emit("vector", lambda e: e.tensor_scalar(out_bf_ap, xt_ap,
                                                         stats[:, 2:3], stats[:, 7:8],
                                                         OP.add, OP.mult),
                     waits=[("S", tok_r), ("S", p.cnt["S"])], inc="V")
        p.drain("vector")
        return tok

    def seq_phase(s, xload):
        for k in range(NKC):
            p.emit("sync", lambda e, k=k: e.dma_start(
                out=WQKVT[:, k * 3072:(k + 1) * 3072], in_=sWqT[k]),
                waits=([("T", p.cnt["T"]), ("S", p.cnt["S"]),
                        ("V", p.cnt["V"])] if k == 0 else []),
                inc="LQ", inc_n=16)
        tok_lq = ("LQ", p.cnt["LQ"])

        prev_ln = [None]
        pend_qk = [None]
        rp_tok = {}
        ln_tok = {}
        tr_tok = {}

        def emit_lntile(t):
            sl = (t % 2) * 128
            rps = "RP0" if t % 2 == 0 else "RP1"
            p.emit("sync", lambda e, t=t, sl=sl: e.dma_start(
                out=ropebuf[:, sl:sl + 64], in_=c_ropeA[:, t * DH:(t + 1) * DH]),
                waits=[("V", p.cnt["V"])], inc=rps, inc_n=16)
            rp_tok[t] = (rps, p.emit("sync", lambda e, t=t, sl=sl: e.dma_start(
                out=ropebuf[:, sl + 64:sl + 128], in_=c_ropeB[:, t * DH:(t + 1) * DH]),
                inc=rps, inc_n=16))
            xt = xres[:, t * D:(t + 1) * D]
            tok_ln = emit_ln(xt, xln, after=[xload[t // 4]],
                             stats_wait=prev_ln[0])
            prev_ln[0] = [("V", tok_ln)]
            ln_tok[t] = tok_ln
            tr_tok[t] = emit_transposes(
                xln, NKC, lambda j, jn: xfst[:, j * 128:(j + jn) * 128
                                             ].rearrange("p (k c) -> p k c", c=128),
                waits=[("V", tok_ln)])

        emit_lntile(0)
        for t in range(NT):
            sl = (t % 2) * 128
            def psn(nck):
                if nck < 4:
                    return PS[:, nck * 512:(nck + 1) * 512]
                return PSD[:, (nck - 4) * 512:(nck - 3) * 512]
            first_w = [("S", tr_tok[t]), ("V", p.cnt["V"]), ("S", p.cnt["S"]),
                       tok_lq]
            for k in range(NKC):
                for nck in range(6):
                    p.emit("tensor", lambda e, k=k, nck=nck: e.matmul(
                        psn(nck), xfst[:, k * 128:(k + 1) * 128],
                        WQKVT[:, k * 3072 + nck * 512: k * 3072 + (nck + 1) * 512],
                        start=(k == 0), stop=(k == NKC - 1 and nck >= 4)),
                        waits=first_w if (k == 0 and nck == 0) else [])
            if pend_qk[0] is not None:
                pend_qk[0]()
                pend_qk[0] = None
            for nck in range(4):
                for i4 in range(4):
                    r16 = nck * 4 + i4
                    p.emit("tensor", lambda e, nck=nck, i4=i4, r16=r16: e.matmul(
                        psn(nck)[:, i4 * 128:(i4 + 1) * 128], onesr,
                        brow[0:1, r16 * 128:(r16 + 1) * 128],
                        start=False, stop=(i4 == 3)),
                        inc="T" if (nck == 3 and i4 == 3) else None)
            tok_qkv = p.cnt["T"]
            if t + 1 < NT:
                emit_lntile(t + 1)
            rA = ropebuf[:, sl:sl + 64].rearrange(
                "p (r two) -> p () r two", two=2).broadcast_to([128, H, DH // 2, 2])
            rB = ropebuf[:, sl + 64:sl + 128].rearrange(
                "p (r two) -> p () r two", two=2).broadcast_to([128, H, DH // 2, 2])
            g4 = lambda a: a.rearrange("p (h r two) -> p h r two", h=H, two=2)
            psq = g4(PS[:, 0:1024])
            psk = g4(PS[:, 1024:2048])
            q4 = g4(qropet[:, 0:1024])
            k4 = g4(kropet[:, 0:1024])
            pd4 = g4(PSD[:, 0:1024])
            p.emit("vector", lambda e, k4=k4, psq=psq, rB=rB: e.tensor_tensor(
                k4, psq[:, :, :, ::-1], rB, OP.mult),
                waits=[("T", tok_qkv), rp_tok[t]])
            p.drain("vector")
            p.emit("vector", lambda e, q4=q4, psq=psq, rA=rA:
                   e.tensor_tensor(q4, psq, rA, OP.mult))
            p.drain("vector")
            p.emit("vector", lambda e, q4=q4, k4=k4:
                   e.tensor_tensor(q4, q4, k4, OP.add))
            p.drain("vector")
            tok_vev = p.emit("scalar", lambda e, t=t: e.copy(
                ub[:, t * D:(t + 1) * D], PSD[:, 0:1024]),
                waits=[("T", tok_qkv)], inc="S")
            p.drain("scalar")
            p.emit("vector", lambda e, pd4=pd4, psk=psk, rB=rB: e.tensor_tensor(
                pd4, psk[:, :, :, ::-1], rB, OP.mult),
                waits=[("S", tok_vev)])
            p.drain("vector")
            p.emit("vector", lambda e, k4=k4, psk=psk, rA=rA:
                   e.tensor_tensor(k4, psk, rA, OP.mult))
            p.drain("vector")
            tok_rope = p.emit("vector", lambda e, k4=k4, pd4=pd4:
                              e.tensor_tensor(k4, k4, pd4, OP.add), inc="V")
            p.drain("vector")
            def qk_tr(t=t, tok_rope=tok_rope):
                for which, src_bf in ((0, qropet), (1, kropet)):
                    def dstqk(j, u, which=which, t=t):
                        v = qkt.rearrange("p (pr c) -> p pr c", c=4096)
                        return v[:, j:j + u,
                                 which * 2048 + t * 128: which * 2048 + (t + 1) * 128]
                    emit_transposes(src_bf[:, 0:1024], NKC,
                                    lambda j, u, which=which, t=t: dstqk(j, u, which, t),
                                    waits=[("V", tok_rope)])
            pend_qk[0] = qk_tr
        if pend_qk[0] is not None:
            pend_qk[0]()
            pend_qk[0] = None
        p.barrier("p3-end")
        return p.cnt["S"]

    def attn_phase(s):
        mrep = wb[:, 20480:22528]
        mrow = wb[0:1, 22528:23552]
        lnrow = htile
        for h in range(H):
            r0 = (h % 2) * 64
            tp = (64, 0) if (h % 2) else None
            qbase = (h // 2) * 4096
            kbase = qbase + 2048
            tok_exp = {}
            tok_mg = {}
            pend = [None]
            for pss in range(2):
                for w in range(2):
                    for sc in range(8 * (w + 1)):
                        ping = PS[:, 0:1024] if sc % 2 == 0 else PS[:, 1024:2048]
                        ptb = PT0 if sc % 2 == 0 else PT1
                        qlo_g = max(w * 1024, sc * 128)
                        lloc = qlo_g - w * 1024
                        a = lloc
                        first = True
                        while a < 1024:
                            b2 = min(1024, (a // 512 + 1) * 512)
                            wl = [("S", tok_exp.get(sc % 2, 0)),
                                  ("V", p.cnt["V"])] if first else []
                            p.emit("tensor", lambda e, a=a, b2=b2, sc=sc, w=w,
                                   ping=ping, r0=r0, kbase=kbase, qbase=qbase,
                                   tp=tp: e.matmul(
                                ping[:, a:b2],
                                qkt[r0:r0 + 64, kbase + sc * 128: kbase + (sc + 1) * 128],
                                qkt[r0:r0 + 64, qbase + w * 1024 + a: qbase + w * 1024 + b2],
                                start=True, stop=True, tile_position=tp),
                                waits=wl,
                                inc="T" if b2 == 1024 else None)
                            first = False
                            a = b2
                        tok_sc = p.cnt["T"]
                        if sc >= 8 * w:
                            p.emit("vector", lambda e, ping=ping, lloc=lloc:
                                   e.tensor_tensor(
                                ping[:, lloc:lloc + 128], ping[:, lloc:lloc + 128],
                                maskt, OP.add), waits=[("T", tok_sc)], inc="V")
                            if pss == 1:
                                p.drain("vector")
                        if pss == 1:
                            p.emit("vector", lambda e, ping=ping, lloc=lloc, w=w:
                                   e.tensor_tensor(
                                ping[:, lloc:1024], ping[:, lloc:1024],
                                mrep[:, w * 1024 + lloc:(w + 1) * 1024], OP.add),
                                waits=[("T", tok_sc), tok_mg[w]], inc="V")
                        tok_m = p.cnt["V"]
                        scl = (1.0 / 64.0) if pss == 0 else 0.125
                        te = p.emit("scalar", lambda e, ping=ping, ptb=ptb,
                                    lloc=lloc, scl=scl: e.activation(
                                    ptb[:, lloc:1024], ping[:, lloc:1024],
                                    AF.Exp, scale=scl),
                                    waits=[("T", tok_sc), ("V", tok_m)], inc="S")
                        p.drain("scalar")
                        tok_exp[sc % 2] = te

                        def consume(sc=sc, w=w, ptb=ptb, te=te, lloc=lloc, pss=pss, h=h):
                            if pss == 0:
                                nsj = (1024 - lloc + 511) // 512
                                aa = lloc
                                for sj in range(nsj):
                                    b3 = min(1024, (aa // 512 + 1) * 512)
                                    is_b4 = aa < 512
                                    stp = (is_b4 and sc == 8 * w + 3) or \
                                          ((not is_b4) and sc == 8 * (w + 1) - 1)
                                    p.emit("tensor", lambda e, aa=aa, b3=b3, ptb=ptb,
                                           sc=sc, stp=stp: e.matmul(
                                        PSD[0:1, aa:b3], onesc,
                                        ptb[:, aa:b3],
                                        start=(sc == 0),
                                        stop=stp),
                                        waits=[("S", te)] if sj == 0 else [],
                                        inc="T" if sj == nsj - 1 else None)
                                    aa = b3
                            else:
                                for qtl in range(max(0, sc - 8 * w), 8):
                                    qtg = 8 * w + qtl
                                    st_ap = ptb[:, qtl * 128:(qtl + 1) * 128]
                                    p.emit("tensor", lambda e, st_ap=st_ap, qtg=qtg,
                                           sc=sc, w=w, qtl=qtl, h=h: e.matmul(
                                        PSD[:, qtg * 64:(qtg + 1) * 64], st_ap,
                                        ub[:, sc * D + h * 64: sc * D + (h + 1) * 64],
                                        start=(sc == 0 and qtg % 8 == 0),
                                        stop=(sc == 8 * (w + 1) - 1 and qtl == 7)),
                                        waits=[("S", te)] if qtl == max(0, sc - 8 * w) else [])
                                    p.emit("tensor", lambda e, st_ap=st_ap, qtg=qtg,
                                           sc=sc, w=w, qtl=qtl: e.matmul(
                                        PSD[:, 1024 + qtg:1025 + qtg], st_ap, onesc,
                                        start=(w == 0 and sc == 0 and qtl == 0),
                                        stop=(w == 1 and sc == 15 and qtl == 7)),
                                        inc="T" if qtl == 7 else None)
                        if pend[0] is not None:
                            pend[0]()
                        pend[0] = consume
                    if pend[0] is not None:
                        pend[0]()
                        pend[0] = None
                    if pss == 0:
                        tok_L = p.cnt["T"]
                        lr = lnrow[0:1, 0:1024]
                        tok_ln2 = p.emit("scalar", lambda e, lr=lr: e.activation(
                            lr, PSD[0:1, 0:1024], AF.Ln),
                            waits=[("T", tok_L), ("V", p.cnt["V"])], inc="S")
                        p.drain("scalar")
                        tok_sr = p.emit("vector", lambda e, lr=lr: e.tensor_scalar(
                            mrow, lr, -64.0, None, OP.mult),
                            waits=[("S", tok_ln2)], inc="V")
                        p.drain("vector")
                        mb_s, mg_s = "MB%d" % w, "MG%d" % w
                        p.emit("sync", lambda e, w=w: e.dma_start(
                            out=mscr[0:1, w * 1024:(w + 1) * 1024], in_=mrow),
                            waits=[("V", tok_sr), (mg_s, p.cnt[mg_s])],
                            inc=mb_s, inc_n=16)
                        p.emit("gpsimd", lambda e, w=w: e.dma_start(
                            out=mrep[:, w * 1024:(w + 1) * 1024],
                            in_=mscr[0:1, w * 1024:(w + 1) * 1024
                                     ].broadcast_to([128, 1024])),
                            waits=[(mb_s, p.cnt[mb_s]), ("V", p.cnt["V"])],
                            inc=mg_s, inc_n=16)
                        tok_mg[w] = (mg_s, p.cnt[mg_s])
            tok_av = p.cnt["T"]
            p.emit("vector", lambda e: e.reciprocal(lrec[:, 0:16], PSD[:, 1024:1040]),
                   waits=[("T", tok_av)])
            p.drain("vector")
            lb = lrec[:, 0:16].rearrange("p q -> p q ()").broadcast_to([128, 16, 64])
            o3 = PSD[:, 0:1024].rearrange("p (q f) -> p q f", f=64)
            d3 = opair.rearrange("p (q f) -> p q f", f=128)[:, :, r0:r0 + 64]
            tok_oev = p.emit("vector", lambda e, o3=o3, d3=d3, lb=lb: e.tensor_tensor(
                d3, o3, lb, OP.mult), inc="V")
            p.drain("vector")
            if h % 2 == 1:
                c = h // 2
                emit_transposes(
                    opair, 16,
                    lambda j, u, c=c: OFM[:, c * 2048 + j * 128: c * 2048 + (j + u) * 128
                                          ].rearrange("p (k c2) -> p k c2", c2=128),
                    waits=[("V", tok_oev)])
                p.emit("vector", None, waits=[("T", p.cnt["T"])])
            p.emit("tensor", None, waits=[("V", tok_oev)])

    def proj_phase(s):
        p.emit("gpsimd", lambda e: e.dma_start(out=bprow2, in_=bproj),
               waits=[("T", p.cnt["T"]), ("S", p.cnt["S"])], inc="LP", inc_n=16)
        for k in range(NKC):
            p.emit("gpsimd", lambda e, k=k: e.dma_start(
                out=WPROJT[:, k * 1024:(k + 1) * 1024], in_=sWpT[k]),
                inc="LP", inc_n=16)
        p.emit("tensor", None, waits=[("LP", p.cnt["LP"]), ("S", p.cnt["S"]),
                                      ("LGC", 32)])
        for c in range(NKC):
            for hs in range(2):
                p.emit("tensor", lambda e, c=c, hs=hs: e.matmul(
                    PSD[0:1, hs * 512:(hs + 1) * 512], bvcol[:, c:c + 1],
                    WPROJT[:, c * 1024 + hs * 512: c * 1024 + (hs + 1) * 512],
                    start=(c == 0), stop=(c == NKC - 1)),
                    inc="T" if (c == NKC - 1 and hs == 1) else None)
        tok_pb = p.cnt["T"]
        tok_pbe = p.emit("scalar", lambda e: e.copy(pbias, PSD[0:1, 0:1024]),
                         waits=[("T", tok_pb)], inc="S")
        p.drain("scalar")
        tok_cb = p.emit("vector", lambda e: e.tensor_tensor(
            cbias, pbias, bprow2, OP.add), waits=[("S", tok_pbe)], inc="V")
        p.drain("vector")
        tok_res = {0: ("V", p.cnt["V"]), 1: ("V", p.cnt["V"])}
        for t in range(NT):
            pb = PS[:, 0:1024] if t % 2 == 0 else PS[:, 1024:2048]
            first_w = [tok_res[t % 2], ("V", tok_cb)]
            for c in range(NKC):
                for hs in range(2):
                    p.emit("tensor", lambda e, c=c, hs=hs, t=t, pb=pb: e.matmul(
                        pb[:, hs * 512:(hs + 1) * 512],
                        OFM[:, c * 2048 + t * 128: c * 2048 + (t + 1) * 128],
                        WPROJT[:, c * 1024 + hs * 512: c * 1024 + (hs + 1) * 512],
                        start=(c == 0), stop=False),
                        waits=first_w if (c == 0 and hs == 0) else [])
            for hs in range(2):
                p.emit("tensor", lambda e, hs=hs, pb=pb: e.matmul(
                    pb[:, hs * 512:(hs + 1) * 512], onesr,
                    cbias[0:1, hs * 512:(hs + 1) * 512], start=False, stop=True),
                    inc="T" if hs == 1 else None)
            tok_p = p.cnt["T"]
            xt = xres[:, t * D:(t + 1) * D]
            tok_r = p.emit("vector", lambda e, xt=xt, pb=pb: e.tensor_tensor(
                xt, pb[:, 0:1024], xt, OP.add), waits=[("T", tok_p)], inc="V")
            p.drain("vector")
            tok_res[t % 2] = ("V", tok_r)
            xl2 = htile[:, 0:D]
            tok_ln = emit_ln(xt, xl2)
            p.emit("vector", None, waits=[("T", p.cnt["T"])])
            tok_q = emit_quant(p, tn_f0, xl2, D, xln, after=[("V", tok_ln)])
            emit_transposes(
                xln, NKC,
                lambda j, u, t=t: ub.rearrange("p (k c) -> p k c", c=2048)
                [:, j:j + u, t * 128:(t + 1) * 128],
                waits=[("V", tok_q)])

    def ffn_phase(s):
        regions = [(PS[:, 0:512], PS[:, 512:1024]),
                   (PS[:, 1024:1536], PS[:, 1536:2048])]
        reg_free = {0: ("V", 0), 1: ("V", 0)}
        hqv2 = [qkt[:, 22528:23936], qkt[:, 23936:25344]]
        pend_td = [None]
        last_res = [("V", 0)]
        prev_q = [("V", 0)]
        gs = 0
        for hf in range(2):
            if hf == 1 and pend_td[0] is not None:
                pend_td[0]()
                pend_td[0] = None
            wl0 = [("T", p.cnt["T"]), ("S", p.cnt["S"])] if hf == 1 else []
            for k in range(NKC):
                p.emit("sync", lambda e, k=k, hf=hf: e.dma_start(
                    out=WGH[:, k * HALF:(k + 1) * HALF],
                    in_=sWg[k, 0:128, hf * HALF:(hf + 1) * HALF]),
                    waits=wl0 if k == 0 else [], inc="LF", inc_n=16)
                p.emit("sync", lambda e, k=k, hf=hf: e.dma_start(
                    out=WUH[:, k * HALF:(k + 1) * HALF],
                    in_=sWu[k, 0:128, hf * HALF:(hf + 1) * HALF]), inc="LF", inc_n=16)
            for j in range(NHC):
                p.emit("sync", lambda e, j=j, hf=hf: e.dma_start(
                    out=WDH[:, j * 1024:(j + 1) * 1024],
                    in_=sWd[hf * NHC + j, 0:128, 0:1024]), inc="LF", inc_n=16)
            tok_w = p.tok("LF")
            for t in range(NT):
                subs = [(0, 512), (512, 1024), (1024, HALF)]
                for a, bq in subs:
                    sw = bq - a
                    greg, ureg = regions[gs % 2]
                    first_w = [tok_w, reg_free[gs % 2]]
                    for which in range(2):
                        WH = WGH if which == 0 else WUH
                        dst = greg if which == 0 else ureg
                        for k in range(NKC):
                            p.emit("tensor", lambda e, k=k, a=a, bq=bq, WH=WH,
                                   dst=dst, t=t, sw=sw: e.matmul(
                                dst[:, 0:sw],
                                ub[:, k * 2048 + t * 128: k * 2048 + (t + 1) * 128],
                                WH[:, k * HALF + a: k * HALF + bq],
                                start=(k == 0), stop=(k == NKC - 1)),
                                waits=first_w if (which == 0 and k == 0) else [],
                                inc="T" if (which == 1 and k == NKC - 1) else None)
                    tok_gu = p.cnt["T"]
                    ts1 = p.emit("scalar", lambda e, a=a, sw=sw, greg=greg:
                                 e.activation(htile[:, a:a + sw], greg[:, 0:sw],
                                              AF.Silu),
                                 waits=[("T", tok_gu), prev_q[0]], inc="S")
                    p.drain("scalar")
                    tok_m = p.emit("vector", lambda e, a=a, sw=sw, ureg=ureg:
                                   e.tensor_tensor(
                        htile[:, a:a + sw], htile[:, a:a + sw], ureg[:, 0:sw],
                        OP.mult), waits=[("S", ts1)], inc="V")
                    p.drain("vector")
                    reg_free[gs % 2] = ("V", tok_m)
                    gs += 1
                if pend_td[0] is not None:
                    pend_td[0]()
                hq = hqv2[t % 2]
                tok_q = emit_quant(p, tn, htile[:, 0:HALF], HALF, hq[:, 0:HALF],
                                   out_wait=[("T", p.cnt["T"])])
                prev_q[0] = ("V", tok_q)

                def td(t=t, hf=hf, tok_q=tok_q, hq=hq):
                    tok_tr = emit_transposes(
                        hq[:, 0:HALF], NHC,
                        lambda j, u: hstage[:, j * 128:(j + u) * 128
                                            ].rearrange("p (k c) -> p k c", c=128),
                        waits=[("V", tok_q)])
                    for j in range(NHC):
                        for ds in range(2):
                            p.emit("tensor", lambda e, j=j, ds=ds: e.matmul(
                                PSD[:, ds * 512:(ds + 1) * 512],
                                hstage[:, j * 128:(j + 1) * 128],
                                WDH[:, j * 1024 + ds * 512: j * 1024 + (ds + 1) * 512],
                                start=(j == 0), stop=(j == NHC - 1)),
                                waits=([("S", tok_tr), last_res[0]]
                                       if (j == 0 and ds == 0) else []),
                                inc="T" if (j == NHC - 1 and ds == 1) else None)
                    tok_d = p.cnt["T"]
                    xt = xres[:, t * D:(t + 1) * D]
                    if hf == 0:
                        tok_o = p.emit("vector", lambda e, xt=xt: e.tensor_tensor(
                            xt, PSD[:, 0:1024], xt, OP.add),
                            waits=[("T", tok_d)], inc="V")
                        p.drain("vector")
                    else:
                        tok_o = p.emit("vector", lambda e, xt=xt: e.tensor_tensor(
                            outt, PSD[:, 0:1024], xt, OP.add),
                            waits=[("T", tok_d), ("ST", p.cnt["ST"])], inc="V")
                        p.drain("vector")
                        p.emit("sync", lambda e, t=t: e.dma_start(
                            out=out_h[s * T + t * 128: s * T + (t + 1) * 128, :],
                            in_=outt), waits=[("V", tok_o)], inc="ST", inc_n=16)
                    last_res[0] = ("V", tok_o)
                pend_td[0] = td
        if pend_td[0] is not None:
            pend_td[0]()
        p.barrier("ffn-end")

    for s in range(NSEQ if limit is None else 1):
        xl = xload0 if s == 0 else emit_xload(s)
        seq_phase(s, xl)
        if limit == "qkv":
            p.barrier("limit")
            break
        attn_phase(s)
        p.barrier("attn-end")
        if limit == "attn":
            break
        proj_phase(s)
        p.barrier("proj-end")
        if limit == "proj":
            break
        ffn_phase(s)
        p.barrier("seq-end")

    p.ops["sync"].append(([("ST", p.cnt["ST"])], None, None, 1))
    p.ops["gpsimd"].append(([("ST", p.cnt["ST"])], None, None, 1))

    with nc.Block() as block:
        p.build(block)
    es.close()
    return nc


_NC_CACHE = [None]


def kernel(L0_attn_x, L0_attn_Wqkv, L0_attn_bqkv, L0_attn_Wproj, L0_attn_bproj,
           L0_ffn_w_gate, L0_ffn_w_up, L0_ffn_w_down):
    x = np.ascontiguousarray(np.asarray(L0_attn_x, dtype=np.float32))
    Wqkv = np.ascontiguousarray(np.asarray(L0_attn_Wqkv, dtype=np.float32))
    bqkv = np.asarray(L0_attn_bqkv, dtype=np.float32)
    Wqkv_p, bqkv_p = _permute_qkv(Wqkv, bqkv)
    consts = _host_consts()
    if _NC_CACHE[0] is None:
        _NC_CACHE[0] = build_program()
    nc = _NC_CACHE[0]
    in_maps = []
    for c in range(NCORES):
        in_maps.append({
            "x": np.ascontiguousarray(x[c * NSEQ:(c + 1) * NSEQ].reshape(NSEQ * T, D)),
            "wqkv": np.ascontiguousarray(Wqkv_p),
            "bqkv": np.ascontiguousarray(bqkv_p[0:2 * D].reshape(1, 2 * D)),
            "bvcolh": np.ascontiguousarray(bqkv_p[2 * D:].reshape(8, 128).T),
            "wproj": np.ascontiguousarray(np.asarray(L0_attn_Wproj, np.float32)),
            "bproj": np.ascontiguousarray(np.asarray(L0_attn_bproj, np.float32).reshape(1, D)),
            "wgate": np.ascontiguousarray(np.asarray(L0_ffn_w_gate, np.float32)),
            "wup": np.ascontiguousarray(np.asarray(L0_ffn_w_up, np.float32)),
            "wdown": np.ascontiguousarray(np.asarray(L0_ffn_w_down, np.float32)),
            "ropeA": consts["ropeA"], "ropeB": consts["ropeB"],
            "ident": consts["ident"], "mask": consts["mask"],
            "ones_row": consts["ones_row"], "ones_col": consts["ones_col"],
            "ones_pair": consts["ones_pair"],
        })
    res = run_bass_kernel_spmd(nc, in_maps, core_ids=list(range(NCORES)))
    shards = [np.asarray(res.results[c]["out"]) for c in range(NCORES)]
    out = np.empty((B, T, D), shards[0].dtype)
    for c in range(NCORES):
        out[c * NSEQ:(c + 1) * NSEQ] = shards[c].reshape(NSEQ, T, D)
    return out


# version 17
# speedup vs baseline: 1.0535x; 1.0535x over previous
import math
import numpy as np
import ml_dtypes
import concourse.bass as bass
import concourse.mybir as mybir
from concourse.bass_utils import run_bass_kernel_spmd
from contextlib import ExitStack

F32 = mybir.dt.float32
BF16 = mybir.dt.bfloat16
AF = mybir.ActivationFunctionType
OP = mybir.AluOpType
AX = mybir.AxisListType

NCORES = 8
B = 16
T = 2048
D = 1024
H = 16
DH = 64
DFF = 2816
NSEQ = B // NCORES
NT = T // 128
NKC = D // 128
NDC = DFF // 128
HALF = DFF // 2
NHC = HALF // 128

MASK_NEG = -30000.0

_DEBUG_TAPS = []
_DEBUG = False


class Prog:
    ENGS = ("sync", "gpsimd", "tensor", "vector", "scalar")

    def __init__(self, nc, es):
        self.nc = nc
        self.es = es
        self.sems = {}
        self.cnt = {}
        for name in ("T", "V", "S"):
            self.sem(name)
        self.ops = {e: [] for e in self.ENGS}
        self.observed = {e: {} for e in self.ENGS}

    def sem(self, name):
        if name not in self.sems:
            self.sems[name] = self.es.enter_context(self.nc.semaphore(name))
            self.cnt[name] = 0
        return name

    def emit(self, eng, fn, waits=(), inc=None, inc_n=1):
        wl = []
        for s, v in waits:
            if self.observed[eng].get(s, 0) < v:
                wl.append((s, v))
                self.observed[eng][s] = v
        tok = None
        if inc is not None:
            self.cnt[inc] += inc_n
            tok = self.cnt[inc]
        self.ops[eng].append((wl, fn, inc, inc_n))
        return tok

    def drain(self, eng):
        self.ops[eng].append(([], lambda e: e.drain(), None, 1))

    def tok(self, sem):
        return (sem, self.cnt[sem])

    def barrier(self, tag=""):
        toks = [(s, v) for s, v in self.cnt.items() if v > 0]
        for e in self.ENGS:
            for s, v in toks:
                if self.observed[e].get(s, 0) < v:
                    self.ops[e].append(([(s, v)], None, None, 1))
                    self.observed[e][s] = v

    def build(self, block):
        sems = self.sems

        def mk(eng_name):
            def body(eng):
                for wl, fn, inc, inc_n in self.ops[eng_name]:
                    for s, v in wl:
                        eng.wait_ge(sems[s], v)
                    if fn is None:
                        continue
                    r = fn(eng)
                    if inc is not None:
                        r.then_inc(sems[inc], inc_n)
            return body

        block.sync(mk("sync"))
        block.gpsimd(mk("gpsimd"))
        block.tensor(mk("tensor"))
        block.vector(mk("vector"))
        block.scalar(mk("scalar"))


def _host_consts():
    import jax
    import jax.numpy as jnp

    def _rope_tensor():
        pos = jnp.arange(T, dtype=jnp.float32)
        angles = pos[:, None] / 10000.0 ** (jnp.arange(0, DH, 2, dtype=jnp.float32) / DH)
        return jnp.stack([jnp.cos(angles), jnp.sin(angles)], axis=-1).reshape(T, DH)

    with jax.default_device(jax.devices("cpu")[0]):
        A = np.asarray(jax.jit(_rope_tensor)(), dtype=np.float32)
    Bm = np.float32(1.0) - A
    Bs = Bm.copy()
    Bs[:, 0::2] *= -1.0
    At = np.zeros((128, NT * DH), np.float32)
    Bt = np.zeros((128, NT * DH), np.float32)
    for t in range(NT):
        At[:, t * DH:(t + 1) * DH] = A[t * 128:(t + 1) * 128]
        Bt[:, t * DH:(t + 1) * DH] = Bs[t * 128:(t + 1) * 128]
    ident = np.eye(128).astype(ml_dtypes.bfloat16)
    mask = np.where(np.arange(128)[None, :] >= np.arange(128)[:, None],
                    0.0, MASK_NEG).astype(np.float32)
    return {
        "ropeA": At,
        "ropeB": Bt,
        "ident": ident,
        "identf": np.eye(128).astype(np.float32),
        "mask": mask,
        "ones_row": np.ones((1, 128), ml_dtypes.bfloat16),
        "ones_pair": (lambda z: (z.__setitem__((0, slice(None)), 1.0),
                                 z.__setitem__((64, slice(None)), 1.0), z)[-1])(
            np.zeros((65, 128), ml_dtypes.bfloat16)),
        "ones_row_f32": np.ones((1, 128), np.float32),
        "ones_col": np.ones((128, 1), ml_dtypes.bfloat16),
    }


def _permute_qkv(Wqkv, bqkv):
    idx = np.zeros(3 * D, np.int64)
    for h in range(H):
        base = h * 3 * DH
        for c in range(DH):
            idx[h * DH + c] = base + c
            idx[D + h * DH + c] = base + DH + c
            idx[2 * D + h * DH + c] = base + 2 * DH + c
    return Wqkv[idx], bqkv[idx]


def emit_quant_head(p, tn, src, F, after=(), sg_wait=None, sq_wait=(),
                    vbuf=None):
    nb = F // 16
    g = lambda a: a.rearrange("p (g k) -> p g k", k=16)
    v_ = p.emit
    mb = tn["mblk"][:, 0:nb]
    rb = tn["rblk"][:, 0:nb]
    r6 = tn["r6sq"][:, 0:nb]
    s2 = tn["s2b"][:, 0:nb]
    sg = tn["sgq"][:, 0:F]
    vt = src if vbuf is None else vbuf[:, 0:F]
    aft = list(after)
    if sg_wait is None:
        sg_wait = [("V", p.cnt["V"])]
    tok_am = v_("vector", lambda e: e.tensor_reduce(mb, g(src), AX.X, OP.max,
                                                    apply_absolute_value=True),
                waits=aft, inc="V")
    p.drain("vector")
    v_("scalar", lambda e: e.sign(sg, src), waits=aft + list(sg_wait))
    p.drain("scalar")
    v_("vector", lambda e: e.reciprocal(rb, mb))
    p.drain("vector")
    v_("vector", lambda e: e.tensor_tensor(r6, rb, rb, OP.mult))
    v_("vector", lambda e: e.tensor_scalar(s2, mb, 1.0 / 12.0, 5.0e-13, OP.mult, OP.max))
    p.drain("vector")
    v_("vector", lambda e: e.tensor_scalar(rb, r6, 36.0, 9.0e24, OP.mult, OP.min))
    p.drain("vector")
    tok_sq = v_("scalar", lambda e: e.activation(vt, src, AF.Square),
                waits=[("V", tok_am)] + list(sq_wait), inc="S")
    p.drain("scalar")
    r6b = rb.rearrange("p g -> p g ()").broadcast_to([128, nb, 16])
    v_("vector", lambda e: e.tensor_tensor(g(vt), g(vt), r6b, OP.mult),
       waits=[("S", tok_sq)])
    p.drain("vector")
    return vt, tok_am


def emit_quant_tail(p, tn, vt, F, out_bf, out_wait=()):
    nb = F // 16
    g = lambda a: a.rearrange("p (g k) -> p g k", k=16)
    v_ = p.emit
    s2 = tn["s2b"][:, 0:nb]
    sg = tn["sgq"][:, 0:F]
    aA = tn["accqA"][:, 0:F]
    aB = tn["accqB"][:, 0:F]
    v_("vector", lambda e: e.tensor_scalar(aA, vt, 25.0, 2.0, OP.is_gt, OP.mult))
    p.drain("vector")
    v_("vector", lambda e: e.scalar_tensor_tensor(aB, vt, 12.25, aA, OP.is_gt, OP.add))
    p.drain("vector")
    v_("vector", lambda e: e.scalar_tensor_tensor(aA, vt, 6.25, aB, OP.is_gt, OP.add))
    p.drain("vector")
    v_("vector", lambda e: e.tensor_scalar(aB, aA, 2.0, None, OP.mult))
    p.drain("vector")
    v_("vector", lambda e: e.scalar_tensor_tensor(aA, vt, 3.0625, aB, OP.is_gt, OP.add))
    p.drain("vector")
    v_("vector", lambda e: e.scalar_tensor_tensor(aB, vt, 1.5625, aA, OP.is_gt, OP.add))
    p.drain("vector")
    v_("vector", lambda e: e.scalar_tensor_tensor(aA, vt, 0.5625, aB, OP.is_gt, OP.add))
    p.drain("vector")
    v_("vector", lambda e: e.scalar_tensor_tensor(aB, vt, 0.0625, aA, OP.is_gt, OP.add))
    p.drain("vector")
    s2b = s2.rearrange("p g -> p g ()").broadcast_to([128, nb, 16])
    v_("vector", lambda e: e.tensor_tensor(g(aA), g(aB), s2b, OP.mult))
    p.drain("vector")
    tok = v_("vector", lambda e: e.tensor_tensor(out_bf, aA, sg, OP.mult),
             waits=list(out_wait), inc="V")
    p.drain("vector")
    return tok


def emit_quant(p, tn, src, F, out_bf, after=(), out_wait=(), sg_wait=None):
    vt, _ = emit_quant_head(p, tn, src, F, after=after, sg_wait=sg_wait)
    return emit_quant_tail(p, tn, vt, F, out_bf, out_wait=out_wait)


def build_program(limit=None):
    nc = bass.Bass("TRN2", target_bir_lowering=False)
    es = ExitStack()

    xh = nc.dram_tensor("x", [NSEQ * T, D], F32, kind="ExternalInput").ap()
    wqkv = nc.dram_tensor("wqkv", [3 * D, D], F32, kind="ExternalInput").ap()
    bqkv = nc.dram_tensor("bqkv", [1, 2048], F32, kind="ExternalInput").ap()
    bvcolh = nc.dram_tensor("bvcolh", [128, 8], F32, kind="ExternalInput").ap()
    wproj = nc.dram_tensor("wproj", [D, D], F32, kind="ExternalInput").ap()
    bproj = nc.dram_tensor("bproj", [1, D], F32, kind="ExternalInput").ap()
    wgate = nc.dram_tensor("wgate", [DFF, D], F32, kind="ExternalInput").ap()
    wup = nc.dram_tensor("wup", [DFF, D], F32, kind="ExternalInput").ap()
    wdown = nc.dram_tensor("wdown", [D, DFF], F32, kind="ExternalInput").ap()
    c_ropeA = nc.dram_tensor("ropeA", [128, NT * DH], F32, kind="ExternalInput").ap()
    c_ropeB = nc.dram_tensor("ropeB", [128, NT * DH], F32, kind="ExternalInput").ap()
    c_ident = nc.dram_tensor("ident", [128, 128], BF16, kind="ExternalInput").ap()
    c_mask = nc.dram_tensor("mask", [128, 128], F32, kind="ExternalInput").ap()
    c_ones_row = nc.dram_tensor("ones_row", [1, 128], BF16, kind="ExternalInput").ap()
    c_ones_col = nc.dram_tensor("ones_col", [128, 1], BF16, kind="ExternalInput").ap()
    c_ones_pair = nc.dram_tensor("ones_pair", [65, 128], BF16, kind="ExternalInput").ap()
    out_h = nc.dram_tensor("out", [NSEQ * T, D], F32, kind="ExternalOutput").ap()
    sWg = nc.dram_tensor("sWg", [NKC, 128, DFF], BF16, kind="Internal").ap()
    sWu = nc.dram_tensor("sWu", [NKC, 128, DFF], BF16, kind="Internal").ap()
    sWd = nc.dram_tensor("sWd", [NDC, 128, D], BF16, kind="Internal").ap()
    mscr = nc.dram_tensor("mscr", [1, 2048], BF16, kind="Internal").ap()
    sWqT = nc.dram_tensor("sWqT", [NKC, 128, 3072], BF16, kind="Internal").ap()
    sWpT = nc.dram_tensor("sWpT", [NKC, 128, 1024], BF16, kind="Internal").ap()

    dbg = {}
    if _DEBUG:
        for name, shape, dt in _DEBUG_TAPS:
            dbg[name] = nc.dram_tensor(name, shape, dt, kind="ExternalOutput").ap()

    sb = lambda name, shape, dt: es.enter_context(nc.sbuf_tensor(name, shape, dt))[:]
    ropebuf = sb("ropebuf", [128, 256], F32)
    ident = sb("ident_s", [128, 128], BF16)
    maskt = sb("mask_s", [128, 128], F32)
    onesr = sb("onesr", [1, 128], BF16)
    onesc = sb("onesc", [128, 1], BF16)
    onespair = sb("onespair", [65, 128], BF16)
    brow = sb("brow", [1, 2048], BF16)
    bvcol = sb("bvcol", [128, 8], BF16)
    xres = sb("xres", [128, NT * D], BF16)
    qkt = sb("qkt", [128, 32 * 1024], BF16)
    ub = sb("ub", [128, 16 * 1024], BF16)
    wb = sb("wb", [128, 24576], BF16)
    vtile = sb("vtile", [128, HALF], F32)
    htile = sb("htile", [128, HALF], F32)
    kropet = sb("kropet", [128, D], BF16)
    qropet = sb("qropet", [128, D], BF16)
    accA = sb("accA", [128, D], BF16)
    accB = sb("accB", [128, D], BF16)
    stats = sb("stats", [128, 8], F32)
    mblk = sb("mblk", [128, HALF // 16], F32)
    rblk = sb("rblk", [128, HALF // 16], F32)
    r6sq = sb("r6sq", [128, HALF // 16], F32)
    s2b = sb("s2b", [128, HALF // 16], F32)
    qn2 = sb("qn2", [128, H], F32)
    kn2 = sb("kn2", [128, H], F32)
    nrm = sb("nrm", [128, H], F32)
    negBs = sb("negBs", [128, H], F32)
    lrec = sb("lrec", [128, H], F32)
    nrow = sb("nrow", [1, 16], BF16)
    xln = accB
    xfst = accA
    outt = vtile[:, 0:1024]
    PS = es.enter_context(nc.psum_tensor("PS", [128, 2048], F32))[:]
    PSD = es.enter_context(nc.psum_tensor("PSD", [128, 1088], F32))[:]
    PST = es.enter_context(nc.psum_tensor("PST", [128, 512], BF16))[:]

    p = Prog(nc, es)
    for sname in ("LC", "LGC", "LW0", "LW1", "LG0", "LG1", "LB0", "LB1",
                  "SB0", "SB1", "LQ",
                  "LX0", "LX1", "LX2", "LX3", "MB0", "MB1", "MG0", "MG1",
                  "RP0", "RP1", "LP", "LF", "ST"):
        p.sem(sname)


    WQKVT = wb[:, 0:24576]
    OFM = wb[:, 0:16384]
    PT0 = wb[:, 16384:17408]
    PT1 = wb[:, 17408:18432]
    opair = wb[:, 18432:20480]
    WDH = wb[:, 0:NHC * 1024]
    accqA = wb[:, 11264:12672]
    accqB = wb[:, 12672:14080]
    sgq = wb[:, 14080:15488]
    hstage = wb[:, 15488:16896]
    WGH = qkt[:, 0:NKC * HALF]
    WUH = qkt[:, NKC * HALF:2 * NKC * HALF]
    hqv = qkt[:, 22528:23936]
    WPROJT = qkt[:, 0:8192]
    bprow2 = qkt[0:1, 8192:9216]
    pbias = qkt[0:1, 9216:10240]
    cbias = qkt[0:1, 10240:11264]

    tn = dict(mblk=mblk, rblk=rblk, r6sq=r6sq, s2b=s2b,
              sgq=sgq, accqA=accqA, accqB=accqB)
    tn_f0 = dict(mblk=mblk, rblk=rblk, r6sq=r6sq, s2b=s2b,
                 accqA=qkt[:, 25344:26368], accqB=qkt[:, 26368:27392],
                 sgq=qkt[:, 27392:28416])

    def qk_slice(h, which, c0, c1):
        r0 = (h % 2) * 64
        base = (h // 2) * 4096 + which * 2048
        return qkt[r0:r0 + 64, base + c0:base + c1]

    p.emit("sync", lambda e: e.dma_start(out=ident, in_=c_ident), inc="LC", inc_n=16)
    p.emit("sync", lambda e: e.dma_start(out=maskt, in_=c_mask), inc="LC", inc_n=16)
    p.emit("sync", lambda e: e.dma_start(out=onesr, in_=c_ones_row), inc="LC", inc_n=16)
    p.emit("sync", lambda e: e.dma_start(out=onesc, in_=c_ones_col), inc="LC", inc_n=16)
    p.emit("sync", lambda e: e.dma_start(out=onespair, in_=c_ones_pair), inc="LC", inc_n=16)
    p.emit("gpsimd", lambda e: e.dma_start(out=brow, in_=bqkv), inc="LGC", inc_n=16)
    p.emit("gpsimd", lambda e: e.dma_start(out=bvcol, in_=bvcolh), inc="LGC", inc_n=16)
    consts_tok = [p.tok("LC"), p.tok("LGC")]

    def emit_transposes(src_bf, n, dst_fn, waits, pair_mode=False):
        ntr = n
        j = 0
        while j < ntr:
            jn = min(4, ntr - j)
            for i in range(jn):
                if pair_mode:
                    hh = j + i
                    r0 = (hh % 2) * 64
                    slot = i // 2
                    fn = (lambda e, hh=hh, slot=slot, r0=r0: e.transpose(
                        PST[r0:r0 + 64, slot * 128:(slot + 1) * 128],
                        src_bf[:, hh * 64:(hh + 1) * 64], ident,
                        tile_position=(0, r0)))
                else:
                    fn = (lambda e, jj=j + i, i=i: e.transpose(
                        PST[:, i * 128:(i + 1) * 128],
                        src_bf[:, jj * 128:(jj + 1) * 128], ident))
                p.emit("tensor", fn, waits=waits if (j == 0 and i == 0) else [],
                       inc="T" if i == jn - 1 else None)
            tok_t = p.cnt["T"]
            nu = (jn // 2 if pair_mode else jn)
            tok_e = p.emit("scalar", lambda e, j=j, nu=nu: e.copy(
                dst_fn(j, nu),
                PST[:, 0:nu * 128].rearrange("p (k c) -> p k c", c=128)),
                waits=[("T", tok_t)], inc="S")
            p.drain("scalar")
            p.emit("tensor", None, waits=[("S", tok_e)])
            j += jn
        return p.cnt["S"]


    def emit_xload(s):
        toks = {}
        for g4 in range(4):
            sem = "LX%d" % g4
            for t in range(g4 * 4, g4 * 4 + 4):
                p.emit("gpsimd", lambda e, t=t, s=s: e.dma_start(
                    out=xres[:, t * D:(t + 1) * D],
                    in_=xh[s * T + t * 128: s * T + (t + 1) * 128, :]),
                    inc=sem, inc_n=16)
            toks[g4] = (sem, p.cnt[sem])
        return toks


    def emit_wprep():
        jobs = []
        for (s_ap, nrb, K, dst, base) in ((wgate, NDC, D, sWg, 0),
                                          (wup, NDC, D, sWu, 0),
                                          (wdown, NKC, DFF, sWd, 0)):
            nh = K // HALF if K > HALF else 1
            for r in range(nrb):
                for half in range(nh):
                    jobs.append((s_ap, r, half, min(HALF, K), dst, base, True))
        bjobs = []
        for r in range(3 * D // 128):
            bjobs.append((wqkv, r, 0, D, sWqT, 0, False))
        for r in range(NKC):
            bjobs.append((wproj, r, 0, D, sWpT, 0, False))
        out = []
        bi = 0
        for i, j in enumerate(jobs):
            out.append(j)
            if i % 2 == 1 and bi < len(bjobs):
                out.append(bjobs[bi]); bi += 1
        out.extend(bjobs[bi:])
        jobs = out
        srcs = [htile, vtile]
        hqs = [qkt[:, 22528:23936], qkt[:, 23936:25344]]
        hss = [wb[:, 15488:16896], wb[:, 16896:18304]]
        tns = [dict(tn), dict(tn)]
        tns[1]["sgq"] = wb[:, 18304:19712]
        lws = ["LW0", "LW1"]
        lgs = ["LG0", "LG1"]
        rel_src = {0: ("V", 0), 1: ("V", 0)}
        rel_hq = {0: ("T", 0), 1: ("T", 0)}
        rel_hs = {0: ("LG0", 0), 1: ("LG1", 0)}
        lbs = ["LB0", "LB1"]
        sbs = ["SB0", "SB1"]
        hqBs = [qkt[:, 25344:26368], qkt[:, 26368:27392]]
        hsBs = [wb[:, 19712:20736], wb[:, 20736:21760]]
        rel_hqB = {0: ("T", 0), 1: ("T", 0)}
        rel_hsB = {0: ("SB0", 0), 1: ("SB1", 0)}
        bcnt = [0]
        pend_fin = [None]

        def finisher(hq, hs, nck, half, r, dst, base, trw, par, do_q, bpar):
            def fin():
                emit_transposes(hq, nck,
                                lambda j, u, hs=hs: hs[:, j * 128:(j + u) * 128
                                                       ].rearrange("p (k c) -> p k c", c=128),
                                waits=trw)
                if do_q:
                    rel_hq[par] = ("T", p.cnt["T"])
                else:
                    rel_hqB[bpar] = ("T", p.cnt["T"])
                tok_ev = p.cnt["S"]
                for j in range(nck):
                    cglob = half * (HALF // 128) + j
                    p.emit("gpsimd", lambda e, j=j, cglob=cglob, r=r, dst=dst,
                           base=base, hs=hs: e.dma_start(
                        out=dst[base + cglob, 0:128, r * 128:(r + 1) * 128],
                        in_=hs[:, j * 128:(j + 1) * 128]),
                        waits=[("S", tok_ev)] if j == 0 else [],
                        inc=(lgs[par] if do_q else sbs[bpar]), inc_n=16)
                if do_q:
                    rel_hs[par] = (lgs[par], p.cnt[lgs[par]])
                else:
                    rel_hsB[bpar] = (sbs[bpar], p.cnt[sbs[bpar]])
            return fin

        for i, (s_ap, r, half, Fh, dst, base, do_q) in enumerate(jobs):
            par = i % 2
            bpar = bcnt[0] % 2
            srcb = srcs[par][:, 0:Fh]
            hq = (hqs[par] if do_q else hqBs[bpar])[:, 0:Fh]
            hs = hss[par] if do_q else hsBs[bpar]
            if do_q:
                tok_ld = p.emit("sync", lambda e, s_ap=s_ap, r=r, half=half,
                                Fh=Fh, srcb=srcb: e.dma_start(
                    out=srcb, in_=s_ap[r * 128:(r + 1) * 128,
                                       half * HALF:half * HALF + Fh]),
                    waits=[rel_src[par]], inc=lws[par], inc_n=16)
                vt, _ = emit_quant_head(p, tns[par], srcb, Fh,
                                        after=[(lws[par], tok_ld)],
                                        sg_wait=[rel_src[par]])
                if pend_fin[0] is not None:
                    pend_fin[0]()
                tok_q = emit_quant_tail(p, tns[par], vt, Fh, hq,
                                        out_wait=[rel_hq[par]])
                rel_src[par] = ("V", tok_q)
                trw = [("V", tok_q), rel_hs[par]]
            else:
                tok_ld = p.emit("gpsimd", lambda e, s_ap=s_ap, r=r, Fh=Fh,
                                hq=hq: e.dma_start(
                    out=hq, in_=s_ap[r * 128:(r + 1) * 128, 0:Fh]),
                    waits=[rel_hqB[bpar]], inc=lbs[bpar], inc_n=16)
                if pend_fin[0] is not None:
                    pend_fin[0]()
                trw = [(lbs[bpar], tok_ld), rel_hsB[bpar]]
                bcnt[0] += 1
            pend_fin[0] = finisher(hq, hs, Fh // 128, half, r, dst, base,
                                   trw, par, do_q, bpar)
        if pend_fin[0] is not None:
            pend_fin[0]()
            pend_fin[0] = None

    for s, v in consts_tok:
        p.emit("vector", None, waits=[(s, v)])
        p.emit("scalar", None, waits=[(s, v)])
        p.emit("tensor", None, waits=[(s, v)])
    xload0 = emit_xload(0)
    emit_wprep()
    p.barrier("after-wprep")

    def emit_ln(xt_ap, out_bf_ap, after=(), stats_wait=None, dump=None):
        if dump is None:
            dump = vtile[:, 0:D]
        if stats_wait is None:
            stats_wait = [("V", p.cnt["V"])]
        gate = list(after) + list(stats_wait)
        p.emit("scalar", lambda e: e.activation(dump, xt_ap, AF.Copy,
                                                accum_out=stats[:, 0:1]),
               waits=gate)
        p.drain("scalar")
        tok_s = p.emit("scalar", lambda e: e.activation(dump, xt_ap, AF.Square,
                                                        accum_out=stats[:, 1:2]),
                       inc="S")
        p.drain("scalar")
        p.emit("vector", lambda e: e.tensor_scalar(stats[:, 2:3], stats[:, 0:1],
                                                   -1.0 / D, None, OP.mult),
               waits=[("S", tok_s)])
        p.drain("vector")
        p.emit("vector", lambda e: e.tensor_tensor(stats[:, 3:4], stats[:, 2:3],
                                                   stats[:, 2:3], OP.mult))
        p.drain("vector")
        p.emit("vector", lambda e: e.scalar_tensor_tensor(stats[:, 4:5], stats[:, 1:2],
                                                          1.0 / D, stats[:, 3:4],
                                                          OP.mult, OP.subtract))
        p.drain("vector")
        tok_v = p.emit("vector", lambda e: e.tensor_scalar(stats[:, 5:6], stats[:, 4:5],
                                                           1e-5, None, OP.add), inc="V")
        p.drain("vector")
        p.emit("scalar", lambda e: e.activation(stats[:, 6:7], stats[:, 5:6], AF.Ln),
               waits=[("V", tok_v)])
        p.drain("scalar")
        tok_r = p.emit("scalar", lambda e: e.activation(stats[:, 7:8], stats[:, 6:7],
                                                        AF.Exp, scale=-0.5), inc="S")
        p.drain("scalar")
        tok = p.emit("vector", lambda e: e.tensor_scalar(out_bf_ap, xt_ap,
                                                         stats[:, 2:3], stats[:, 7:8],
                                                         OP.add, OP.mult),
                     waits=[("S", tok_r), ("S", p.cnt["S"])], inc="V")
        p.drain("vector")
        return tok

    def seq_phase(s, xload):
        for k in range(NKC):
            p.emit("sync", lambda e, k=k: e.dma_start(
                out=WQKVT[:, k * 3072:(k + 1) * 3072], in_=sWqT[k]),
                waits=([("T", p.cnt["T"]), ("S", p.cnt["S"]),
                        ("V", p.cnt["V"])] if k == 0 else []),
                inc="LQ", inc_n=16)
        tok_lq = ("LQ", p.cnt["LQ"])

        prev_ln = [None]
        pend_qk = [None]
        rp_tok = {}
        ln_tok = {}
        tr_tok = {}

        def emit_lntile(t):
            sl = (t % 2) * 128
            rps = "RP0" if t % 2 == 0 else "RP1"
            p.emit("sync", lambda e, t=t, sl=sl: e.dma_start(
                out=ropebuf[:, sl:sl + 64], in_=c_ropeA[:, t * DH:(t + 1) * DH]),
                waits=[("V", p.cnt["V"])], inc=rps, inc_n=16)
            rp_tok[t] = (rps, p.emit("sync", lambda e, t=t, sl=sl: e.dma_start(
                out=ropebuf[:, sl + 64:sl + 128], in_=c_ropeB[:, t * DH:(t + 1) * DH]),
                inc=rps, inc_n=16))
            xt = xres[:, t * D:(t + 1) * D]
            tok_ln = emit_ln(xt, xln, after=[xload[t // 4]],
                             stats_wait=prev_ln[0])
            prev_ln[0] = [("V", tok_ln)]
            ln_tok[t] = tok_ln
            tr_tok[t] = emit_transposes(
                xln, NKC, lambda j, jn: xfst[:, j * 128:(j + jn) * 128
                                             ].rearrange("p (k c) -> p k c", c=128),
                waits=[("V", tok_ln)])

        emit_lntile(0)
        for t in range(NT):
            sl = (t % 2) * 128
            def psn(nck):
                if nck < 4:
                    return PS[:, nck * 512:(nck + 1) * 512]
                return PSD[:, (nck - 4) * 512:(nck - 3) * 512]
            first_w = [("S", tr_tok[t]), ("V", p.cnt["V"]), ("S", p.cnt["S"]),
                       tok_lq]
            for k in range(NKC):
                for nck in range(6):
                    p.emit("tensor", lambda e, k=k, nck=nck: e.matmul(
                        psn(nck), xfst[:, k * 128:(k + 1) * 128],
                        WQKVT[:, k * 3072 + nck * 512: k * 3072 + (nck + 1) * 512],
                        start=(k == 0), stop=(k == NKC - 1 and nck >= 4)),
                        waits=first_w if (k == 0 and nck == 0) else [])
            if pend_qk[0] is not None:
                pend_qk[0]()
                pend_qk[0] = None
            for nck in range(4):
                for i4 in range(4):
                    r16 = nck * 4 + i4
                    p.emit("tensor", lambda e, nck=nck, i4=i4, r16=r16: e.matmul(
                        psn(nck)[:, i4 * 128:(i4 + 1) * 128], onesr,
                        brow[0:1, r16 * 128:(r16 + 1) * 128],
                        start=False, stop=(i4 == 3)),
                        inc="T" if (nck == 3 and i4 == 3) else None)
            tok_qkv = p.cnt["T"]
            if t + 1 < NT:
                emit_lntile(t + 1)
            rA = ropebuf[:, sl:sl + 64].rearrange(
                "p (r two) -> p () r two", two=2).broadcast_to([128, H, DH // 2, 2])
            rB = ropebuf[:, sl + 64:sl + 128].rearrange(
                "p (r two) -> p () r two", two=2).broadcast_to([128, H, DH // 2, 2])
            g4 = lambda a: a.rearrange("p (h r two) -> p h r two", h=H, two=2)
            psq = g4(PS[:, 0:1024])
            psk = g4(PS[:, 1024:2048])
            q4 = g4(qropet[:, 0:1024])
            k4 = g4(kropet[:, 0:1024])
            pd4 = g4(PSD[:, 0:1024])
            p.emit("vector", lambda e, k4=k4, psq=psq, rB=rB: e.tensor_tensor(
                k4, psq[:, :, :, ::-1], rB, OP.mult),
                waits=[("T", tok_qkv), rp_tok[t]])
            p.drain("vector")
            p.emit("vector", lambda e, q4=q4, psq=psq, rA=rA:
                   e.tensor_tensor(q4, psq, rA, OP.mult))
            p.drain("vector")
            p.emit("vector", lambda e, q4=q4, k4=k4:
                   e.tensor_tensor(q4, q4, k4, OP.add))
            p.drain("vector")
            tok_vev = p.emit("scalar", lambda e, t=t: e.copy(
                ub[:, t * D:(t + 1) * D], PSD[:, 0:1024]),
                waits=[("T", tok_qkv)], inc="S")
            p.drain("scalar")
            p.emit("vector", lambda e, pd4=pd4, psk=psk, rB=rB: e.tensor_tensor(
                pd4, psk[:, :, :, ::-1], rB, OP.mult),
                waits=[("S", tok_vev)])
            p.drain("vector")
            p.emit("vector", lambda e, k4=k4, psk=psk, rA=rA:
                   e.tensor_tensor(k4, psk, rA, OP.mult))
            p.drain("vector")
            tok_rope = p.emit("vector", lambda e, k4=k4, pd4=pd4:
                              e.tensor_tensor(k4, k4, pd4, OP.add), inc="V")
            p.drain("vector")
            def qk_tr(t=t, tok_rope=tok_rope):
                for which, src_bf in ((0, qropet), (1, kropet)):
                    def dstqk(j, u, which=which, t=t):
                        v = qkt.rearrange("p (pr c) -> p pr c", c=4096)
                        return v[:, j:j + u,
                                 which * 2048 + t * 128: which * 2048 + (t + 1) * 128]
                    emit_transposes(src_bf[:, 0:1024], NKC,
                                    lambda j, u, which=which, t=t: dstqk(j, u, which, t),
                                    waits=[("V", tok_rope)])
            pend_qk[0] = qk_tr
        if pend_qk[0] is not None:
            pend_qk[0]()
            pend_qk[0] = None
        p.barrier("p3-end")
        return p.cnt["S"]

    def attn_phase(s):
        mrep = wb[:, 20480:22528]
        mrow = wb[0:1, 22528:23552]
        lnrow = htile
        for h in range(H):
            r0 = (h % 2) * 64
            tp = (64, 0) if (h % 2) else None
            qbase = (h // 2) * 4096
            kbase = qbase + 2048
            tok_exp = {}
            tok_mg = {}
            pend = [None]
            for pss in range(2):
                for w in range(2):
                    for sc in range(8 * (w + 1)):
                        ping = PS[:, 0:1024] if sc % 2 == 0 else PS[:, 1024:2048]
                        ptb = PT0 if sc % 2 == 0 else PT1
                        qlo_g = max(w * 1024, sc * 128)
                        lloc = qlo_g - w * 1024
                        a = lloc
                        first = True
                        while a < 1024:
                            b2 = min(1024, (a // 512 + 1) * 512)
                            wl = [("S", tok_exp.get(sc % 2, 0)),
                                  ("V", p.cnt["V"])] if first else []
                            p.emit("tensor", lambda e, a=a, b2=b2, sc=sc, w=w,
                                   ping=ping, r0=r0, kbase=kbase, qbase=qbase,
                                   tp=tp: e.matmul(
                                ping[:, a:b2],
                                qkt[r0:r0 + 64, kbase + sc * 128: kbase + (sc + 1) * 128],
                                qkt[r0:r0 + 64, qbase + w * 1024 + a: qbase + w * 1024 + b2],
                                start=True, stop=True, tile_position=tp),
                                waits=wl,
                                inc="T" if b2 == 1024 else None)
                            first = False
                            a = b2
                        tok_sc = p.cnt["T"]
                        if sc >= 8 * w:
                            p.emit("vector", lambda e, ping=ping, lloc=lloc:
                                   e.tensor_tensor(
                                ping[:, lloc:lloc + 128], ping[:, lloc:lloc + 128],
                                maskt, OP.add), waits=[("T", tok_sc)], inc="V")
                            if pss == 1:
                                p.drain("vector")
                        if pss == 1:
                            p.emit("vector", lambda e, ping=ping, lloc=lloc, w=w:
                                   e.tensor_tensor(
                                ping[:, lloc:1024], ping[:, lloc:1024],
                                mrep[:, w * 1024 + lloc:(w + 1) * 1024], OP.add),
                                waits=[("T", tok_sc), tok_mg[w]], inc="V")
                        tok_m = p.cnt["V"]
                        scl = (1.0 / 64.0) if pss == 0 else 0.125
                        te = p.emit("scalar", lambda e, ping=ping, ptb=ptb,
                                    lloc=lloc, scl=scl: e.activation(
                                    ptb[:, lloc:1024], ping[:, lloc:1024],
                                    AF.Exp, scale=scl),
                                    waits=[("T", tok_sc), ("V", tok_m)], inc="S")
                        p.drain("scalar")
                        tok_exp[sc % 2] = te

                        def consume(sc=sc, w=w, ptb=ptb, te=te, lloc=lloc, pss=pss, h=h):
                            if pss == 0:
                                nsj = (1024 - lloc + 511) // 512
                                aa = lloc
                                for sj in range(nsj):
                                    b3 = min(1024, (aa // 512 + 1) * 512)
                                    is_b4 = aa < 512
                                    stp = (is_b4 and sc == 8 * w + 3) or \
                                          ((not is_b4) and sc == 8 * (w + 1) - 1)
                                    p.emit("tensor", lambda e, aa=aa, b3=b3, ptb=ptb,
                                           sc=sc, stp=stp: e.matmul(
                                        PSD[0:1, aa:b3], onesc,
                                        ptb[:, aa:b3],
                                        start=(sc == 0),
                                        stop=stp),
                                        waits=[("S", te)] if sj == 0 else [],
                                        inc="T" if sj == nsj - 1 else None)
                                    aa = b3
                            else:
                                for qtl in range(max(0, sc - 8 * w), 8):
                                    qtg = 8 * w + qtl
                                    st_ap = ptb[:, qtl * 128:(qtl + 1) * 128]
                                    p.emit("tensor", lambda e, st_ap=st_ap, qtg=qtg,
                                           sc=sc, w=w, qtl=qtl, h=h: e.matmul(
                                        PSD[:, qtg * 64:(qtg + 1) * 64], st_ap,
                                        ub[:, sc * D + h * 64: sc * D + (h + 1) * 64],
                                        start=(sc == 0 and qtg % 8 == 0),
                                        stop=(sc == 8 * (w + 1) - 1 and qtl == 7)),
                                        waits=[("S", te)] if qtl == max(0, sc - 8 * w) else [])
                                    p.emit("tensor", lambda e, st_ap=st_ap, qtg=qtg,
                                           sc=sc, w=w, qtl=qtl: e.matmul(
                                        PSD[:, 1024 + qtg:1025 + qtg], st_ap, onesc,
                                        start=(w == 0 and sc == 0 and qtl == 0),
                                        stop=(w == 1 and sc == 15 and qtl == 7)),
                                        inc="T" if qtl == 7 else None)
                        if pend[0] is not None:
                            pend[0]()
                        pend[0] = consume
                    if pend[0] is not None:
                        pend[0]()
                        pend[0] = None
                    if pss == 0:
                        tok_L = p.cnt["T"]
                        lr = lnrow[0:1, 0:1024]
                        tok_ln2 = p.emit("scalar", lambda e, lr=lr: e.activation(
                            lr, PSD[0:1, 0:1024], AF.Ln),
                            waits=[("T", tok_L), ("V", p.cnt["V"])], inc="S")
                        p.drain("scalar")
                        tok_sr = p.emit("vector", lambda e, lr=lr: e.tensor_scalar(
                            mrow, lr, -64.0, None, OP.mult),
                            waits=[("S", tok_ln2)], inc="V")
                        p.drain("vector")
                        mb_s, mg_s = "MB%d" % w, "MG%d" % w
                        p.emit("sync", lambda e, w=w: e.dma_start(
                            out=mscr[0:1, w * 1024:(w + 1) * 1024], in_=mrow),
                            waits=[("V", tok_sr), (mg_s, p.cnt[mg_s])],
                            inc=mb_s, inc_n=16)
                        p.emit("gpsimd", lambda e, w=w: e.dma_start(
                            out=mrep[:, w * 1024:(w + 1) * 1024],
                            in_=mscr[0:1, w * 1024:(w + 1) * 1024
                                     ].broadcast_to([128, 1024])),
                            waits=[(mb_s, p.cnt[mb_s]), ("V", p.cnt["V"])],
                            inc=mg_s, inc_n=16)
                        tok_mg[w] = (mg_s, p.cnt[mg_s])
            tok_av = p.cnt["T"]
            p.emit("vector", lambda e: e.reciprocal(lrec[:, 0:16], PSD[:, 1024:1040]),
                   waits=[("T", tok_av)])
            p.drain("vector")
            lb = lrec[:, 0:16].rearrange("p q -> p q ()").broadcast_to([128, 16, 64])
            o3 = PSD[:, 0:1024].rearrange("p (q f) -> p q f", f=64)
            d3 = opair.rearrange("p (q f) -> p q f", f=128)[:, :, r0:r0 + 64]
            tok_oev = p.emit("vector", lambda e, o3=o3, d3=d3, lb=lb: e.tensor_tensor(
                d3, o3, lb, OP.mult), inc="V")
            p.drain("vector")
            if h % 2 == 1:
                c = h // 2
                emit_transposes(
                    opair, 16,
                    lambda j, u, c=c: OFM[:, c * 2048 + j * 128: c * 2048 + (j + u) * 128
                                          ].rearrange("p (k c2) -> p k c2", c2=128),
                    waits=[("V", tok_oev)])
                p.emit("vector", None, waits=[("T", p.cnt["T"])])
            p.emit("tensor", None, waits=[("V", tok_oev)])

    def proj_phase(s):
        p.emit("gpsimd", lambda e: e.dma_start(out=bprow2, in_=bproj),
               waits=[("T", p.cnt["T"]), ("S", p.cnt["S"])], inc="LP", inc_n=16)
        for k in range(NKC):
            p.emit("gpsimd", lambda e, k=k: e.dma_start(
                out=WPROJT[:, k * 1024:(k + 1) * 1024], in_=sWpT[k]),
                inc="LP", inc_n=16)
        p.emit("tensor", None, waits=[("LP", p.cnt["LP"]), ("S", p.cnt["S"]),
                                      ("LGC", 32)])
        for c in range(NKC):
            for hs in range(2):
                p.emit("tensor", lambda e, c=c, hs=hs: e.matmul(
                    PSD[0:1, hs * 512:(hs + 1) * 512], bvcol[:, c:c + 1],
                    WPROJT[:, c * 1024 + hs * 512: c * 1024 + (hs + 1) * 512],
                    start=(c == 0), stop=(c == NKC - 1)),
                    inc="T" if (c == NKC - 1 and hs == 1) else None)
        tok_pb = p.cnt["T"]
        tok_pbe = p.emit("scalar", lambda e: e.copy(pbias, PSD[0:1, 0:1024]),
                         waits=[("T", tok_pb)], inc="S")
        p.drain("scalar")
        tok_cb = p.emit("vector", lambda e: e.tensor_tensor(
            cbias, pbias, bprow2, OP.add), waits=[("S", tok_pbe)], inc="V")
        p.drain("vector")
        tok_res = {0: ("V", p.cnt["V"]), 1: ("V", p.cnt["V"])}
        for t in range(NT):
            pb = PS[:, 0:1024] if t % 2 == 0 else PS[:, 1024:2048]
            first_w = [tok_res[t % 2], ("V", tok_cb)]
            for c in range(NKC):
                for hs in range(2):
                    p.emit("tensor", lambda e, c=c, hs=hs, t=t, pb=pb: e.matmul(
                        pb[:, hs * 512:(hs + 1) * 512],
                        OFM[:, c * 2048 + t * 128: c * 2048 + (t + 1) * 128],
                        WPROJT[:, c * 1024 + hs * 512: c * 1024 + (hs + 1) * 512],
                        start=(c == 0), stop=False),
                        waits=first_w if (c == 0 and hs == 0) else [])
            for hs in range(2):
                p.emit("tensor", lambda e, hs=hs, pb=pb: e.matmul(
                    pb[:, hs * 512:(hs + 1) * 512], onesr,
                    cbias[0:1, hs * 512:(hs + 1) * 512], start=False, stop=True),
                    inc="T" if hs == 1 else None)
            tok_p = p.cnt["T"]
            xt = xres[:, t * D:(t + 1) * D]
            tok_r = p.emit("vector", lambda e, xt=xt, pb=pb: e.tensor_tensor(
                xt, pb[:, 0:1024], xt, OP.add), waits=[("T", tok_p)], inc="V")
            p.drain("vector")
            tok_res[t % 2] = ("V", tok_r)
            xl2 = htile[:, 0:D]
            tok_ln = emit_ln(xt, xl2)
            p.emit("vector", None, waits=[("T", p.cnt["T"])])
            tok_q = emit_quant(p, tn_f0, xl2, D, xln, after=[("V", tok_ln)])
            emit_transposes(
                xln, NKC,
                lambda j, u, t=t: ub.rearrange("p (k c) -> p k c", c=2048)
                [:, j:j + u, t * 128:(t + 1) * 128],
                waits=[("V", tok_q)])

    def ffn_phase(s):
        regions = [(PS[:, 0:512], PS[:, 512:1024]),
                   (PS[:, 1024:1536], PS[:, 1536:2048])]
        reg_free = {0: ("V", 0), 1: ("V", 0)}
        hqv2 = [qkt[:, 22528:23936], qkt[:, 23936:25344]]
        pend_td = [None]
        last_res = [("V", 0)]
        prev_q = [("V", 0)]
        gs = 0
        for hf in range(2):
            if hf == 1 and pend_td[0] is not None:
                pend_td[0]()
                pend_td[0] = None
            wl0 = [("T", p.cnt["T"]), ("S", p.cnt["S"])] if hf == 1 else []
            for k in range(NKC):
                p.emit("sync", lambda e, k=k, hf=hf: e.dma_start(
                    out=WGH[:, k * HALF:(k + 1) * HALF],
                    in_=sWg[k, 0:128, hf * HALF:(hf + 1) * HALF]),
                    waits=wl0 if k == 0 else [], inc="LF", inc_n=16)
                p.emit("sync", lambda e, k=k, hf=hf: e.dma_start(
                    out=WUH[:, k * HALF:(k + 1) * HALF],
                    in_=sWu[k, 0:128, hf * HALF:(hf + 1) * HALF]), inc="LF", inc_n=16)
            for j in range(NHC):
                p.emit("sync", lambda e, j=j, hf=hf: e.dma_start(
                    out=WDH[:, j * 1024:(j + 1) * 1024],
                    in_=sWd[hf * NHC + j, 0:128, 0:1024]), inc="LF", inc_n=16)
            tok_w = p.tok("LF")
            for t in range(NT):
                subs = [(0, 512), (512, 1024), (1024, HALF)]
                for a, bq in subs:
                    sw = bq - a
                    greg, ureg = regions[gs % 2]
                    first_w = [tok_w, reg_free[gs % 2]]
                    for which in range(2):
                        WH = WGH if which == 0 else WUH
                        dst = greg if which == 0 else ureg
                        for k in range(NKC):
                            p.emit("tensor", lambda e, k=k, a=a, bq=bq, WH=WH,
                                   dst=dst, t=t, sw=sw: e.matmul(
                                dst[:, 0:sw],
                                ub[:, k * 2048 + t * 128: k * 2048 + (t + 1) * 128],
                                WH[:, k * HALF + a: k * HALF + bq],
                                start=(k == 0), stop=(k == NKC - 1)),
                                waits=first_w if (which == 0 and k == 0) else [],
                                inc="T" if (which == 1 and k == NKC - 1) else None)
                    tok_gu = p.cnt["T"]
                    ts1 = p.emit("scalar", lambda e, a=a, sw=sw, greg=greg:
                                 e.activation(htile[:, a:a + sw], greg[:, 0:sw],
                                              AF.Silu),
                                 waits=[("T", tok_gu), prev_q[0]], inc="S")
                    p.drain("scalar")
                    tok_m = p.emit("vector", lambda e, a=a, sw=sw, ureg=ureg:
                                   e.tensor_tensor(
                        htile[:, a:a + sw], htile[:, a:a + sw], ureg[:, 0:sw],
                        OP.mult), waits=[("S", ts1)], inc="V")
                    p.drain("vector")
                    reg_free[gs % 2] = ("V", tok_m)
                    gs += 1
                if pend_td[0] is not None:
                    pend_td[0]()
                hq = hqv2[t % 2]
                tok_q = emit_quant(p, tn, htile[:, 0:HALF], HALF, hq[:, 0:HALF],
                                   out_wait=[("T", p.cnt["T"])])
                prev_q[0] = ("V", tok_q)

                def td(t=t, hf=hf, tok_q=tok_q, hq=hq):
                    tok_tr = emit_transposes(
                        hq[:, 0:HALF], NHC,
                        lambda j, u: hstage[:, j * 128:(j + u) * 128
                                            ].rearrange("p (k c) -> p k c", c=128),
                        waits=[("V", tok_q)])
                    for j in range(NHC):
                        for ds in range(2):
                            p.emit("tensor", lambda e, j=j, ds=ds: e.matmul(
                                PSD[:, ds * 512:(ds + 1) * 512],
                                hstage[:, j * 128:(j + 1) * 128],
                                WDH[:, j * 1024 + ds * 512: j * 1024 + (ds + 1) * 512],
                                start=(j == 0), stop=(j == NHC - 1)),
                                waits=([("S", tok_tr), last_res[0]]
                                       if (j == 0 and ds == 0) else []),
                                inc="T" if (j == NHC - 1 and ds == 1) else None)
                    tok_d = p.cnt["T"]
                    xt = xres[:, t * D:(t + 1) * D]
                    if hf == 0:
                        tok_o = p.emit("vector", lambda e, xt=xt: e.tensor_tensor(
                            xt, PSD[:, 0:1024], xt, OP.add),
                            waits=[("T", tok_d)], inc="V")
                        p.drain("vector")
                    else:
                        tok_o = p.emit("vector", lambda e, xt=xt: e.tensor_tensor(
                            outt, PSD[:, 0:1024], xt, OP.add),
                            waits=[("T", tok_d), ("ST", p.cnt["ST"])], inc="V")
                        p.drain("vector")
                        p.emit("sync", lambda e, t=t: e.dma_start(
                            out=out_h[s * T + t * 128: s * T + (t + 1) * 128, :],
                            in_=outt), waits=[("V", tok_o)], inc="ST", inc_n=16)
                    last_res[0] = ("V", tok_o)
                pend_td[0] = td
        if pend_td[0] is not None:
            pend_td[0]()
        p.barrier("ffn-end")

    for s in range(NSEQ if limit is None else 1):
        xl = xload0 if s == 0 else emit_xload(s)
        seq_phase(s, xl)
        if limit == "qkv":
            p.barrier("limit")
            break
        attn_phase(s)
        p.barrier("attn-end")
        if limit == "attn":
            break
        proj_phase(s)
        p.barrier("proj-end")
        if limit == "proj":
            break
        ffn_phase(s)
        p.barrier("seq-end")

    p.ops["sync"].append(([("ST", p.cnt["ST"])], None, None, 1))
    p.ops["gpsimd"].append(([("ST", p.cnt["ST"])], None, None, 1))

    with nc.Block() as block:
        p.build(block)
    es.close()
    return nc


_NC_CACHE = [None]


def kernel(L0_attn_x, L0_attn_Wqkv, L0_attn_bqkv, L0_attn_Wproj, L0_attn_bproj,
           L0_ffn_w_gate, L0_ffn_w_up, L0_ffn_w_down):
    x = np.ascontiguousarray(np.asarray(L0_attn_x, dtype=np.float32))
    Wqkv = np.ascontiguousarray(np.asarray(L0_attn_Wqkv, dtype=np.float32))
    bqkv = np.asarray(L0_attn_bqkv, dtype=np.float32)
    Wqkv_p, bqkv_p = _permute_qkv(Wqkv, bqkv)
    consts = _host_consts()
    if _NC_CACHE[0] is None:
        _NC_CACHE[0] = build_program()
    nc = _NC_CACHE[0]
    in_maps = []
    for c in range(NCORES):
        in_maps.append({
            "x": np.ascontiguousarray(x[c * NSEQ:(c + 1) * NSEQ].reshape(NSEQ * T, D)),
            "wqkv": np.ascontiguousarray(Wqkv_p),
            "bqkv": np.ascontiguousarray(bqkv_p[0:2 * D].reshape(1, 2 * D)),
            "bvcolh": np.ascontiguousarray(bqkv_p[2 * D:].reshape(8, 128).T),
            "wproj": np.ascontiguousarray(np.asarray(L0_attn_Wproj, np.float32)),
            "bproj": np.ascontiguousarray(np.asarray(L0_attn_bproj, np.float32).reshape(1, D)),
            "wgate": np.ascontiguousarray(np.asarray(L0_ffn_w_gate, np.float32)),
            "wup": np.ascontiguousarray(np.asarray(L0_ffn_w_up, np.float32)),
            "wdown": np.ascontiguousarray(np.asarray(L0_ffn_w_down, np.float32)),
            "ropeA": consts["ropeA"], "ropeB": consts["ropeB"],
            "ident": consts["ident"], "mask": consts["mask"],
            "ones_row": consts["ones_row"], "ones_col": consts["ones_col"],
            "ones_pair": consts["ones_pair"],
        })
    res = run_bass_kernel_spmd(nc, in_maps, core_ids=list(range(NCORES)))
    shards = [np.asarray(res.results[c]["out"]) for c in range(NCORES)]
    out = np.empty((B, T, D), shards[0].dtype)
    for c in range(NCORES):
        out[c * NSEQ:(c + 1) * NSEQ] = shards[c].reshape(NSEQ, T, D)
    return out


# version 18
# speedup vs baseline: 1.0940x; 1.0384x over previous
import math
import numpy as np
import ml_dtypes
import concourse.bass as bass
import concourse.mybir as mybir
from concourse.bass_utils import run_bass_kernel_spmd
from contextlib import ExitStack

F32 = mybir.dt.float32
BF16 = mybir.dt.bfloat16
AF = mybir.ActivationFunctionType
OP = mybir.AluOpType
AX = mybir.AxisListType

NCORES = 8
B = 16
T = 2048
D = 1024
H = 16
DH = 64
DFF = 2816
NSEQ = B // NCORES
NT = T // 128
NKC = D // 128
NDC = DFF // 128
HALF = DFF // 2
NHC = HALF // 128

MASK_NEG = -30000.0

_DEBUG_TAPS = []
_DEBUG = False


class Prog:
    ENGS = ("sync", "gpsimd", "tensor", "vector", "scalar")

    def __init__(self, nc, es):
        self.nc = nc
        self.es = es
        self.sems = {}
        self.cnt = {}
        for name in ("T", "V", "S"):
            self.sem(name)
        self.ops = {e: [] for e in self.ENGS}
        self.observed = {e: {} for e in self.ENGS}

    def sem(self, name):
        if name not in self.sems:
            self.sems[name] = self.es.enter_context(self.nc.semaphore(name))
            self.cnt[name] = 0
        return name

    def emit(self, eng, fn, waits=(), inc=None, inc_n=1):
        wl = []
        for s, v in waits:
            if self.observed[eng].get(s, 0) < v:
                wl.append((s, v))
                self.observed[eng][s] = v
        tok = None
        if inc is not None:
            self.cnt[inc] += inc_n
            tok = self.cnt[inc]
        self.ops[eng].append((wl, fn, inc, inc_n))
        return tok

    def drain(self, eng):
        self.ops[eng].append(([], lambda e: e.drain(), None, 1))

    def tok(self, sem):
        return (sem, self.cnt[sem])

    def barrier(self, tag=""):
        toks = [(s, v) for s, v in self.cnt.items() if v > 0]
        for e in self.ENGS:
            for s, v in toks:
                if self.observed[e].get(s, 0) < v:
                    self.ops[e].append(([(s, v)], None, None, 1))
                    self.observed[e][s] = v

    def build(self, block):
        sems = self.sems

        def mk(eng_name):
            def body(eng):
                for wl, fn, inc, inc_n in self.ops[eng_name]:
                    for s, v in wl:
                        eng.wait_ge(sems[s], v)
                    if fn is None:
                        continue
                    r = fn(eng)
                    if inc is not None:
                        r.then_inc(sems[inc], inc_n)
            return body

        block.sync(mk("sync"))
        block.gpsimd(mk("gpsimd"))
        block.tensor(mk("tensor"))
        block.vector(mk("vector"))
        block.scalar(mk("scalar"))


def _host_consts():
    import jax
    import jax.numpy as jnp

    def _rope_tensor():
        pos = jnp.arange(T, dtype=jnp.float32)
        angles = pos[:, None] / 10000.0 ** (jnp.arange(0, DH, 2, dtype=jnp.float32) / DH)
        return jnp.stack([jnp.cos(angles), jnp.sin(angles)], axis=-1).reshape(T, DH)

    with jax.default_device(jax.devices("cpu")[0]):
        A = np.asarray(jax.jit(_rope_tensor)(), dtype=np.float32)
    Bm = np.float32(1.0) - A
    Bs = Bm.copy()
    Bs[:, 0::2] *= -1.0
    At = np.zeros((128, NT * DH), np.float32)
    Bt = np.zeros((128, NT * DH), np.float32)
    for t in range(NT):
        At[:, t * DH:(t + 1) * DH] = A[t * 128:(t + 1) * 128]
        Bt[:, t * DH:(t + 1) * DH] = Bs[t * 128:(t + 1) * 128]
    ident = np.eye(128).astype(ml_dtypes.bfloat16)
    mask = np.where(np.arange(128)[None, :] >= np.arange(128)[:, None],
                    0.0, MASK_NEG).astype(np.float32)
    return {
        "ropeA": At,
        "ropeB": Bt,
        "ident": ident,
        "identf": np.eye(128).astype(np.float32),
        "mask": mask,
        "ones_row": np.ones((1, 128), ml_dtypes.bfloat16),
        "ones_pair": (lambda z: (z.__setitem__((0, slice(None)), 1.0),
                                 z.__setitem__((64, slice(None)), 1.0), z)[-1])(
            np.zeros((65, 128), ml_dtypes.bfloat16)),
        "ones_row_f32": np.ones((1, 128), np.float32),
        "ones_col": np.ones((128, 1), ml_dtypes.bfloat16),
    }


def _permute_qkv(Wqkv, bqkv):
    idx = np.zeros(3 * D, np.int64)
    for h in range(H):
        base = h * 3 * DH
        for c in range(DH):
            idx[h * DH + c] = base + c
            idx[D + h * DH + c] = base + DH + c
            idx[2 * D + h * DH + c] = base + 2 * DH + c
    return Wqkv[idx], bqkv[idx]


def emit_quant_head(p, tn, src, F, after=(), sg_wait=None, sq_wait=(),
                    vbuf=None):
    nb = F // 16
    g = lambda a: a.rearrange("p (g k) -> p g k", k=16)
    v_ = p.emit
    mb = tn["mblk"][:, 0:nb]
    rb = tn["rblk"][:, 0:nb]
    r6 = tn["r6sq"][:, 0:nb]
    s2 = tn["s2b"][:, 0:nb]
    sg = tn["sgq"][:, 0:F]
    vt = src if vbuf is None else vbuf[:, 0:F]
    aft = list(after)
    if sg_wait is None:
        sg_wait = [("V", p.cnt["V"])]
    tok_am = v_("vector", lambda e: e.tensor_reduce(mb, g(src), AX.X, OP.max,
                                                    apply_absolute_value=True),
                waits=aft, inc="V")
    p.drain("vector")
    v_("scalar", lambda e: e.sign(sg, src), waits=aft + list(sg_wait))
    p.drain("scalar")
    v_("vector", lambda e: e.reciprocal(rb, mb))
    p.drain("vector")
    v_("vector", lambda e: e.tensor_tensor(r6, rb, rb, OP.mult))
    v_("vector", lambda e: e.tensor_scalar(s2, mb, 1.0 / 12.0, 5.0e-13, OP.mult, OP.max))
    p.drain("vector")
    v_("vector", lambda e: e.tensor_scalar(rb, r6, 36.0, 9.0e24, OP.mult, OP.min))
    p.drain("vector")
    tok_sq = v_("scalar", lambda e: e.activation(vt, src, AF.Square),
                waits=[("V", tok_am)] + list(sq_wait), inc="S")
    p.drain("scalar")
    r6b = rb.rearrange("p g -> p g ()").broadcast_to([128, nb, 16])
    v_("vector", lambda e: e.tensor_tensor(g(vt), g(vt), r6b, OP.mult),
       waits=[("S", tok_sq)])
    p.drain("vector")
    return vt, tok_am


def emit_quant_tail(p, tn, vt, F, out_bf, out_wait=()):
    nb = F // 16
    g = lambda a: a.rearrange("p (g k) -> p g k", k=16)
    v_ = p.emit
    s2 = tn["s2b"][:, 0:nb]
    sg = tn["sgq"][:, 0:F]
    aA = tn["accqA"][:, 0:F]
    aB = tn["accqB"][:, 0:F]
    v_("vector", lambda e: e.tensor_scalar(aA, vt, 25.0, 2.0, OP.is_gt, OP.mult))
    p.drain("vector")
    v_("vector", lambda e: e.scalar_tensor_tensor(aB, vt, 12.25, aA, OP.is_gt, OP.add))
    p.drain("vector")
    v_("vector", lambda e: e.scalar_tensor_tensor(aA, vt, 6.25, aB, OP.is_gt, OP.add))
    p.drain("vector")
    v_("vector", lambda e: e.tensor_scalar(aB, aA, 2.0, None, OP.mult))
    p.drain("vector")
    v_("vector", lambda e: e.scalar_tensor_tensor(aA, vt, 3.0625, aB, OP.is_gt, OP.add))
    p.drain("vector")
    v_("vector", lambda e: e.scalar_tensor_tensor(aB, vt, 1.5625, aA, OP.is_gt, OP.add))
    p.drain("vector")
    v_("vector", lambda e: e.scalar_tensor_tensor(aA, vt, 0.5625, aB, OP.is_gt, OP.add))
    p.drain("vector")
    v_("vector", lambda e: e.scalar_tensor_tensor(aB, vt, 0.0625, aA, OP.is_gt, OP.add))
    p.drain("vector")
    s2b = s2.rearrange("p g -> p g ()").broadcast_to([128, nb, 16])
    v_("vector", lambda e: e.tensor_tensor(g(aA), g(aB), s2b, OP.mult))
    p.drain("vector")
    tok = v_("vector", lambda e: e.tensor_tensor(out_bf, aA, sg, OP.mult),
             waits=list(out_wait), inc="V")
    p.drain("vector")
    return tok


def emit_quant(p, tn, src, F, out_bf, after=(), out_wait=(), sg_wait=None):
    vt, _ = emit_quant_head(p, tn, src, F, after=after, sg_wait=sg_wait)
    return emit_quant_tail(p, tn, vt, F, out_bf, out_wait=out_wait)


def build_program(limit=None):
    nc = bass.Bass("TRN2", target_bir_lowering=False)
    es = ExitStack()

    xh = nc.dram_tensor("x", [NSEQ * T, D], F32, kind="ExternalInput").ap()
    wqkv = nc.dram_tensor("wqkv", [3 * D, D], F32, kind="ExternalInput").ap()
    bqkv = nc.dram_tensor("bqkv", [1, 2048], F32, kind="ExternalInput").ap()
    bvcolh = nc.dram_tensor("bvcolh", [128, 8], F32, kind="ExternalInput").ap()
    wproj = nc.dram_tensor("wproj", [D, D], F32, kind="ExternalInput").ap()
    bproj = nc.dram_tensor("bproj", [1, D], F32, kind="ExternalInput").ap()
    wgate = nc.dram_tensor("wgate", [DFF, D], F32, kind="ExternalInput").ap()
    wup = nc.dram_tensor("wup", [DFF, D], F32, kind="ExternalInput").ap()
    wdown = nc.dram_tensor("wdown", [D, DFF], F32, kind="ExternalInput").ap()
    c_ropeA = nc.dram_tensor("ropeA", [128, NT * DH], F32, kind="ExternalInput").ap()
    c_ropeB = nc.dram_tensor("ropeB", [128, NT * DH], F32, kind="ExternalInput").ap()
    c_ident = nc.dram_tensor("ident", [128, 128], BF16, kind="ExternalInput").ap()
    c_mask = nc.dram_tensor("mask", [128, 128], F32, kind="ExternalInput").ap()
    c_ones_row = nc.dram_tensor("ones_row", [1, 128], BF16, kind="ExternalInput").ap()
    c_ones_col = nc.dram_tensor("ones_col", [128, 1], BF16, kind="ExternalInput").ap()
    c_ones_pair = nc.dram_tensor("ones_pair", [65, 128], BF16, kind="ExternalInput").ap()
    out_h = nc.dram_tensor("out", [NSEQ * T, D], F32, kind="ExternalOutput").ap()
    sWg = nc.dram_tensor("sWg", [NKC, 128, DFF], BF16, kind="Internal").ap()
    sWu = nc.dram_tensor("sWu", [NKC, 128, DFF], BF16, kind="Internal").ap()
    sWd = nc.dram_tensor("sWd", [NDC, 128, D], BF16, kind="Internal").ap()
    mscr = nc.dram_tensor("mscr", [1, 2048], BF16, kind="Internal").ap()
    sWqT = nc.dram_tensor("sWqT", [NKC, 128, 3072], BF16, kind="Internal").ap()
    sWpT = nc.dram_tensor("sWpT", [NKC, 128, 1024], BF16, kind="Internal").ap()

    dbg = {}
    if _DEBUG:
        for name, shape, dt in _DEBUG_TAPS:
            dbg[name] = nc.dram_tensor(name, shape, dt, kind="ExternalOutput").ap()

    sb = lambda name, shape, dt: es.enter_context(nc.sbuf_tensor(name, shape, dt))[:]
    ropebuf = sb("ropebuf", [128, 256], F32)
    ident = sb("ident_s", [128, 128], BF16)
    maskt = sb("mask_s", [128, 128], F32)
    onesr = sb("onesr", [1, 128], BF16)
    onesc = sb("onesc", [128, 1], BF16)
    onespair = sb("onespair", [65, 128], BF16)
    brow = sb("brow", [1, 2048], BF16)
    bvcol = sb("bvcol", [128, 8], BF16)
    xres = sb("xres", [128, NT * D], BF16)
    qkt = sb("qkt", [128, 32 * 1024], BF16)
    ub = sb("ub", [128, 16 * 1024], BF16)
    wb = sb("wb", [128, 24576], BF16)
    vtile = sb("vtile", [128, HALF], F32)
    htile = sb("htile", [128, HALF], F32)
    kropet = sb("kropet", [128, D], BF16)
    qropet = sb("qropet", [128, D], BF16)
    accA = sb("accA", [128, D], BF16)
    accB = sb("accB", [128, D], BF16)
    stats = sb("stats", [128, 8], F32)
    mblk = sb("mblk", [128, HALF // 16], F32)
    rblk = sb("rblk", [128, HALF // 16], F32)
    r6sq = sb("r6sq", [128, HALF // 16], F32)
    s2b = sb("s2b", [128, HALF // 16], F32)
    qn2 = sb("qn2", [128, H], F32)
    kn2 = sb("kn2", [128, H], F32)
    nrm = sb("nrm", [128, H], F32)
    negBs = sb("negBs", [128, H], F32)
    lrec = sb("lrec", [128, H], F32)
    nrow = sb("nrow", [1, 16], BF16)
    xln = accB
    xfst = accA
    outt = vtile[:, 0:1024]
    PS = es.enter_context(nc.psum_tensor("PS", [128, 2048], F32))[:]
    PSD = es.enter_context(nc.psum_tensor("PSD", [128, 1088], F32))[:]
    PST = es.enter_context(nc.psum_tensor("PST", [128, 512], BF16))[:]

    p = Prog(nc, es)
    for sname in ("LC", "LGC", "LW0", "LW1", "LG0", "LG1", "LB0", "LB1",
                  "SB0", "SB1", "LQ",
                  "LX0", "LX1", "LX2", "LX3", "MB0", "MB1", "MG0", "MG1",
                  "RP0", "RP1", "LP", "LF", "ST"):
        p.sem(sname)


    WQKVT = wb[:, 0:24576]
    OFM = wb[:, 0:16384]
    PT0 = wb[:, 16384:17408]
    PT1 = wb[:, 17408:18432]
    opair = wb[:, 18432:20480]
    WDH = wb[:, 0:NHC * 1024]
    accqA = wb[:, 11264:12672]
    accqB = wb[:, 12672:14080]
    sgq = wb[:, 14080:15488]
    hstage = wb[:, 15488:16896]
    WGH = qkt[:, 0:NKC * HALF]
    WUH = qkt[:, NKC * HALF:2 * NKC * HALF]
    hqv = qkt[:, 22528:23936]
    WPROJT = qkt[:, 0:8192]
    bprow2 = qkt[0:1, 8192:9216]
    pbias = qkt[0:1, 9216:10240]
    cbias = qkt[0:1, 10240:11264]

    tn = dict(mblk=mblk, rblk=rblk, r6sq=r6sq, s2b=s2b,
              sgq=sgq, accqA=accqA, accqB=accqB)
    tn_f0 = dict(mblk=mblk, rblk=rblk, r6sq=r6sq, s2b=s2b,
                 accqA=qkt[:, 25344:26368], accqB=qkt[:, 26368:27392],
                 sgq=qkt[:, 27392:28416])

    def qk_slice(h, which, c0, c1):
        r0 = (h % 2) * 64
        base = (h // 2) * 4096 + which * 2048
        return qkt[r0:r0 + 64, base + c0:base + c1]

    p.emit("sync", lambda e: e.dma_start(out=ident, in_=c_ident), inc="LC", inc_n=16)
    p.emit("sync", lambda e: e.dma_start(out=maskt, in_=c_mask), inc="LC", inc_n=16)
    p.emit("sync", lambda e: e.dma_start(out=onesr, in_=c_ones_row), inc="LC", inc_n=16)
    p.emit("sync", lambda e: e.dma_start(out=onesc, in_=c_ones_col), inc="LC", inc_n=16)
    p.emit("sync", lambda e: e.dma_start(out=onespair, in_=c_ones_pair), inc="LC", inc_n=16)
    p.emit("gpsimd", lambda e: e.dma_start(out=brow, in_=bqkv), inc="LGC", inc_n=16)
    p.emit("gpsimd", lambda e: e.dma_start(out=bvcol, in_=bvcolh), inc="LGC", inc_n=16)
    consts_tok = [p.tok("LC"), p.tok("LGC")]

    def emit_transposes(src_bf, n, dst_fn, waits, pair_mode=False):
        ntr = n
        j = 0
        while j < ntr:
            jn = min(4, ntr - j)
            for i in range(jn):
                if pair_mode:
                    hh = j + i
                    r0 = (hh % 2) * 64
                    slot = i // 2
                    fn = (lambda e, hh=hh, slot=slot, r0=r0: e.transpose(
                        PST[r0:r0 + 64, slot * 128:(slot + 1) * 128],
                        src_bf[:, hh * 64:(hh + 1) * 64], ident,
                        tile_position=(0, r0)))
                else:
                    fn = (lambda e, jj=j + i, i=i: e.transpose(
                        PST[:, i * 128:(i + 1) * 128],
                        src_bf[:, jj * 128:(jj + 1) * 128], ident))
                p.emit("tensor", fn, waits=waits if (j == 0 and i == 0) else [],
                       inc="T" if i == jn - 1 else None)
            tok_t = p.cnt["T"]
            nu = (jn // 2 if pair_mode else jn)
            tok_e = p.emit("scalar", lambda e, j=j, nu=nu: e.copy(
                dst_fn(j, nu),
                PST[:, 0:nu * 128].rearrange("p (k c) -> p k c", c=128)),
                waits=[("T", tok_t)], inc="S")
            p.drain("scalar")
            p.emit("tensor", None, waits=[("S", tok_e)])
            j += jn
        return p.cnt["S"]


    def emit_xload(s):
        toks = {}
        for g4 in range(4):
            sem = "LX%d" % g4
            for t in range(g4 * 4, g4 * 4 + 4):
                p.emit("gpsimd", lambda e, t=t, s=s: e.dma_start(
                    out=xres[:, t * D:(t + 1) * D],
                    in_=xh[s * T + t * 128: s * T + (t + 1) * 128, :]),
                    inc=sem, inc_n=16)
            toks[g4] = (sem, p.cnt[sem])
        return toks


    def emit_wprep():
        jobs = []
        for (s_ap, nrb, K, dst, base) in ((wgate, NDC, D, sWg, 0),
                                          (wup, NDC, D, sWu, 0),
                                          (wdown, NKC, DFF, sWd, 0)):
            nh = K // HALF if K > HALF else 1
            for r in range(nrb):
                for half in range(nh):
                    jobs.append((s_ap, r, half, min(HALF, K), dst, base, True))
        bjobs = []
        for r in range(3 * D // 128):
            bjobs.append((wqkv, r, 0, D, sWqT, 0, False))
        for r in range(NKC):
            bjobs.append((wproj, r, 0, D, sWpT, 0, False))
        out = []
        bi = 0
        for i, j in enumerate(jobs):
            out.append(j)
            if i % 2 == 1 and bi < len(bjobs):
                out.append(bjobs[bi]); bi += 1
        out.extend(bjobs[bi:])
        jobs = out
        srcs = [htile, vtile]
        hqs = [qkt[:, 22528:23936], qkt[:, 23936:25344]]
        hss = [wb[:, 15488:16896], wb[:, 16896:18304]]
        tns = [dict(tn), dict(tn)]
        tns[1]["sgq"] = wb[:, 18304:19712]
        lws = ["LW0", "LW1"]
        lgs = ["LG0", "LG1"]
        rel_src = {0: ("V", 0), 1: ("V", 0)}
        rel_hq = {0: ("T", 0), 1: ("T", 0)}
        rel_hs = {0: ("LG0", 0), 1: ("LG1", 0)}
        lbs = ["LB0", "LB1"]
        sbs = ["SB0", "SB1"]
        hqBs = [qkt[:, 25344:26368], qkt[:, 26368:27392]]
        hsBs = [wb[:, 19712:20736], wb[:, 20736:21760]]
        rel_hqB = {0: ("T", 0), 1: ("T", 0)}
        rel_hsB = {0: ("SB0", 0), 1: ("SB1", 0)}
        bcnt = [0]
        pend_fin = [None]

        def finisher(hq, hs, nck, half, r, dst, base, trw, par, do_q, bpar):
            def fin():
                emit_transposes(hq, nck,
                                lambda j, u, hs=hs: hs[:, j * 128:(j + u) * 128
                                                       ].rearrange("p (k c) -> p k c", c=128),
                                waits=trw)
                if do_q:
                    rel_hq[par] = ("T", p.cnt["T"])
                else:
                    rel_hqB[bpar] = ("T", p.cnt["T"])
                tok_ev = p.cnt["S"]
                for j in range(nck):
                    cglob = half * (HALF // 128) + j
                    p.emit("gpsimd", lambda e, j=j, cglob=cglob, r=r, dst=dst,
                           base=base, hs=hs: e.dma_start(
                        out=dst[base + cglob, 0:128, r * 128:(r + 1) * 128],
                        in_=hs[:, j * 128:(j + 1) * 128]),
                        waits=[("S", tok_ev)] if j == 0 else [],
                        inc=(lgs[par] if do_q else sbs[bpar]), inc_n=16)
                if do_q:
                    rel_hs[par] = (lgs[par], p.cnt[lgs[par]])
                else:
                    rel_hsB[bpar] = (sbs[bpar], p.cnt[sbs[bpar]])
            return fin

        for i, (s_ap, r, half, Fh, dst, base, do_q) in enumerate(jobs):
            par = i % 2
            bpar = bcnt[0] % 2
            srcb = srcs[par][:, 0:Fh]
            hq = (hqs[par] if do_q else hqBs[bpar])[:, 0:Fh]
            hs = hss[par] if do_q else hsBs[bpar]
            if do_q:
                tok_ld = p.emit("sync", lambda e, s_ap=s_ap, r=r, half=half,
                                Fh=Fh, srcb=srcb: e.dma_start(
                    out=srcb, in_=s_ap[r * 128:(r + 1) * 128,
                                       half * HALF:half * HALF + Fh]),
                    waits=[rel_src[par]], inc=lws[par], inc_n=16)
                vt, _ = emit_quant_head(p, tns[par], srcb, Fh,
                                        after=[(lws[par], tok_ld)],
                                        sg_wait=[rel_src[par]])
                if pend_fin[0] is not None:
                    pend_fin[0]()
                tok_q = emit_quant_tail(p, tns[par], vt, Fh, hq,
                                        out_wait=[rel_hq[par]])
                rel_src[par] = ("V", tok_q)
                trw = [("V", tok_q), rel_hs[par]]
            else:
                tok_ld = p.emit("gpsimd", lambda e, s_ap=s_ap, r=r, Fh=Fh,
                                hq=hq: e.dma_start(
                    out=hq, in_=s_ap[r * 128:(r + 1) * 128, 0:Fh]),
                    waits=[rel_hqB[bpar]], inc=lbs[bpar], inc_n=16)
                if pend_fin[0] is not None:
                    pend_fin[0]()
                trw = [(lbs[bpar], tok_ld), rel_hsB[bpar]]
                bcnt[0] += 1
            pend_fin[0] = finisher(hq, hs, Fh // 128, half, r, dst, base,
                                   trw, par, do_q, bpar)
        if pend_fin[0] is not None:
            pend_fin[0]()
            pend_fin[0] = None

    for s, v in consts_tok:
        p.emit("vector", None, waits=[(s, v)])
        p.emit("scalar", None, waits=[(s, v)])
        p.emit("tensor", None, waits=[(s, v)])
    xload0 = emit_xload(0)
    emit_wprep()
    p.barrier("after-wprep")

    def emit_ln(xt_ap, out_bf_ap, after=(), stats_wait=None, dump=None):
        if dump is None:
            dump = vtile[:, 0:D]
        if stats_wait is None:
            stats_wait = [("V", p.cnt["V"])]
        gate = list(after) + list(stats_wait)
        p.emit("scalar", lambda e: e.activation(dump, xt_ap, AF.Copy,
                                                accum_out=stats[:, 0:1]),
               waits=gate)
        p.drain("scalar")
        tok_s = p.emit("scalar", lambda e: e.activation(dump, xt_ap, AF.Square,
                                                        accum_out=stats[:, 1:2]),
                       inc="S")
        p.drain("scalar")
        p.emit("vector", lambda e: e.tensor_scalar(stats[:, 2:3], stats[:, 0:1],
                                                   -1.0 / D, None, OP.mult),
               waits=[("S", tok_s)])
        p.drain("vector")
        p.emit("vector", lambda e: e.tensor_tensor(stats[:, 3:4], stats[:, 2:3],
                                                   stats[:, 2:3], OP.mult))
        p.drain("vector")
        p.emit("vector", lambda e: e.scalar_tensor_tensor(stats[:, 4:5], stats[:, 1:2],
                                                          1.0 / D, stats[:, 3:4],
                                                          OP.mult, OP.subtract))
        p.drain("vector")
        tok_v = p.emit("vector", lambda e: e.tensor_scalar(stats[:, 5:6], stats[:, 4:5],
                                                           1e-5, None, OP.add), inc="V")
        p.drain("vector")
        p.emit("scalar", lambda e: e.activation(stats[:, 6:7], stats[:, 5:6], AF.Ln),
               waits=[("V", tok_v)])
        p.drain("scalar")
        tok_r = p.emit("scalar", lambda e: e.activation(stats[:, 7:8], stats[:, 6:7],
                                                        AF.Exp, scale=-0.5), inc="S")
        p.drain("scalar")
        tok = p.emit("vector", lambda e: e.tensor_scalar(out_bf_ap, xt_ap,
                                                         stats[:, 2:3], stats[:, 7:8],
                                                         OP.add, OP.mult),
                     waits=[("S", tok_r), ("S", p.cnt["S"])], inc="V")
        p.drain("vector")
        return tok

    def seq_phase(s, xload):
        for k in range(NKC):
            p.emit("sync", lambda e, k=k: e.dma_start(
                out=WQKVT[:, k * 3072:(k + 1) * 3072], in_=sWqT[k]),
                waits=([("T", p.cnt["T"]), ("S", p.cnt["S"]),
                        ("V", p.cnt["V"])] if k == 0 else []),
                inc="LQ", inc_n=16)
        tok_lq = ("LQ", p.cnt["LQ"])

        prev_ln = [None]
        pend_qk = [None]
        rp_tok = {}
        ln_tok = {}
        tr_tok = {}

        def emit_lntile(t):
            sl = (t % 2) * 128
            rps = "RP0" if t % 2 == 0 else "RP1"
            p.emit("sync", lambda e, t=t, sl=sl: e.dma_start(
                out=ropebuf[:, sl:sl + 64], in_=c_ropeA[:, t * DH:(t + 1) * DH]),
                waits=[("V", p.cnt["V"])], inc=rps, inc_n=16)
            rp_tok[t] = (rps, p.emit("sync", lambda e, t=t, sl=sl: e.dma_start(
                out=ropebuf[:, sl + 64:sl + 128], in_=c_ropeB[:, t * DH:(t + 1) * DH]),
                inc=rps, inc_n=16))
            xt = xres[:, t * D:(t + 1) * D]
            tok_ln = emit_ln(xt, xln, after=[xload[t // 4]],
                             stats_wait=prev_ln[0])
            prev_ln[0] = [("V", tok_ln)]
            ln_tok[t] = tok_ln
            tr_tok[t] = emit_transposes(
                xln, NKC, lambda j, jn: xfst[:, j * 128:(j + jn) * 128
                                             ].rearrange("p (k c) -> p k c", c=128),
                waits=[("V", tok_ln)])

        emit_lntile(0)
        for t in range(NT):
            sl = (t % 2) * 128
            def psn(nck):
                if nck < 4:
                    return PS[:, nck * 512:(nck + 1) * 512]
                return PSD[:, (nck - 4) * 512:(nck - 3) * 512]
            first_w = [("S", tr_tok[t]), ("V", p.cnt["V"]), ("S", p.cnt["S"]),
                       tok_lq]
            for k in range(NKC):
                for nck in range(6):
                    p.emit("tensor", lambda e, k=k, nck=nck: e.matmul(
                        psn(nck), xfst[:, k * 128:(k + 1) * 128],
                        WQKVT[:, k * 3072 + nck * 512: k * 3072 + (nck + 1) * 512],
                        start=(k == 0), stop=(k == NKC - 1 and nck >= 4)),
                        waits=first_w if (k == 0 and nck == 0) else [])
            if pend_qk[0] is not None:
                pend_qk[0]()
                pend_qk[0] = None
            for nck in range(4):
                for i4 in range(4):
                    r16 = nck * 4 + i4
                    p.emit("tensor", lambda e, nck=nck, i4=i4, r16=r16: e.matmul(
                        psn(nck)[:, i4 * 128:(i4 + 1) * 128], onesr,
                        brow[0:1, r16 * 128:(r16 + 1) * 128],
                        start=False, stop=(i4 == 3)),
                        inc="T" if (nck == 3 and i4 == 3) else None)
            tok_qkv = p.cnt["T"]
            if t + 1 < NT:
                emit_lntile(t + 1)
            rA = ropebuf[:, sl:sl + 64].rearrange(
                "p (r two) -> p () r two", two=2).broadcast_to([128, H, DH // 2, 2])
            rB = ropebuf[:, sl + 64:sl + 128].rearrange(
                "p (r two) -> p () r two", two=2).broadcast_to([128, H, DH // 2, 2])
            g4 = lambda a: a.rearrange("p (h r two) -> p h r two", h=H, two=2)
            psq = g4(PS[:, 0:1024])
            psk = g4(PS[:, 1024:2048])
            q4 = g4(qropet[:, 0:1024])
            k4 = g4(kropet[:, 0:1024])
            pd4 = g4(PSD[:, 0:1024])
            p.emit("vector", lambda e, k4=k4, psq=psq, rB=rB: e.tensor_tensor(
                k4, psq[:, :, :, ::-1], rB, OP.mult),
                waits=[("T", tok_qkv), rp_tok[t]])
            p.drain("vector")
            p.emit("vector", lambda e, q4=q4, psq=psq, rA=rA:
                   e.tensor_tensor(q4, psq, rA, OP.mult))
            p.drain("vector")
            p.emit("vector", lambda e, q4=q4, k4=k4:
                   e.tensor_tensor(q4, q4, k4, OP.add))
            p.drain("vector")
            tok_vev = p.emit("scalar", lambda e, t=t: e.copy(
                ub[:, t * D:(t + 1) * D], PSD[:, 0:1024]),
                waits=[("T", tok_qkv)], inc="S")
            p.drain("scalar")
            p.emit("vector", lambda e, pd4=pd4, psk=psk, rB=rB: e.tensor_tensor(
                pd4, psk[:, :, :, ::-1], rB, OP.mult),
                waits=[("S", tok_vev)])
            p.drain("vector")
            p.emit("vector", lambda e, k4=k4, psk=psk, rA=rA:
                   e.tensor_tensor(k4, psk, rA, OP.mult))
            p.drain("vector")
            tok_rope = p.emit("vector", lambda e, k4=k4, pd4=pd4:
                              e.tensor_tensor(k4, k4, pd4, OP.add), inc="V")
            p.drain("vector")
            def qk_tr(t=t, tok_rope=tok_rope):
                for which, src_bf in ((0, qropet), (1, kropet)):
                    def dstqk(j, u, which=which, t=t):
                        v = qkt.rearrange("p (pr c) -> p pr c", c=4096)
                        return v[:, j:j + u,
                                 which * 2048 + t * 128: which * 2048 + (t + 1) * 128]
                    emit_transposes(src_bf[:, 0:1024], NKC,
                                    lambda j, u, which=which, t=t: dstqk(j, u, which, t),
                                    waits=[("V", tok_rope)])
            pend_qk[0] = qk_tr
        if pend_qk[0] is not None:
            pend_qk[0]()
            pend_qk[0] = None
        p.barrier("p3-end")
        return p.cnt["S"]

    def attn_phase(s):
        mrep = wb[:, 20480:22528]
        mrow = wb[0:1, 22528:23552]
        lnrow = htile
        for h in range(H):
            r0 = (h % 2) * 64
            tp = (64, 0) if (h % 2) else None
            qbase = (h // 2) * 4096
            kbase = qbase + 2048
            tok_exp = {}
            tok_mg = {}
            pend = [None]
            for pss in range(2):
                for w in range(2):
                    for sc in range(8 * (w + 1)):
                        ping = PS[:, 0:1024] if sc % 2 == 0 else PS[:, 1024:2048]
                        ptb = PT0 if sc % 2 == 0 else PT1
                        qlo_g = max(w * 1024, sc * 128)
                        lloc = qlo_g - w * 1024
                        a = lloc
                        first = True
                        while a < 1024:
                            b2 = min(1024, (a // 512 + 1) * 512)
                            wl = [("S", tok_exp.get(sc % 2, 0)),
                                  ("V", p.cnt["V"])] if first else []
                            p.emit("tensor", lambda e, a=a, b2=b2, sc=sc, w=w,
                                   ping=ping, r0=r0, kbase=kbase, qbase=qbase,
                                   tp=tp: e.matmul(
                                ping[:, a:b2],
                                qkt[r0:r0 + 64, kbase + sc * 128: kbase + (sc + 1) * 128],
                                qkt[r0:r0 + 64, qbase + w * 1024 + a: qbase + w * 1024 + b2],
                                start=True, stop=True, tile_position=tp),
                                waits=wl,
                                inc="T" if b2 == 1024 else None)
                            first = False
                            a = b2
                        tok_sc = p.cnt["T"]
                        if sc >= 8 * w:
                            p.emit("vector", lambda e, ping=ping, lloc=lloc:
                                   e.tensor_tensor(
                                ping[:, lloc:lloc + 128], ping[:, lloc:lloc + 128],
                                maskt, OP.add), waits=[("T", tok_sc)], inc="V")
                            if pss == 1:
                                p.drain("vector")
                        if pss == 1:
                            p.emit("vector", lambda e, ping=ping, lloc=lloc, w=w:
                                   e.tensor_tensor(
                                ping[:, lloc:1024], ping[:, lloc:1024],
                                mrep[:, w * 1024 + lloc:(w + 1) * 1024], OP.add),
                                waits=[("T", tok_sc), tok_mg[w]], inc="V")
                        tok_m = p.cnt["V"]
                        scl = (1.0 / 64.0) if pss == 0 else 0.125
                        te = p.emit("scalar", lambda e, ping=ping, ptb=ptb,
                                    lloc=lloc, scl=scl: e.activation(
                                    ptb[:, lloc:1024], ping[:, lloc:1024],
                                    AF.Exp, scale=scl),
                                    waits=[("T", tok_sc), ("V", tok_m)], inc="S")
                        p.drain("scalar")
                        tok_exp[sc % 2] = te

                        def consume(sc=sc, w=w, ptb=ptb, te=te, lloc=lloc, pss=pss, h=h):
                            if pss == 0:
                                nsj = (1024 - lloc + 511) // 512
                                aa = lloc
                                for sj in range(nsj):
                                    b3 = min(1024, (aa // 512 + 1) * 512)
                                    is_b4 = aa < 512
                                    stp = (is_b4 and sc == 8 * w + 3) or \
                                          ((not is_b4) and sc == 8 * (w + 1) - 1)
                                    p.emit("tensor", lambda e, aa=aa, b3=b3, ptb=ptb,
                                           sc=sc, stp=stp: e.matmul(
                                        PSD[0:1, aa:b3], onesc,
                                        ptb[:, aa:b3],
                                        start=(sc == 0),
                                        stop=stp),
                                        waits=[("S", te)] if sj == 0 else [],
                                        inc="T" if sj == nsj - 1 else None)
                                    aa = b3
                            else:
                                for qtl in range(max(0, sc - 8 * w), 8):
                                    qtg = 8 * w + qtl
                                    st_ap = ptb[:, qtl * 128:(qtl + 1) * 128]
                                    p.emit("tensor", lambda e, st_ap=st_ap, qtg=qtg,
                                           sc=sc, w=w, qtl=qtl, h=h: e.matmul(
                                        PSD[:, qtg * 64:(qtg + 1) * 64], st_ap,
                                        ub[:, sc * D + h * 64: sc * D + (h + 1) * 64],
                                        start=(sc == 0 and qtg % 8 == 0),
                                        stop=(sc == 8 * (w + 1) - 1 and qtl == 7)),
                                        waits=[("S", te)] if qtl == max(0, sc - 8 * w) else [])
                                    p.emit("tensor", lambda e, st_ap=st_ap, qtg=qtg,
                                           sc=sc, w=w, qtl=qtl: e.matmul(
                                        PSD[:, 1024 + qtg:1025 + qtg], st_ap, onesc,
                                        start=(w == 0 and sc == 0 and qtl == 0),
                                        stop=(w == 1 and sc == 15 and qtl == 7)),
                                        inc="T" if qtl == 7 else None)
                        if pend[0] is not None:
                            pend[0]()
                        pend[0] = consume
                    if pend[0] is not None:
                        pend[0]()
                        pend[0] = None
                    if pss == 0:
                        tok_L = p.cnt["T"]
                        lr = lnrow[0:1, 0:1024]
                        tok_ln2 = p.emit("scalar", lambda e, lr=lr: e.activation(
                            lr, PSD[0:1, 0:1024], AF.Ln),
                            waits=[("T", tok_L), ("V", p.cnt["V"])], inc="S")
                        p.drain("scalar")
                        tok_sr = p.emit("vector", lambda e, lr=lr: e.tensor_scalar(
                            mrow, lr, -64.0, None, OP.mult),
                            waits=[("S", tok_ln2)], inc="V")
                        p.drain("vector")
                        mb_s, mg_s = "MB%d" % w, "MG%d" % w
                        p.emit("sync", lambda e, w=w: e.dma_start(
                            out=mscr[0:1, w * 1024:(w + 1) * 1024], in_=mrow),
                            waits=[("V", tok_sr), (mg_s, p.cnt[mg_s])],
                            inc=mb_s, inc_n=16)
                        p.emit("gpsimd", lambda e, w=w: e.dma_start(
                            out=mrep[:, w * 1024:(w + 1) * 1024],
                            in_=mscr[0:1, w * 1024:(w + 1) * 1024
                                     ].broadcast_to([128, 1024])),
                            waits=[(mb_s, p.cnt[mb_s]), ("V", p.cnt["V"])],
                            inc=mg_s, inc_n=16)
                        tok_mg[w] = (mg_s, p.cnt[mg_s])
            tok_av = p.cnt["T"]
            p.emit("vector", lambda e: e.reciprocal(lrec[:, 0:16], PSD[:, 1024:1040]),
                   waits=[("T", tok_av)])
            p.drain("vector")
            lb = lrec[:, 0:16].rearrange("p q -> p q ()").broadcast_to([128, 16, 64])
            o3 = PSD[:, 0:1024].rearrange("p (q f) -> p q f", f=64)
            d3 = opair.rearrange("p (q f) -> p q f", f=128)[:, :, r0:r0 + 64]
            tok_oev = p.emit("vector", lambda e, o3=o3, d3=d3, lb=lb: e.tensor_tensor(
                d3, o3, lb, OP.mult), inc="V")
            p.drain("vector")
            if h % 2 == 1:
                c = h // 2
                emit_transposes(
                    opair, 16,
                    lambda j, u, c=c: OFM[:, c * 2048 + j * 128: c * 2048 + (j + u) * 128
                                          ].rearrange("p (k c2) -> p k c2", c2=128),
                    waits=[("V", tok_oev)])
                p.emit("vector", None, waits=[("T", p.cnt["T"])])
            p.emit("tensor", None, waits=[("V", tok_oev)])

    def proj_phase(s):
        p.emit("gpsimd", lambda e: e.dma_start(out=bprow2, in_=bproj),
               waits=[("T", p.cnt["T"]), ("S", p.cnt["S"])], inc="LP", inc_n=16)
        for k in range(NKC):
            p.emit("gpsimd", lambda e, k=k: e.dma_start(
                out=WPROJT[:, k * 1024:(k + 1) * 1024], in_=sWpT[k]),
                inc="LP", inc_n=16)
        p.emit("tensor", None, waits=[("LP", p.cnt["LP"]), ("S", p.cnt["S"]),
                                      ("LGC", 32)])
        for c in range(NKC):
            for hs in range(2):
                p.emit("tensor", lambda e, c=c, hs=hs: e.matmul(
                    PSD[0:1, hs * 512:(hs + 1) * 512], bvcol[:, c:c + 1],
                    WPROJT[:, c * 1024 + hs * 512: c * 1024 + (hs + 1) * 512],
                    start=(c == 0), stop=(c == NKC - 1)),
                    inc="T" if (c == NKC - 1 and hs == 1) else None)
        tok_pb = p.cnt["T"]
        tok_pbe = p.emit("scalar", lambda e: e.copy(pbias, PSD[0:1, 0:1024]),
                         waits=[("T", tok_pb)], inc="S")
        p.drain("scalar")
        tok_cb = p.emit("vector", lambda e: e.tensor_tensor(
            cbias, pbias, bprow2, OP.add), waits=[("S", tok_pbe)], inc="V")
        p.drain("vector")
        tok_res = {0: ("V", p.cnt["V"]), 1: ("V", p.cnt["V"])}
        pend_f0 = [None]
        for t in range(NT):
            pb = PS[:, 0:1024] if t % 2 == 0 else PS[:, 1024:2048]
            first_w = [tok_res[t % 2], ("V", tok_cb)]
            for c in range(NKC):
                for hs in range(2):
                    p.emit("tensor", lambda e, c=c, hs=hs, t=t, pb=pb: e.matmul(
                        pb[:, hs * 512:(hs + 1) * 512],
                        OFM[:, c * 2048 + t * 128: c * 2048 + (t + 1) * 128],
                        WPROJT[:, c * 1024 + hs * 512: c * 1024 + (hs + 1) * 512],
                        start=(c == 0), stop=False),
                        waits=first_w if (c == 0 and hs == 0) else [])
            for hs in range(2):
                p.emit("tensor", lambda e, hs=hs, pb=pb: e.matmul(
                    pb[:, hs * 512:(hs + 1) * 512], onesr,
                    cbias[0:1, hs * 512:(hs + 1) * 512], start=False, stop=True),
                    inc="T" if hs == 1 else None)
            tok_p = p.cnt["T"]
            xt = xres[:, t * D:(t + 1) * D]
            tok_r = p.emit("vector", lambda e, xt=xt, pb=pb: e.tensor_tensor(
                xt, pb[:, 0:1024], xt, OP.add), waits=[("T", tok_p)], inc="V")
            p.drain("vector")
            tok_res[t % 2] = ("V", tok_r)
            xl2 = htile[:, 0:D]
            tok_ln = emit_ln(xt, xl2)
            vt0, _ = emit_quant_head(p, tn_f0, xl2, D, after=[("V", tok_ln)])
            if pend_f0[0] is not None:
                pend_f0[0]()
            p.emit("vector", None, waits=[("T", p.cnt["T"])])
            tok_q = emit_quant_tail(p, tn_f0, vt0, D, xln)

            def f0_tr(t=t, tok_q=tok_q):
                emit_transposes(
                    xln, NKC,
                    lambda j, u, t=t: ub.rearrange("p (k c) -> p k c", c=2048)
                    [:, j:j + u, t * 128:(t + 1) * 128],
                    waits=[("V", tok_q)])
            pend_f0[0] = f0_tr
        if pend_f0[0] is not None:
            pend_f0[0]()
            pend_f0[0] = None

    def ffn_phase(s):
        regions = [(PS[:, 0:512], PS[:, 512:1024]),
                   (PS[:, 1024:1536], PS[:, 1536:2048])]
        reg_free = {0: ("V", 0), 1: ("V", 0)}
        hqv2 = [qkt[:, 22528:23936], qkt[:, 23936:25344]]
        pend_td = [None]
        last_res = [("V", 0)]
        prev_q = [("V", 0)]
        gs = 0
        for hf in range(2):
            if hf == 1 and pend_td[0] is not None:
                pend_td[0]()
                pend_td[0] = None
            wl0 = [("T", p.cnt["T"]), ("S", p.cnt["S"])] if hf == 1 else []
            for k in range(NKC):
                p.emit("sync", lambda e, k=k, hf=hf: e.dma_start(
                    out=WGH[:, k * HALF:(k + 1) * HALF],
                    in_=sWg[k, 0:128, hf * HALF:(hf + 1) * HALF]),
                    waits=wl0 if k == 0 else [], inc="LF", inc_n=16)
                p.emit("sync", lambda e, k=k, hf=hf: e.dma_start(
                    out=WUH[:, k * HALF:(k + 1) * HALF],
                    in_=sWu[k, 0:128, hf * HALF:(hf + 1) * HALF]), inc="LF", inc_n=16)
            for j in range(NHC):
                p.emit("sync", lambda e, j=j, hf=hf: e.dma_start(
                    out=WDH[:, j * 1024:(j + 1) * 1024],
                    in_=sWd[hf * NHC + j, 0:128, 0:1024]), inc="LF", inc_n=16)
            tok_w = p.tok("LF")
            for t in range(NT):
                subs = [(0, 512), (512, 1024), (1024, HALF)]
                for a, bq in subs:
                    sw = bq - a
                    greg, ureg = regions[gs % 2]
                    first_w = [tok_w, reg_free[gs % 2]]
                    for which in range(2):
                        WH = WGH if which == 0 else WUH
                        dst = greg if which == 0 else ureg
                        for k in range(NKC):
                            p.emit("tensor", lambda e, k=k, a=a, bq=bq, WH=WH,
                                   dst=dst, t=t, sw=sw: e.matmul(
                                dst[:, 0:sw],
                                ub[:, k * 2048 + t * 128: k * 2048 + (t + 1) * 128],
                                WH[:, k * HALF + a: k * HALF + bq],
                                start=(k == 0), stop=(k == NKC - 1)),
                                waits=first_w if (which == 0 and k == 0) else [],
                                inc="T" if (which == 1 and k == NKC - 1) else None)
                    tok_gu = p.cnt["T"]
                    ts1 = p.emit("scalar", lambda e, a=a, sw=sw, greg=greg:
                                 e.activation(htile[:, a:a + sw], greg[:, 0:sw],
                                              AF.Silu),
                                 waits=[("T", tok_gu), prev_q[0]], inc="S")
                    p.drain("scalar")
                    tok_m = p.emit("vector", lambda e, a=a, sw=sw, ureg=ureg:
                                   e.tensor_tensor(
                        htile[:, a:a + sw], htile[:, a:a + sw], ureg[:, 0:sw],
                        OP.mult), waits=[("S", ts1)], inc="V")
                    p.drain("vector")
                    reg_free[gs % 2] = ("V", tok_m)
                    gs += 1
                vt, _ = emit_quant_head(p, tn, htile[:, 0:HALF], HALF)
                if pend_td[0] is not None:
                    pend_td[0]()
                hq = hqv2[t % 2]
                tok_q = emit_quant_tail(p, tn, vt, HALF, hq[:, 0:HALF],
                                        out_wait=[("T", p.cnt["T"])])
                prev_q[0] = ("V", tok_q)

                def td(t=t, hf=hf, tok_q=tok_q, hq=hq):
                    tok_tr = emit_transposes(
                        hq[:, 0:HALF], NHC,
                        lambda j, u: hstage[:, j * 128:(j + u) * 128
                                            ].rearrange("p (k c) -> p k c", c=128),
                        waits=[("V", tok_q)])
                    for j in range(NHC):
                        for ds in range(2):
                            p.emit("tensor", lambda e, j=j, ds=ds: e.matmul(
                                PSD[:, ds * 512:(ds + 1) * 512],
                                hstage[:, j * 128:(j + 1) * 128],
                                WDH[:, j * 1024 + ds * 512: j * 1024 + (ds + 1) * 512],
                                start=(j == 0), stop=(j == NHC - 1)),
                                waits=([("S", tok_tr), last_res[0]]
                                       if (j == 0 and ds == 0) else []),
                                inc="T" if (j == NHC - 1 and ds == 1) else None)
                    tok_d = p.cnt["T"]
                    xt = xres[:, t * D:(t + 1) * D]
                    if hf == 0:
                        tok_o = p.emit("vector", lambda e, xt=xt: e.tensor_tensor(
                            xt, PSD[:, 0:1024], xt, OP.add),
                            waits=[("T", tok_d)], inc="V")
                        p.drain("vector")
                    else:
                        tok_o = p.emit("vector", lambda e, xt=xt: e.tensor_tensor(
                            outt, PSD[:, 0:1024], xt, OP.add),
                            waits=[("T", tok_d), ("ST", p.cnt["ST"])], inc="V")
                        p.drain("vector")
                        p.emit("sync", lambda e, t=t: e.dma_start(
                            out=out_h[s * T + t * 128: s * T + (t + 1) * 128, :],
                            in_=outt), waits=[("V", tok_o)], inc="ST", inc_n=16)
                    last_res[0] = ("V", tok_o)
                pend_td[0] = td
        if pend_td[0] is not None:
            pend_td[0]()
        p.barrier("ffn-end")

    for s in range(NSEQ if limit is None else 1):
        xl = xload0 if s == 0 else emit_xload(s)
        seq_phase(s, xl)
        if limit == "qkv":
            p.barrier("limit")
            break
        attn_phase(s)
        p.barrier("attn-end")
        if limit == "attn":
            break
        proj_phase(s)
        p.barrier("proj-end")
        if limit == "proj":
            break
        ffn_phase(s)
        p.barrier("seq-end")

    p.ops["sync"].append(([("ST", p.cnt["ST"])], None, None, 1))
    p.ops["gpsimd"].append(([("ST", p.cnt["ST"])], None, None, 1))

    with nc.Block() as block:
        p.build(block)
    es.close()
    return nc


_NC_CACHE = [None]


def kernel(L0_attn_x, L0_attn_Wqkv, L0_attn_bqkv, L0_attn_Wproj, L0_attn_bproj,
           L0_ffn_w_gate, L0_ffn_w_up, L0_ffn_w_down):
    x = np.ascontiguousarray(np.asarray(L0_attn_x, dtype=np.float32))
    Wqkv = np.ascontiguousarray(np.asarray(L0_attn_Wqkv, dtype=np.float32))
    bqkv = np.asarray(L0_attn_bqkv, dtype=np.float32)
    Wqkv_p, bqkv_p = _permute_qkv(Wqkv, bqkv)
    consts = _host_consts()
    if _NC_CACHE[0] is None:
        _NC_CACHE[0] = build_program()
    nc = _NC_CACHE[0]
    in_maps = []
    for c in range(NCORES):
        in_maps.append({
            "x": np.ascontiguousarray(x[c * NSEQ:(c + 1) * NSEQ].reshape(NSEQ * T, D)),
            "wqkv": np.ascontiguousarray(Wqkv_p),
            "bqkv": np.ascontiguousarray(bqkv_p[0:2 * D].reshape(1, 2 * D)),
            "bvcolh": np.ascontiguousarray(bqkv_p[2 * D:].reshape(8, 128).T),
            "wproj": np.ascontiguousarray(np.asarray(L0_attn_Wproj, np.float32)),
            "bproj": np.ascontiguousarray(np.asarray(L0_attn_bproj, np.float32).reshape(1, D)),
            "wgate": np.ascontiguousarray(np.asarray(L0_ffn_w_gate, np.float32)),
            "wup": np.ascontiguousarray(np.asarray(L0_ffn_w_up, np.float32)),
            "wdown": np.ascontiguousarray(np.asarray(L0_ffn_w_down, np.float32)),
            "ropeA": consts["ropeA"], "ropeB": consts["ropeB"],
            "ident": consts["ident"], "mask": consts["mask"],
            "ones_row": consts["ones_row"], "ones_col": consts["ones_col"],
            "ones_pair": consts["ones_pair"],
        })
    res = run_bass_kernel_spmd(nc, in_maps, core_ids=list(range(NCORES)))
    shards = [np.asarray(res.results[c]["out"]) for c in range(NCORES)]
    out = np.empty((B, T, D), shards[0].dtype)
    for c in range(NCORES):
        out[c * NSEQ:(c + 1) * NSEQ] = shards[c].reshape(NSEQ, T, D)
    return out


# version 19 (final)
# speedup vs baseline: 1.0975x; 1.0032x over previous
import math
import numpy as np
import ml_dtypes
import concourse.bass as bass
import concourse.mybir as mybir
from concourse.bass_utils import run_bass_kernel_spmd
from contextlib import ExitStack

F32 = mybir.dt.float32
BF16 = mybir.dt.bfloat16
AF = mybir.ActivationFunctionType
OP = mybir.AluOpType
AX = mybir.AxisListType

NCORES = 8
B = 16
T = 2048
D = 1024
H = 16
DH = 64
DFF = 2816
NSEQ = B // NCORES
NT = T // 128
NKC = D // 128
NDC = DFF // 128
HALF = DFF // 2
NHC = HALF // 128

MASK_NEG = -30000.0

_DEBUG_TAPS = []
_DEBUG = False


class Prog:
    ENGS = ("sync", "gpsimd", "tensor", "vector", "scalar")

    def __init__(self, nc, es):
        self.nc = nc
        self.es = es
        self.sems = {}
        self.cnt = {}
        for name in ("T", "V", "S"):
            self.sem(name)
        self.ops = {e: [] for e in self.ENGS}
        self.observed = {e: {} for e in self.ENGS}

    def sem(self, name):
        if name not in self.sems:
            self.sems[name] = self.es.enter_context(self.nc.semaphore(name))
            self.cnt[name] = 0
        return name

    def emit(self, eng, fn, waits=(), inc=None, inc_n=1):
        wl = []
        for s, v in waits:
            if self.observed[eng].get(s, 0) < v:
                wl.append((s, v))
                self.observed[eng][s] = v
        tok = None
        if inc is not None:
            self.cnt[inc] += inc_n
            tok = self.cnt[inc]
        self.ops[eng].append((wl, fn, inc, inc_n))
        return tok

    def drain(self, eng):
        self.ops[eng].append(([], lambda e: e.drain(), None, 1))

    def tok(self, sem):
        return (sem, self.cnt[sem])

    def barrier(self, tag=""):
        toks = [(s, v) for s, v in self.cnt.items() if v > 0]
        for e in self.ENGS:
            for s, v in toks:
                if self.observed[e].get(s, 0) < v:
                    self.ops[e].append(([(s, v)], None, None, 1))
                    self.observed[e][s] = v

    def build(self, block):
        sems = self.sems

        def mk(eng_name):
            def body(eng):
                for wl, fn, inc, inc_n in self.ops[eng_name]:
                    for s, v in wl:
                        eng.wait_ge(sems[s], v)
                    if fn is None:
                        continue
                    r = fn(eng)
                    if inc is not None:
                        r.then_inc(sems[inc], inc_n)
            return body

        block.sync(mk("sync"))
        block.gpsimd(mk("gpsimd"))
        block.tensor(mk("tensor"))
        block.vector(mk("vector"))
        block.scalar(mk("scalar"))


def _host_consts():
    import jax
    import jax.numpy as jnp

    def _rope_tensor():
        pos = jnp.arange(T, dtype=jnp.float32)
        angles = pos[:, None] / 10000.0 ** (jnp.arange(0, DH, 2, dtype=jnp.float32) / DH)
        return jnp.stack([jnp.cos(angles), jnp.sin(angles)], axis=-1).reshape(T, DH)

    with jax.default_device(jax.devices("cpu")[0]):
        A = np.asarray(jax.jit(_rope_tensor)(), dtype=np.float32)
    Bm = np.float32(1.0) - A
    Bs = Bm.copy()
    Bs[:, 0::2] *= -1.0
    At = np.zeros((128, NT * DH), np.float32)
    Bt = np.zeros((128, NT * DH), np.float32)
    for t in range(NT):
        At[:, t * DH:(t + 1) * DH] = A[t * 128:(t + 1) * 128]
        Bt[:, t * DH:(t + 1) * DH] = Bs[t * 128:(t + 1) * 128]
    ident = np.eye(128).astype(ml_dtypes.bfloat16)
    mask = np.where(np.arange(128)[None, :] >= np.arange(128)[:, None],
                    0.0, MASK_NEG).astype(np.float32)
    return {
        "ropeA": At,
        "ropeB": Bt,
        "ident": ident,
        "identf": np.eye(128).astype(np.float32),
        "mask": mask,
        "ones_row": np.ones((1, 128), ml_dtypes.bfloat16),
        "ones_pair": (lambda z: (z.__setitem__((0, slice(None)), 1.0),
                                 z.__setitem__((64, slice(None)), 1.0), z)[-1])(
            np.zeros((65, 128), ml_dtypes.bfloat16)),
        "ones_row_f32": np.ones((1, 128), np.float32),
        "ones_col": np.ones((128, 1), ml_dtypes.bfloat16),
    }


def _permute_qkv(Wqkv, bqkv):
    idx = np.zeros(3 * D, np.int64)
    for h in range(H):
        base = h * 3 * DH
        for c in range(DH):
            idx[h * DH + c] = base + c
            idx[D + h * DH + c] = base + DH + c
            idx[2 * D + h * DH + c] = base + 2 * DH + c
    return Wqkv[idx], bqkv[idx]


def emit_quant_head(p, tn, src, F, after=(), sg_wait=None, sq_wait=(),
                    vbuf=None):
    nb = F // 16
    g = lambda a: a.rearrange("p (g k) -> p g k", k=16)
    v_ = p.emit
    mb = tn["mblk"][:, 0:nb]
    rb = tn["rblk"][:, 0:nb]
    r6 = tn["r6sq"][:, 0:nb]
    s2 = tn["s2b"][:, 0:nb]
    sg = tn["sgq"][:, 0:F]
    vt = src if vbuf is None else vbuf[:, 0:F]
    aft = list(after)
    if sg_wait is None:
        sg_wait = [("V", p.cnt["V"])]
    tok_am = v_("vector", lambda e: e.tensor_reduce(mb, g(src), AX.X, OP.max,
                                                    apply_absolute_value=True),
                waits=aft, inc="V")
    p.drain("vector")
    v_("scalar", lambda e: e.sign(sg, src), waits=aft + list(sg_wait))
    p.drain("scalar")
    v_("vector", lambda e: e.reciprocal(rb, mb))
    p.drain("vector")
    v_("vector", lambda e: e.tensor_tensor(r6, rb, rb, OP.mult))
    v_("vector", lambda e: e.tensor_scalar(s2, mb, 1.0 / 12.0, 5.0e-13, OP.mult, OP.max))
    p.drain("vector")
    v_("vector", lambda e: e.tensor_scalar(rb, r6, 36.0, 9.0e24, OP.mult, OP.min))
    p.drain("vector")
    tok_sq = v_("scalar", lambda e: e.activation(vt, src, AF.Square),
                waits=[("V", tok_am)] + list(sq_wait), inc="S")
    p.drain("scalar")
    r6b = rb.rearrange("p g -> p g ()").broadcast_to([128, nb, 16])
    v_("vector", lambda e: e.tensor_tensor(g(vt), g(vt), r6b, OP.mult),
       waits=[("S", tok_sq)])
    p.drain("vector")
    return vt, tok_am


def emit_quant_tail(p, tn, vt, F, out_bf, out_wait=()):
    nb = F // 16
    g = lambda a: a.rearrange("p (g k) -> p g k", k=16)
    v_ = p.emit
    s2 = tn["s2b"][:, 0:nb]
    sg = tn["sgq"][:, 0:F]
    aA = tn["accqA"][:, 0:F]
    aB = tn["accqB"][:, 0:F]
    v_("vector", lambda e: e.tensor_scalar(aA, vt, 25.0, 2.0, OP.is_gt, OP.mult))
    p.drain("vector")
    v_("vector", lambda e: e.scalar_tensor_tensor(aB, vt, 12.25, aA, OP.is_gt, OP.add))
    p.drain("vector")
    v_("vector", lambda e: e.scalar_tensor_tensor(aA, vt, 6.25, aB, OP.is_gt, OP.add))
    p.drain("vector")
    v_("vector", lambda e: e.tensor_scalar(aB, aA, 2.0, None, OP.mult))
    p.drain("vector")
    v_("vector", lambda e: e.scalar_tensor_tensor(aA, vt, 3.0625, aB, OP.is_gt, OP.add))
    p.drain("vector")
    v_("vector", lambda e: e.scalar_tensor_tensor(aB, vt, 1.5625, aA, OP.is_gt, OP.add))
    p.drain("vector")
    v_("vector", lambda e: e.scalar_tensor_tensor(aA, vt, 0.5625, aB, OP.is_gt, OP.add))
    p.drain("vector")
    v_("vector", lambda e: e.scalar_tensor_tensor(aB, vt, 0.0625, aA, OP.is_gt, OP.add))
    p.drain("vector")
    s2b = s2.rearrange("p g -> p g ()").broadcast_to([128, nb, 16])
    v_("vector", lambda e: e.tensor_tensor(g(aA), g(aB), s2b, OP.mult))
    p.drain("vector")
    tok = v_("vector", lambda e: e.tensor_tensor(out_bf, aA, sg, OP.mult),
             waits=list(out_wait), inc="V")
    p.drain("vector")
    return tok


def emit_quant(p, tn, src, F, out_bf, after=(), out_wait=(), sg_wait=None):
    vt, _ = emit_quant_head(p, tn, src, F, after=after, sg_wait=sg_wait)
    return emit_quant_tail(p, tn, vt, F, out_bf, out_wait=out_wait)


def build_program(limit=None):
    nc = bass.Bass("TRN2", target_bir_lowering=False)
    es = ExitStack()

    xh = nc.dram_tensor("x", [NSEQ * T, D], F32, kind="ExternalInput").ap()
    wqkv = nc.dram_tensor("wqkv", [3 * D, D], F32, kind="ExternalInput").ap()
    bqkv = nc.dram_tensor("bqkv", [1, 2048], F32, kind="ExternalInput").ap()
    bvcolh = nc.dram_tensor("bvcolh", [128, 8], F32, kind="ExternalInput").ap()
    wproj = nc.dram_tensor("wproj", [D, D], F32, kind="ExternalInput").ap()
    bproj = nc.dram_tensor("bproj", [1, D], F32, kind="ExternalInput").ap()
    wgate = nc.dram_tensor("wgate", [DFF, D], F32, kind="ExternalInput").ap()
    wup = nc.dram_tensor("wup", [DFF, D], F32, kind="ExternalInput").ap()
    wdown = nc.dram_tensor("wdown", [D, DFF], F32, kind="ExternalInput").ap()
    c_ropeA = nc.dram_tensor("ropeA", [128, NT * DH], F32, kind="ExternalInput").ap()
    c_ropeB = nc.dram_tensor("ropeB", [128, NT * DH], F32, kind="ExternalInput").ap()
    c_ident = nc.dram_tensor("ident", [128, 128], BF16, kind="ExternalInput").ap()
    c_mask = nc.dram_tensor("mask", [128, 128], F32, kind="ExternalInput").ap()
    c_ones_row = nc.dram_tensor("ones_row", [1, 128], BF16, kind="ExternalInput").ap()
    c_ones_col = nc.dram_tensor("ones_col", [128, 1], BF16, kind="ExternalInput").ap()
    c_ones_pair = nc.dram_tensor("ones_pair", [65, 128], BF16, kind="ExternalInput").ap()
    out_h = nc.dram_tensor("out", [NSEQ * T, D], F32, kind="ExternalOutput").ap()
    sWg = nc.dram_tensor("sWg", [NKC, 128, DFF], BF16, kind="Internal").ap()
    sWu = nc.dram_tensor("sWu", [NKC, 128, DFF], BF16, kind="Internal").ap()
    sWd = nc.dram_tensor("sWd", [NDC, 128, D], BF16, kind="Internal").ap()
    mscr = nc.dram_tensor("mscr", [1, 2048], BF16, kind="Internal").ap()
    sWqT = nc.dram_tensor("sWqT", [NKC, 128, 3072], BF16, kind="Internal").ap()
    sWpT = nc.dram_tensor("sWpT", [NKC, 128, 1024], BF16, kind="Internal").ap()

    dbg = {}
    if _DEBUG:
        for name, shape, dt in _DEBUG_TAPS:
            dbg[name] = nc.dram_tensor(name, shape, dt, kind="ExternalOutput").ap()

    sb = lambda name, shape, dt: es.enter_context(nc.sbuf_tensor(name, shape, dt))[:]
    ropebuf = sb("ropebuf", [128, 256], F32)
    ident = sb("ident_s", [128, 128], BF16)
    maskt = sb("mask_s", [128, 128], F32)
    onesr = sb("onesr", [1, 128], BF16)
    onesc = sb("onesc", [128, 1], BF16)
    onespair = sb("onespair", [65, 128], BF16)
    brow = sb("brow", [1, 2048], BF16)
    bvcol = sb("bvcol", [128, 8], BF16)
    xres = sb("xres", [128, NT * D], BF16)
    qkt = sb("qkt", [128, 32 * 1024], BF16)
    ub = sb("ub", [128, 16 * 1024], BF16)
    wb = sb("wb", [128, 24576], BF16)
    vtile = sb("vtile", [128, HALF], F32)
    htile = sb("htile", [128, HALF], F32)
    kropet = sb("kropet", [128, D], BF16)
    qropet = sb("qropet", [128, D], BF16)
    accA = sb("accA", [128, D], BF16)
    accB = sb("accB", [128, D], BF16)
    stats = sb("stats", [128, 8], F32)
    mblk = sb("mblk", [128, HALF // 16], F32)
    rblk = sb("rblk", [128, HALF // 16], F32)
    r6sq = sb("r6sq", [128, HALF // 16], F32)
    s2b = sb("s2b", [128, HALF // 16], F32)
    qn2 = sb("qn2", [128, H], F32)
    kn2 = sb("kn2", [128, H], F32)
    nrm = sb("nrm", [128, H], F32)
    negBs = sb("negBs", [128, H], F32)
    lrec = sb("lrec", [128, H], F32)
    nrow = sb("nrow", [1, 16], BF16)
    xln = accB
    xfst = accA
    outt = vtile[:, 0:1024]
    PS = es.enter_context(nc.psum_tensor("PS", [128, 2048], F32))[:]
    PSD = es.enter_context(nc.psum_tensor("PSD", [128, 1088], F32))[:]
    PST = es.enter_context(nc.psum_tensor("PST", [128, 512], BF16))[:]

    p = Prog(nc, es)
    for sname in ("LC", "LGC", "LW0", "LW1", "LG0", "LG1", "LB0", "LB1",
                  "SB0", "SB1", "LQ",
                  "LX0", "LX1", "LX2", "LX3", "MB0", "MB1", "MG0", "MG1",
                  "RP0", "RP1", "LP", "LF", "ST"):
        p.sem(sname)


    WQKVT = wb[:, 0:24576]
    OFM = wb[:, 0:16384]
    PT0 = wb[:, 16384:17408]
    PT1 = wb[:, 17408:18432]
    opair = wb[:, 18432:20480]
    WDH = wb[:, 0:NHC * 1024]
    accqA = wb[:, 11264:12672]
    accqB = wb[:, 12672:14080]
    sgq = wb[:, 14080:15488]
    hstage = wb[:, 15488:16896]
    WGH = qkt[:, 0:NKC * HALF]
    WUH = qkt[:, NKC * HALF:2 * NKC * HALF]
    hqv = qkt[:, 22528:23936]
    WPROJT = qkt[:, 0:8192]
    bprow2 = qkt[0:1, 8192:9216]
    pbias = qkt[0:1, 9216:10240]
    cbias = qkt[0:1, 10240:11264]

    tn = dict(mblk=mblk, rblk=rblk, r6sq=r6sq, s2b=s2b,
              sgq=sgq, accqA=accqA, accqB=accqB)
    tn_f0 = dict(mblk=mblk, rblk=rblk, r6sq=r6sq, s2b=s2b,
                 accqA=qkt[:, 25344:26368], accqB=qkt[:, 26368:27392],
                 sgq=qkt[:, 27392:28416])

    def qk_slice(h, which, c0, c1):
        r0 = (h % 2) * 64
        base = (h // 2) * 4096 + which * 2048
        return qkt[r0:r0 + 64, base + c0:base + c1]

    p.emit("sync", lambda e: e.dma_start(out=ident, in_=c_ident), inc="LC", inc_n=16)
    p.emit("sync", lambda e: e.dma_start(out=maskt, in_=c_mask), inc="LC", inc_n=16)
    p.emit("sync", lambda e: e.dma_start(out=onesr, in_=c_ones_row), inc="LC", inc_n=16)
    p.emit("sync", lambda e: e.dma_start(out=onesc, in_=c_ones_col), inc="LC", inc_n=16)
    p.emit("sync", lambda e: e.dma_start(out=onespair, in_=c_ones_pair), inc="LC", inc_n=16)
    p.emit("gpsimd", lambda e: e.dma_start(out=brow, in_=bqkv), inc="LGC", inc_n=16)
    p.emit("gpsimd", lambda e: e.dma_start(out=bvcol, in_=bvcolh), inc="LGC", inc_n=16)
    consts_tok = [p.tok("LC"), p.tok("LGC")]

    def emit_transposes(src_bf, n, dst_fn, waits, pair_mode=False):
        ntr = n
        j = 0
        while j < ntr:
            jn = min(4, ntr - j)
            for i in range(jn):
                if pair_mode:
                    hh = j + i
                    r0 = (hh % 2) * 64
                    slot = i // 2
                    fn = (lambda e, hh=hh, slot=slot, r0=r0: e.transpose(
                        PST[r0:r0 + 64, slot * 128:(slot + 1) * 128],
                        src_bf[:, hh * 64:(hh + 1) * 64], ident,
                        tile_position=(0, r0)))
                else:
                    fn = (lambda e, jj=j + i, i=i: e.transpose(
                        PST[:, i * 128:(i + 1) * 128],
                        src_bf[:, jj * 128:(jj + 1) * 128], ident))
                p.emit("tensor", fn, waits=waits if (j == 0 and i == 0) else [],
                       inc="T" if i == jn - 1 else None)
            tok_t = p.cnt["T"]
            nu = (jn // 2 if pair_mode else jn)
            tok_e = p.emit("scalar", lambda e, j=j, nu=nu: e.copy(
                dst_fn(j, nu),
                PST[:, 0:nu * 128].rearrange("p (k c) -> p k c", c=128)),
                waits=[("T", tok_t)], inc="S")
            p.drain("scalar")
            p.emit("tensor", None, waits=[("S", tok_e)])
            j += jn
        return p.cnt["S"]


    def emit_xload(s):
        toks = {}
        for g4 in range(4):
            sem = "LX%d" % g4
            for t in range(g4 * 4, g4 * 4 + 4):
                p.emit("gpsimd", lambda e, t=t, s=s: e.dma_start(
                    out=xres[:, t * D:(t + 1) * D],
                    in_=xh[s * T + t * 128: s * T + (t + 1) * 128, :]),
                    inc=sem, inc_n=16)
            toks[g4] = (sem, p.cnt[sem])
        return toks


    def emit_wprep():
        jobs = []
        for (s_ap, nrb, K, dst, base) in ((wgate, NDC, D, sWg, 0),
                                          (wup, NDC, D, sWu, 0),
                                          (wdown, NKC, DFF, sWd, 0)):
            nh = K // HALF if K > HALF else 1
            for r in range(nrb):
                for half in range(nh):
                    jobs.append((s_ap, r, half, min(HALF, K), dst, base, True))
        bjobs = []
        for r in range(3 * D // 128):
            bjobs.append((wqkv, r, 0, D, sWqT, 0, False))
        for r in range(NKC):
            bjobs.append((wproj, r, 0, D, sWpT, 0, False))
        out = []
        bi = 0
        for i, j in enumerate(jobs):
            out.append(j)
            if i % 2 == 1 and bi < len(bjobs):
                out.append(bjobs[bi]); bi += 1
        out.extend(bjobs[bi:])
        jobs = out
        srcs = [htile, vtile]
        hqs = [qkt[:, 22528:23936], qkt[:, 23936:25344]]
        hss = [wb[:, 15488:16896], wb[:, 16896:18304]]
        tns = [dict(tn), dict(tn)]
        tns[1]["sgq"] = wb[:, 18304:19712]
        lws = ["LW0", "LW1"]
        lgs = ["LG0", "LG1"]
        rel_src = {0: ("V", 0), 1: ("V", 0)}
        rel_hq = {0: ("T", 0), 1: ("T", 0)}
        rel_hs = {0: ("LG0", 0), 1: ("LG1", 0)}
        lbs = ["LB0", "LB1"]
        sbs = ["SB0", "SB1"]
        hqBs = [qkt[:, 25344:26368], qkt[:, 26368:27392]]
        hsBs = [wb[:, 19712:20736], wb[:, 20736:21760]]
        rel_hqB = {0: ("T", 0), 1: ("T", 0)}
        rel_hsB = {0: ("SB0", 0), 1: ("SB1", 0)}
        bcnt = [0]
        pend_fin = [None]

        def finisher(hq, hs, nck, half, r, dst, base, trw, par, do_q, bpar):
            def fin():
                emit_transposes(hq, nck,
                                lambda j, u, hs=hs: hs[:, j * 128:(j + u) * 128
                                                       ].rearrange("p (k c) -> p k c", c=128),
                                waits=trw)
                if do_q:
                    rel_hq[par] = ("T", p.cnt["T"])
                else:
                    rel_hqB[bpar] = ("T", p.cnt["T"])
                tok_ev = p.cnt["S"]
                for j in range(nck):
                    cglob = half * (HALF // 128) + j
                    p.emit("gpsimd", lambda e, j=j, cglob=cglob, r=r, dst=dst,
                           base=base, hs=hs: e.dma_start(
                        out=dst[base + cglob, 0:128, r * 128:(r + 1) * 128],
                        in_=hs[:, j * 128:(j + 1) * 128]),
                        waits=[("S", tok_ev)] if j == 0 else [],
                        inc=(lgs[par] if do_q else sbs[bpar]), inc_n=16)
                if do_q:
                    rel_hs[par] = (lgs[par], p.cnt[lgs[par]])
                else:
                    rel_hsB[bpar] = (sbs[bpar], p.cnt[sbs[bpar]])
            return fin

        for i, (s_ap, r, half, Fh, dst, base, do_q) in enumerate(jobs):
            par = i % 2
            bpar = bcnt[0] % 2
            srcb = srcs[par][:, 0:Fh]
            hq = (hqs[par] if do_q else hqBs[bpar])[:, 0:Fh]
            hs = hss[par] if do_q else hsBs[bpar]
            if do_q:
                tok_ld = p.emit("sync", lambda e, s_ap=s_ap, r=r, half=half,
                                Fh=Fh, srcb=srcb: e.dma_start(
                    out=srcb, in_=s_ap[r * 128:(r + 1) * 128,
                                       half * HALF:half * HALF + Fh]),
                    waits=[rel_src[par]], inc=lws[par], inc_n=16)
                vt, _ = emit_quant_head(p, tns[par], srcb, Fh,
                                        after=[(lws[par], tok_ld)],
                                        sg_wait=[rel_src[par]])
                if pend_fin[0] is not None:
                    pend_fin[0]()
                tok_q = emit_quant_tail(p, tns[par], vt, Fh, hq,
                                        out_wait=[rel_hq[par]])
                rel_src[par] = ("V", tok_q)
                trw = [("V", tok_q), rel_hs[par]]
            else:
                tok_ld = p.emit("gpsimd", lambda e, s_ap=s_ap, r=r, Fh=Fh,
                                hq=hq: e.dma_start(
                    out=hq, in_=s_ap[r * 128:(r + 1) * 128, 0:Fh]),
                    waits=[rel_hqB[bpar]], inc=lbs[bpar], inc_n=16)
                if pend_fin[0] is not None:
                    pend_fin[0]()
                trw = [(lbs[bpar], tok_ld), rel_hsB[bpar]]
                bcnt[0] += 1
            pend_fin[0] = finisher(hq, hs, Fh // 128, half, r, dst, base,
                                   trw, par, do_q, bpar)
        if pend_fin[0] is not None:
            pend_fin[0]()
            pend_fin[0] = None

    for s, v in consts_tok:
        p.emit("vector", None, waits=[(s, v)])
        p.emit("scalar", None, waits=[(s, v)])
        p.emit("tensor", None, waits=[(s, v)])
    xload0 = emit_xload(0)
    emit_wprep()
    p.barrier("after-wprep")

    def emit_ln(xt_ap, out_bf_ap, after=(), stats_wait=None, dump=None):
        if dump is None:
            dump = vtile[:, 0:D]
        if stats_wait is None:
            stats_wait = [("V", p.cnt["V"])]
        gate = list(after) + list(stats_wait)
        p.emit("scalar", lambda e: e.activation(dump, xt_ap, AF.Copy,
                                                accum_out=stats[:, 0:1]),
               waits=gate)
        p.drain("scalar")
        tok_s = p.emit("scalar", lambda e: e.activation(dump, xt_ap, AF.Square,
                                                        accum_out=stats[:, 1:2]),
                       inc="S")
        p.drain("scalar")
        p.emit("vector", lambda e: e.tensor_scalar(stats[:, 2:3], stats[:, 0:1],
                                                   -1.0 / D, None, OP.mult),
               waits=[("S", tok_s)])
        p.drain("vector")
        p.emit("vector", lambda e: e.tensor_tensor(stats[:, 3:4], stats[:, 2:3],
                                                   stats[:, 2:3], OP.mult))
        p.drain("vector")
        p.emit("vector", lambda e: e.scalar_tensor_tensor(stats[:, 4:5], stats[:, 1:2],
                                                          1.0 / D, stats[:, 3:4],
                                                          OP.mult, OP.subtract))
        p.drain("vector")
        tok_v = p.emit("vector", lambda e: e.tensor_scalar(stats[:, 5:6], stats[:, 4:5],
                                                           1e-5, None, OP.add), inc="V")
        p.drain("vector")
        p.emit("scalar", lambda e: e.activation(stats[:, 6:7], stats[:, 5:6], AF.Ln),
               waits=[("V", tok_v)])
        p.drain("scalar")
        tok_r = p.emit("scalar", lambda e: e.activation(stats[:, 7:8], stats[:, 6:7],
                                                        AF.Exp, scale=-0.5), inc="S")
        p.drain("scalar")
        tok = p.emit("vector", lambda e: e.tensor_scalar(out_bf_ap, xt_ap,
                                                         stats[:, 2:3], stats[:, 7:8],
                                                         OP.add, OP.mult),
                     waits=[("S", tok_r), ("S", p.cnt["S"])], inc="V")
        p.drain("vector")
        return tok

    def seq_phase(s, xload):
        for k in range(NKC):
            p.emit("sync", lambda e, k=k: e.dma_start(
                out=WQKVT[:, k * 3072:(k + 1) * 3072], in_=sWqT[k]),
                waits=([("T", p.cnt["T"]), ("S", p.cnt["S"]),
                        ("V", p.cnt["V"])] if k == 0 else []),
                inc="LQ", inc_n=16)
        tok_lq = ("LQ", p.cnt["LQ"])

        prev_ln = [None]
        pend_qk = [None]
        rp_tok = {}
        ln_tok = {}
        tr_tok = {}

        def emit_lntile(t):
            sl = (t % 2) * 128
            rps = "RP0" if t % 2 == 0 else "RP1"
            p.emit("sync", lambda e, t=t, sl=sl: e.dma_start(
                out=ropebuf[:, sl:sl + 64], in_=c_ropeA[:, t * DH:(t + 1) * DH]),
                waits=[("V", p.cnt["V"])], inc=rps, inc_n=16)
            rp_tok[t] = (rps, p.emit("sync", lambda e, t=t, sl=sl: e.dma_start(
                out=ropebuf[:, sl + 64:sl + 128], in_=c_ropeB[:, t * DH:(t + 1) * DH]),
                inc=rps, inc_n=16))
            xt = xres[:, t * D:(t + 1) * D]
            tok_ln = emit_ln(xt, xln, after=[xload[t // 4]],
                             stats_wait=prev_ln[0])
            prev_ln[0] = [("V", tok_ln)]
            ln_tok[t] = tok_ln
            tr_tok[t] = emit_transposes(
                xln, NKC, lambda j, jn: xfst[:, j * 128:(j + jn) * 128
                                             ].rearrange("p (k c) -> p k c", c=128),
                waits=[("V", tok_ln)])

        emit_lntile(0)
        for t in range(NT):
            sl = (t % 2) * 128
            def psn(nck):
                if nck < 4:
                    return PS[:, nck * 512:(nck + 1) * 512]
                return PSD[:, (nck - 4) * 512:(nck - 3) * 512]
            first_w = [("S", tr_tok[t]), ("V", p.cnt["V"]), ("S", p.cnt["S"]),
                       tok_lq]
            for k in range(NKC):
                for nck in range(6):
                    p.emit("tensor", lambda e, k=k, nck=nck: e.matmul(
                        psn(nck), xfst[:, k * 128:(k + 1) * 128],
                        WQKVT[:, k * 3072 + nck * 512: k * 3072 + (nck + 1) * 512],
                        start=(k == 0), stop=(k == NKC - 1 and nck >= 4)),
                        waits=first_w if (k == 0 and nck == 0) else [])
            if pend_qk[0] is not None:
                pend_qk[0]()
                pend_qk[0] = None
            for nck in range(4):
                for i4 in range(4):
                    r16 = nck * 4 + i4
                    p.emit("tensor", lambda e, nck=nck, i4=i4, r16=r16: e.matmul(
                        psn(nck)[:, i4 * 128:(i4 + 1) * 128], onesr,
                        brow[0:1, r16 * 128:(r16 + 1) * 128],
                        start=False, stop=(i4 == 3)),
                        inc="T" if (nck == 3 and i4 == 3) else None)
            tok_qkv = p.cnt["T"]
            if t + 1 < NT:
                emit_lntile(t + 1)
            rA = ropebuf[:, sl:sl + 64].rearrange(
                "p (r two) -> p () r two", two=2).broadcast_to([128, H, DH // 2, 2])
            rB = ropebuf[:, sl + 64:sl + 128].rearrange(
                "p (r two) -> p () r two", two=2).broadcast_to([128, H, DH // 2, 2])
            g4 = lambda a: a.rearrange("p (h r two) -> p h r two", h=H, two=2)
            psq = g4(PS[:, 0:1024])
            psk = g4(PS[:, 1024:2048])
            q4 = g4(qropet[:, 0:1024])
            k4 = g4(kropet[:, 0:1024])
            pd4 = g4(PSD[:, 0:1024])
            p.emit("vector", lambda e, k4=k4, psq=psq, rB=rB: e.tensor_tensor(
                k4, psq[:, :, :, ::-1], rB, OP.mult),
                waits=[("T", tok_qkv), rp_tok[t]])
            p.drain("vector")
            p.emit("vector", lambda e, q4=q4, psq=psq, rA=rA:
                   e.tensor_tensor(q4, psq, rA, OP.mult))
            p.drain("vector")
            p.emit("vector", lambda e, q4=q4, k4=k4:
                   e.tensor_tensor(q4, q4, k4, OP.add))
            p.drain("vector")
            tok_vev = p.emit("scalar", lambda e, t=t: e.copy(
                ub[:, t * D:(t + 1) * D], PSD[:, 0:1024]),
                waits=[("T", tok_qkv)], inc="S")
            p.drain("scalar")
            p.emit("vector", lambda e, pd4=pd4, psk=psk, rB=rB: e.tensor_tensor(
                pd4, psk[:, :, :, ::-1], rB, OP.mult),
                waits=[("S", tok_vev)])
            p.drain("vector")
            p.emit("vector", lambda e, k4=k4, psk=psk, rA=rA:
                   e.tensor_tensor(k4, psk, rA, OP.mult))
            p.drain("vector")
            tok_rope = p.emit("vector", lambda e, k4=k4, pd4=pd4:
                              e.tensor_tensor(k4, k4, pd4, OP.add), inc="V")
            p.drain("vector")
            def qk_tr(t=t, tok_rope=tok_rope):
                for which, src_bf in ((0, qropet), (1, kropet)):
                    def dstqk(j, u, which=which, t=t):
                        v = qkt.rearrange("p (pr c) -> p pr c", c=4096)
                        return v[:, j:j + u,
                                 which * 2048 + t * 128: which * 2048 + (t + 1) * 128]
                    emit_transposes(src_bf[:, 0:1024], NKC,
                                    lambda j, u, which=which, t=t: dstqk(j, u, which, t),
                                    waits=[("V", tok_rope)])
            pend_qk[0] = qk_tr
        if pend_qk[0] is not None:
            pend_qk[0]()
            pend_qk[0] = None
        p.barrier("p3-end")
        return p.cnt["S"]

    def attn_phase(s):
        mrep = wb[:, 20480:22528]
        mrow = wb[0:1, 22528:23552]
        lnrow = htile
        for h in range(H):
            r0 = (h % 2) * 64
            tp = (64, 0) if (h % 2) else None
            qbase = (h // 2) * 4096
            kbase = qbase + 2048
            tok_exp = {}
            tok_mg = {}
            pend = [None]
            for pss in range(2):
                for w in range(2):
                    for sc in range(8 * (w + 1)):
                        ping = PS[:, 0:1024] if sc % 2 == 0 else PS[:, 1024:2048]
                        ptb = PT0 if sc % 2 == 0 else PT1
                        qlo_g = max(w * 1024, sc * 128)
                        lloc = qlo_g - w * 1024
                        a = lloc
                        first = True
                        while a < 1024:
                            b2 = min(1024, (a // 512 + 1) * 512)
                            wl = [("S", tok_exp.get(sc % 2, 0)),
                                  ("V", p.cnt["V"])] if first else []
                            p.emit("tensor", lambda e, a=a, b2=b2, sc=sc, w=w,
                                   ping=ping, r0=r0, kbase=kbase, qbase=qbase,
                                   tp=tp: e.matmul(
                                ping[:, a:b2],
                                qkt[r0:r0 + 64, kbase + sc * 128: kbase + (sc + 1) * 128],
                                qkt[r0:r0 + 64, qbase + w * 1024 + a: qbase + w * 1024 + b2],
                                start=True, stop=True, tile_position=tp),
                                waits=wl,
                                inc="T" if b2 == 1024 else None)
                            first = False
                            a = b2
                        tok_sc = p.cnt["T"]
                        if sc >= 8 * w:
                            p.emit("vector", lambda e, ping=ping, lloc=lloc:
                                   e.tensor_tensor(
                                ping[:, lloc:lloc + 128], ping[:, lloc:lloc + 128],
                                maskt, OP.add), waits=[("T", tok_sc)], inc="V")
                            if pss == 1:
                                p.drain("vector")
                        if pss == 1:
                            p.emit("vector", lambda e, ping=ping, lloc=lloc, w=w:
                                   e.tensor_tensor(
                                ping[:, lloc:1024], ping[:, lloc:1024],
                                mrep[:, w * 1024 + lloc:(w + 1) * 1024], OP.add),
                                waits=[("T", tok_sc), tok_mg[w]], inc="V")
                        tok_m = p.cnt["V"]
                        scl = (1.0 / 64.0) if pss == 0 else 0.125
                        te = p.emit("scalar", lambda e, ping=ping, ptb=ptb,
                                    lloc=lloc, scl=scl: e.activation(
                                    ptb[:, lloc:1024], ping[:, lloc:1024],
                                    AF.Exp, scale=scl),
                                    waits=[("T", tok_sc), ("V", tok_m)], inc="S")
                        p.drain("scalar")
                        tok_exp[sc % 2] = te

                        def consume(sc=sc, w=w, ptb=ptb, te=te, lloc=lloc, pss=pss, h=h):
                            if pss == 0:
                                nsj = (1024 - lloc + 511) // 512
                                aa = lloc
                                for sj in range(nsj):
                                    b3 = min(1024, (aa // 512 + 1) * 512)
                                    is_b4 = aa < 512
                                    stp = (is_b4 and sc == 8 * w + 3) or \
                                          ((not is_b4) and sc == 8 * (w + 1) - 1)
                                    p.emit("tensor", lambda e, aa=aa, b3=b3, ptb=ptb,
                                           sc=sc, stp=stp: e.matmul(
                                        PSD[0:1, aa:b3], onesc,
                                        ptb[:, aa:b3],
                                        start=(sc == 0),
                                        stop=stp),
                                        waits=[("S", te)] if sj == 0 else [],
                                        inc="T" if sj == nsj - 1 else None)
                                    aa = b3
                            else:
                                for qtl in range(max(0, sc - 8 * w), 8):
                                    qtg = 8 * w + qtl
                                    st_ap = ptb[:, qtl * 128:(qtl + 1) * 128]
                                    p.emit("tensor", lambda e, st_ap=st_ap, qtg=qtg,
                                           sc=sc, w=w, qtl=qtl, h=h: e.matmul(
                                        PSD[:, qtg * 64:(qtg + 1) * 64], st_ap,
                                        ub[:, sc * D + h * 64: sc * D + (h + 1) * 64],
                                        start=(sc == 0 and qtg % 8 == 0),
                                        stop=(sc == 8 * (w + 1) - 1 and qtl == 7)),
                                        waits=[("S", te)] if qtl == max(0, sc - 8 * w) else [])
                                    p.emit("tensor", lambda e, st_ap=st_ap, qtg=qtg,
                                           sc=sc, w=w, qtl=qtl: e.matmul(
                                        PSD[:, 1024 + qtg:1025 + qtg], st_ap, onesc,
                                        start=(w == 0 and sc == 0 and qtl == 0),
                                        stop=(w == 1 and sc == 15 and qtl == 7)),
                                        inc="T" if qtl == 7 else None)
                        if pend[0] is not None:
                            pend[0]()
                        pend[0] = consume
                    if pend[0] is not None:
                        pend[0]()
                        pend[0] = None
                    if pss == 0:
                        tok_L = p.cnt["T"]
                        lr = lnrow[0:1, 0:1024]
                        tok_ln2 = p.emit("scalar", lambda e, lr=lr: e.activation(
                            lr, PSD[0:1, 0:1024], AF.Ln),
                            waits=[("T", tok_L), ("V", p.cnt["V"])], inc="S")
                        p.drain("scalar")
                        tok_sr = p.emit("vector", lambda e, lr=lr: e.tensor_scalar(
                            mrow, lr, -64.0, None, OP.mult),
                            waits=[("S", tok_ln2), ("MB0", p.cnt["MB0"]),
                                   ("MB1", p.cnt["MB1"])], inc="V")
                        p.drain("vector")
                        mb_s, mg_s = "MB%d" % w, "MG%d" % w
                        p.emit("sync", lambda e, w=w: e.dma_start(
                            out=mscr[0:1, w * 1024:(w + 1) * 1024], in_=mrow),
                            waits=[("V", tok_sr), (mg_s, p.cnt[mg_s])],
                            inc=mb_s, inc_n=16)
                        p.emit("gpsimd", lambda e, w=w: e.dma_start(
                            out=mrep[:, w * 1024:(w + 1) * 1024],
                            in_=mscr[0:1, w * 1024:(w + 1) * 1024
                                     ].broadcast_to([128, 1024])),
                            waits=[(mb_s, p.cnt[mb_s]), ("V", p.cnt["V"])],
                            inc=mg_s, inc_n=16)
                        tok_mg[w] = (mg_s, p.cnt[mg_s])
            tok_av = p.cnt["T"]
            p.emit("vector", lambda e: e.reciprocal(lrec[:, 0:16], PSD[:, 1024:1040]),
                   waits=[("T", tok_av)])
            p.drain("vector")
            lb = lrec[:, 0:16].rearrange("p q -> p q ()").broadcast_to([128, 16, 64])
            o3 = PSD[:, 0:1024].rearrange("p (q f) -> p q f", f=64)
            d3 = opair.rearrange("p (q f) -> p q f", f=128)[:, :, r0:r0 + 64]
            tok_oev = p.emit("vector", lambda e, o3=o3, d3=d3, lb=lb: e.tensor_tensor(
                d3, o3, lb, OP.mult), inc="V")
            p.drain("vector")
            if h % 2 == 1:
                c = h // 2
                emit_transposes(
                    opair, 16,
                    lambda j, u, c=c: OFM[:, c * 2048 + j * 128: c * 2048 + (j + u) * 128
                                          ].rearrange("p (k c2) -> p k c2", c2=128),
                    waits=[("V", tok_oev)])
                p.emit("vector", None, waits=[("T", p.cnt["T"])])
            p.emit("tensor", None, waits=[("V", tok_oev)])

    def proj_phase(s):
        p.emit("gpsimd", lambda e: e.dma_start(out=bprow2, in_=bproj),
               waits=[("T", p.cnt["T"]), ("S", p.cnt["S"])], inc="LP", inc_n=16)
        for k in range(NKC):
            p.emit("gpsimd", lambda e, k=k: e.dma_start(
                out=WPROJT[:, k * 1024:(k + 1) * 1024], in_=sWpT[k]),
                inc="LP", inc_n=16)
        p.emit("tensor", None, waits=[("LP", p.cnt["LP"]), ("S", p.cnt["S"]),
                                      ("LGC", 32)])
        for c in range(NKC):
            for hs in range(2):
                p.emit("tensor", lambda e, c=c, hs=hs: e.matmul(
                    PSD[0:1, hs * 512:(hs + 1) * 512], bvcol[:, c:c + 1],
                    WPROJT[:, c * 1024 + hs * 512: c * 1024 + (hs + 1) * 512],
                    start=(c == 0), stop=(c == NKC - 1)),
                    inc="T" if (c == NKC - 1 and hs == 1) else None)
        tok_pb = p.cnt["T"]
        tok_pbe = p.emit("scalar", lambda e: e.copy(pbias, PSD[0:1, 0:1024]),
                         waits=[("T", tok_pb)], inc="S")
        p.drain("scalar")
        tok_cb = p.emit("vector", lambda e: e.tensor_tensor(
            cbias, pbias, bprow2, OP.add), waits=[("S", tok_pbe)], inc="V")
        p.drain("vector")
        tok_res = {0: ("V", p.cnt["V"]), 1: ("V", p.cnt["V"])}
        pend_f0 = [None]
        for t in range(NT):
            pb = PS[:, 0:1024] if t % 2 == 0 else PS[:, 1024:2048]
            first_w = [tok_res[t % 2], ("V", tok_cb)]
            for c in range(NKC):
                for hs in range(2):
                    p.emit("tensor", lambda e, c=c, hs=hs, t=t, pb=pb: e.matmul(
                        pb[:, hs * 512:(hs + 1) * 512],
                        OFM[:, c * 2048 + t * 128: c * 2048 + (t + 1) * 128],
                        WPROJT[:, c * 1024 + hs * 512: c * 1024 + (hs + 1) * 512],
                        start=(c == 0), stop=False),
                        waits=first_w if (c == 0 and hs == 0) else [])
            for hs in range(2):
                p.emit("tensor", lambda e, hs=hs, pb=pb: e.matmul(
                    pb[:, hs * 512:(hs + 1) * 512], onesr,
                    cbias[0:1, hs * 512:(hs + 1) * 512], start=False, stop=True),
                    inc="T" if hs == 1 else None)
            tok_p = p.cnt["T"]
            xt = xres[:, t * D:(t + 1) * D]
            tok_r = p.emit("vector", lambda e, xt=xt, pb=pb: e.tensor_tensor(
                xt, pb[:, 0:1024], xt, OP.add), waits=[("T", tok_p)], inc="V")
            p.drain("vector")
            tok_res[t % 2] = ("V", tok_r)
            xl2 = htile[:, 0:D]
            tok_ln = emit_ln(xt, xl2)
            vt0, _ = emit_quant_head(p, tn_f0, xl2, D, after=[("V", tok_ln)])
            if pend_f0[0] is not None:
                pend_f0[0]()
            p.emit("vector", None, waits=[("T", p.cnt["T"])])
            tok_q = emit_quant_tail(p, tn_f0, vt0, D, xln)

            def f0_tr(t=t, tok_q=tok_q):
                emit_transposes(
                    xln, NKC,
                    lambda j, u, t=t: ub.rearrange("p (k c) -> p k c", c=2048)
                    [:, j:j + u, t * 128:(t + 1) * 128],
                    waits=[("V", tok_q)])
            pend_f0[0] = f0_tr
        if pend_f0[0] is not None:
            pend_f0[0]()
            pend_f0[0] = None

    def ffn_phase(s):
        regions = [(PS[:, 0:512], PS[:, 512:1024]),
                   (PS[:, 1024:1536], PS[:, 1536:2048])]
        reg_free = {0: ("V", 0), 1: ("V", 0)}
        hqv2 = [qkt[:, 22528:23936], qkt[:, 23936:25344]]
        pend_td = [None]
        last_res = [("V", 0)]
        prev_q = [("V", 0)]
        gs = 0
        for hf in range(2):
            if hf == 1 and pend_td[0] is not None:
                pend_td[0]()
                pend_td[0] = None
            wl0 = [("T", p.cnt["T"]), ("S", p.cnt["S"])] if hf == 1 else []
            for k in range(NKC):
                p.emit("sync", lambda e, k=k, hf=hf: e.dma_start(
                    out=WGH[:, k * HALF:(k + 1) * HALF],
                    in_=sWg[k, 0:128, hf * HALF:(hf + 1) * HALF]),
                    waits=wl0 if k == 0 else [], inc="LF", inc_n=16)
                p.emit("sync", lambda e, k=k, hf=hf: e.dma_start(
                    out=WUH[:, k * HALF:(k + 1) * HALF],
                    in_=sWu[k, 0:128, hf * HALF:(hf + 1) * HALF]), inc="LF", inc_n=16)
            for j in range(NHC):
                p.emit("sync", lambda e, j=j, hf=hf: e.dma_start(
                    out=WDH[:, j * 1024:(j + 1) * 1024],
                    in_=sWd[hf * NHC + j, 0:128, 0:1024]), inc="LF", inc_n=16)
            tok_w = p.tok("LF")
            for t in range(NT):
                subs = [(0, 512), (512, 1024), (1024, HALF)]
                for a, bq in subs:
                    sw = bq - a
                    greg, ureg = regions[gs % 2]
                    first_w = [tok_w, reg_free[gs % 2]]
                    for which in range(2):
                        WH = WGH if which == 0 else WUH
                        dst = greg if which == 0 else ureg
                        for k in range(NKC):
                            p.emit("tensor", lambda e, k=k, a=a, bq=bq, WH=WH,
                                   dst=dst, t=t, sw=sw: e.matmul(
                                dst[:, 0:sw],
                                ub[:, k * 2048 + t * 128: k * 2048 + (t + 1) * 128],
                                WH[:, k * HALF + a: k * HALF + bq],
                                start=(k == 0), stop=(k == NKC - 1)),
                                waits=first_w if (which == 0 and k == 0) else [],
                                inc="T" if (which == 1 and k == NKC - 1) else None)
                    tok_gu = p.cnt["T"]
                    ts1 = p.emit("scalar", lambda e, a=a, sw=sw, greg=greg:
                                 e.activation(htile[:, a:a + sw], greg[:, 0:sw],
                                              AF.Silu),
                                 waits=[("T", tok_gu), prev_q[0]], inc="S")
                    p.drain("scalar")
                    tok_m = p.emit("vector", lambda e, a=a, sw=sw, ureg=ureg:
                                   e.tensor_tensor(
                        htile[:, a:a + sw], htile[:, a:a + sw], ureg[:, 0:sw],
                        OP.mult), waits=[("S", ts1)], inc="V")
                    p.drain("vector")
                    reg_free[gs % 2] = ("V", tok_m)
                    gs += 1
                vt, _ = emit_quant_head(p, tn, htile[:, 0:HALF], HALF)
                if pend_td[0] is not None:
                    pend_td[0]()
                hq = hqv2[t % 2]
                tok_q = emit_quant_tail(p, tn, vt, HALF, hq[:, 0:HALF],
                                        out_wait=[("T", p.cnt["T"])])
                prev_q[0] = ("V", tok_q)

                def td(t=t, hf=hf, tok_q=tok_q, hq=hq):
                    tok_tr = emit_transposes(
                        hq[:, 0:HALF], NHC,
                        lambda j, u: hstage[:, j * 128:(j + u) * 128
                                            ].rearrange("p (k c) -> p k c", c=128),
                        waits=[("V", tok_q)])
                    for j in range(NHC):
                        for ds in range(2):
                            p.emit("tensor", lambda e, j=j, ds=ds: e.matmul(
                                PSD[:, ds * 512:(ds + 1) * 512],
                                hstage[:, j * 128:(j + 1) * 128],
                                WDH[:, j * 1024 + ds * 512: j * 1024 + (ds + 1) * 512],
                                start=(j == 0), stop=(j == NHC - 1)),
                                waits=([("S", tok_tr), last_res[0]]
                                       if (j == 0 and ds == 0) else []),
                                inc="T" if (j == NHC - 1 and ds == 1) else None)
                    tok_d = p.cnt["T"]
                    xt = xres[:, t * D:(t + 1) * D]
                    if hf == 0:
                        tok_o = p.emit("vector", lambda e, xt=xt: e.tensor_tensor(
                            xt, PSD[:, 0:1024], xt, OP.add),
                            waits=[("T", tok_d)], inc="V")
                        p.drain("vector")
                    else:
                        tok_o = p.emit("vector", lambda e, xt=xt: e.tensor_tensor(
                            outt, PSD[:, 0:1024], xt, OP.add),
                            waits=[("T", tok_d), ("ST", p.cnt["ST"])], inc="V")
                        p.drain("vector")
                        p.emit("sync", lambda e, t=t: e.dma_start(
                            out=out_h[s * T + t * 128: s * T + (t + 1) * 128, :],
                            in_=outt), waits=[("V", tok_o)], inc="ST", inc_n=16)
                    last_res[0] = ("V", tok_o)
                pend_td[0] = td
        if pend_td[0] is not None:
            pend_td[0]()
        p.barrier("ffn-end")

    for s in range(NSEQ if limit is None else 1):
        xl = xload0 if s == 0 else emit_xload(s)
        seq_phase(s, xl)
        if limit == "qkv":
            p.barrier("limit")
            break
        attn_phase(s)
        p.barrier("attn-end")
        if limit == "attn":
            break
        proj_phase(s)
        p.barrier("proj-end")
        if limit == "proj":
            break
        ffn_phase(s)
        p.barrier("seq-end")

    p.ops["sync"].append(([("ST", p.cnt["ST"])], None, None, 1))
    p.ops["gpsimd"].append(([("ST", p.cnt["ST"])], None, None, 1))

    with nc.Block() as block:
        p.build(block)
    es.close()
    return nc


_NC_CACHE = [None]


def kernel(L0_attn_x, L0_attn_Wqkv, L0_attn_bqkv, L0_attn_Wproj, L0_attn_bproj,
           L0_ffn_w_gate, L0_ffn_w_up, L0_ffn_w_down):
    x = np.ascontiguousarray(np.asarray(L0_attn_x, dtype=np.float32))
    Wqkv = np.ascontiguousarray(np.asarray(L0_attn_Wqkv, dtype=np.float32))
    bqkv = np.asarray(L0_attn_bqkv, dtype=np.float32)
    Wqkv_p, bqkv_p = _permute_qkv(Wqkv, bqkv)
    consts = _host_consts()
    if _NC_CACHE[0] is None:
        _NC_CACHE[0] = build_program()
    nc = _NC_CACHE[0]
    in_maps = []
    for c in range(NCORES):
        in_maps.append({
            "x": np.ascontiguousarray(x[c * NSEQ:(c + 1) * NSEQ].reshape(NSEQ * T, D)),
            "wqkv": np.ascontiguousarray(Wqkv_p),
            "bqkv": np.ascontiguousarray(bqkv_p[0:2 * D].reshape(1, 2 * D)),
            "bvcolh": np.ascontiguousarray(bqkv_p[2 * D:].reshape(8, 128).T),
            "wproj": np.ascontiguousarray(np.asarray(L0_attn_Wproj, np.float32)),
            "bproj": np.ascontiguousarray(np.asarray(L0_attn_bproj, np.float32).reshape(1, D)),
            "wgate": np.ascontiguousarray(np.asarray(L0_ffn_w_gate, np.float32)),
            "wup": np.ascontiguousarray(np.asarray(L0_ffn_w_up, np.float32)),
            "wdown": np.ascontiguousarray(np.asarray(L0_ffn_w_down, np.float32)),
            "ropeA": consts["ropeA"], "ropeB": consts["ropeB"],
            "ident": consts["ident"], "mask": consts["mask"],
            "ones_row": consts["ones_row"], "ones_col": consts["ones_col"],
            "ones_pair": consts["ones_pair"],
        })
    res = run_bass_kernel_spmd(nc, in_maps, core_ids=list(range(NCORES)))
    shards = [np.asarray(res.results[c]["out"]) for c in range(NCORES)]
    out = np.empty((B, T, D), shards[0].dtype)
    for c in range(NCORES):
        out[c * NSEQ:(c + 1) * NSEQ] = shards[c].reshape(NSEQ, T, D)
    return out
